# version 11
# baseline (speedup 1.0000x reference)
import math
import numpy as np
import ml_dtypes
import concourse.bass as bass
import concourse.mybir as mybir
from concourse.bass_utils import run_bass_kernel_spmd
from concourse.aot_env import allow_ici_all_to_all

F32 = mybir.dt.float32
BF16 = mybir.dt.bfloat16
AF = mybir.ActivationFunctionType
ALU = mybir.AluOpType
AX = mybir.AxisListType

NC = 8
D = 2048
T = 8192
TOK = 1024
INTER, NSTATE, DTRANK, KCONV = 4096, 16, 128, 4
DSL = 512
FF1 = 6144
DFF = 8192
NH, HD = 16, 128
ISL = 1024
TC = 512
NTC = 16
MAGIC = float(3 << 22)
INV_LN2 = float(1.0 / math.log(2.0))
LN2 = float(math.log(2.0))
ATT_SCALE = float(1.0 / np.sqrt(np.float32(HD)))
NEG_BIG = -1e9
LN_EPS = 1e-5
RMS_EPS = 1e-6

DEBUG = []


def _bf(x):
    return np.asarray(x, dtype=ml_dtypes.bfloat16)


SEM_NAMES = ["sQ", "sG", "sCC", "sT", "sV", "sA", "sV2", "sA2", "sFL", "sFG", "sP", "sQ2", "sQ3"]
LANES = ["sQ", "sQ2", "sQ3"]


class Guard:

    def __init__(self, E):
        self.E = E
        self.marks = []

    def mark(self, sem):
        self.marks.append((sem, self.E.val(sem)))

    def wait(self, e, k):
        if 0 <= k < len(self.marks):
            s, v = self.marks[k]
            e.wait_ge(self.E.sems[s], v)
        else:
            e.wait_ge(self.E.sems["sQ"], 0)

    def wait_last(self, e):
        self.wait(e, len(self.marks) - 1)


class Script:
    def __init__(self, active, eng, sems):
        self.active = active
        self.eng = eng
        self.sems = sems
        self.cnt = dict.fromkeys(SEM_NAMES, 0)

    def on(self, name):
        return self.active == name

    def inc(self, name, n=1):
        self.cnt[name] += n
        return self.cnt[name]

    def val(self, name):
        return self.cnt[name]

    def guard(self):
        return Guard(self)

    def dma_wait(self, e, sem="sQ"):
        if sem == "sQ":
            if self.active == "sync":
                e.wait_ge(self.sems["sQ"], self.cnt["sQ"])
            else:
                e.wait_ge(self.sems["sFL"], self.cnt["sFL"])
        else:
            if self.active == "gpsimd":
                e.wait_ge(self.sems["sG"], self.cnt["sG"])
            else:
                e.wait_ge(self.sems["sFG"], self.cnt["sFG"])

    def flush(self, e, sem="sQ", proxy=None):
        owner = "sync" if sem == "sQ" else "gpsimd"
        fl = "sFL" if sem == "sQ" else "sFG"
        who = proxy or owner
        if self.active == who:
            self.eng.wait_ge(self.sems[sem], self.cnt[sem])
            self.eng.sem_inc(self.sems[fl], 1)
        self.cnt[fl] += 1

    def lane_open(self, e, qs):
        if self.active == "sync":
            self.eng.wait_ge(self.sems[qs], self.cnt[qs])


def build_program(debug=()):
    nc = bass.Bass("TRN2", target_bir_lowering=False, dynamic_dma_scratch_size=16384)
    sems = {s: nc.semaphore(s).__enter__() for s in SEM_NAMES}
    inp = lambda name, shape, dtype=F32: nc.dram_tensor(name, list(shape), dtype, kind="ExternalInput").ap()
    dram = lambda name, shape, dtype=BF16: nc.dram_tensor(name, list(shape), dtype, kind="Internal").ap()
    sbuf = lambda name, shape, dtype=F32: nc.alloc_sbuf_tensor(name, list(shape), dtype).ap()
    psum = lambda name, shape, dtype=F32: nc.alloc_psum_tensor(name, list(shape), dtype).ap()

    x_own = inp("x_own", [TOK, D])
    wi_h = inp("wi_h", [D, DSL])
    wi_g = inp("wi_g", [D, DSL])
    conv_w_pp = inp("conv_w_pp", [128, 4 * KCONV])
    conv_b_pp = inp("conv_b_pp", [128, 4])
    xw_slice = inp("xw_slice", [DSL, 160])
    wdt_slice = inp("wdt_slice", [DTRANK, DSL])
    dtb_pp = inp("dtb_pp", [128, 4])
    alog_pp = inp("alog_pp", [128, 4 * NSTATE])
    dpar_pp = inp("dpar_pp", [128, 4])
    wout = inp("wout", [INTER, D])
    w8g_slice = inp("w8g_slice", [D // NC, FF1])
    w8u_slice = inp("w8u_slice", [D // NC, FF1])
    w8d_slice = inp("w8d_slice", [FF1 // NC, D])
    gs_pp = inp("gs_pp", [128, FF1 // 128])
    us_pp = inp("us_pp", [128, FF1 // 128])
    ds_pp = inp("ds_pp", [128, D // 128])
    wq_s = inp("wq_s", [D, 2 * HD])
    wk_s = inp("wk_s", [D, 2 * HD])
    wv_s = inp("wv_s", [D, 2 * HD])
    bq_pp = inp("bq_pp", [128, 2])
    bk_pp = inp("bk_pp", [128, 2])
    bv_row = inp("bv_row", [1, 2 * HD])
    wg_slice = inp("wg_slice", [ISL, D])
    wu_slice = inp("wu_slice", [ISL, D])
    wd_slice = inp("wd_slice", [D, ISL])
    out_own = nc.dram_tensor("out_own", [TOK, D], F32, kind="ExternalOutput").ap()

    c_ident = nc.inline_tensor(np.eye(128, dtype=ml_dtypes.bfloat16), "c_ident").ap()
    en_np = np.zeros((16, 16 * 128), dtype=ml_dtypes.bfloat16)
    for n in range(16):
        en_np[n, n * 128:(n + 1) * 128] = 1
    c_en = nc.inline_tensor(en_np, "c_en").ap()
    ii, jj = np.meshgrid(np.arange(128), np.arange(128), indexing="ij")
    pc = np.vectorize(lambda a: bin(a).count("1"))(ii & jj) % 2
    c_h128 = nc.inline_tensor(_bf(1.0 - 2.0 * pc), "c_h128").ap()
    mtile = np.where(np.tril(np.ones((128, 128))) > 0, 0.0, NEG_BIG).astype(np.float32)
    c_mask = nc.inline_tensor(mtile, "c_mask").ap()
    c_ones = nc.inline_tensor(np.ones((1, 128), dtype=ml_dtypes.bfloat16), "c_ones").ap()
    c_identf = nc.inline_tensor(np.eye(128, dtype=np.float32), "c_identf").ap()
    c_onesf = nc.inline_tensor(np.ones((1, 128), dtype=np.float32), "c_onesf").ap()
    kvals = [LN_EPS, RMS_EPS, ATT_SCALE, LN2, -LN2, -3.0 * LN2, 3.0 * LN2, 1e-30]
    c_kons = nc.inline_tensor(
        np.tile(np.asarray(kvals, np.float32)[None, :], (128, 1)), "c_kons").ap()

    h_xnT_own = dram("h_xnT_own", [16, 128, TOK])
    h_xnT_all = dram("h_xnT_all", [NC, 16, 128, TOK])
    h_hsT = dram("h_hsT", [4, 128, T])
    h_sgT = dram("h_sgT", [4, 128, T])
    h_ssmp_part = dram("h_ssmp_part", [NC, TOK, 160], F32)
    h_ssmp_recv = dram("h_ssmp_recv", [NC, TOK, 160], F32)
    h_nrm_own = dram("h_nrm_own", [TOK, 160])
    h_nrm_all = dram("h_nrm_all", [NC, TOK, 160])
    h_yg_stage = dram("h_yg_stage", [NC, 4, 128, TOK])
    h_yg_recv = dram("h_yg_recv", [NC, 4, 128, TOK])
    h_w8g_own = dram("h_w8g_own", [D // NC, FF1])
    h_w8u_own = dram("h_w8u_own", [D // NC, FF1])
    h_w8d_own = dram("h_w8d_own", [FF1 // NC, D])
    h_w8g = dram("h_w8g", [NC, D // NC, FF1])
    h_w8u = dram("h_w8u", [NC, D // NC, FF1])
    h_w8d = dram("h_w8d", [NC, FF1 // NC, D])
    h_xn3_own = dram("h_xn3_own", [16, 128, TOK])
    h_xn3_all = dram("h_xn3_all", [NC, 16, 128, TOK])
    h_qkT = dram("h_qkT", [2, 2, 128, T])
    h_v = dram("h_v", [2, T, HD])
    h_o_stage = dram("h_o_stage", [NC, 2, TOK, HD])
    h_o_recv = dram("h_o_recv", [NC, 2, TOK, HD])
    h_th_own = dram("h_th_own", [3, 1], F32)
    h_th_all = dram("h_th_all", [3, 1], F32)
    h_wgT = dram("h_wgT", [D, ISL])
    h_wuT = dram("h_wuT", [D, ISL])
    h_wdT_own = dram("h_wdT_own", [ISL, D])
    h_wdT = dram("h_wdT", [NC, ISL, D])
    h_a_own = dram("h_a_own", [16, 128, TOK])
    h_a_all = dram("h_a_all", [NC, 16, 128, TOK])
    h_fu_stage = dram("h_fu_stage", [NC, 8, 128, TOK])
    h_fu_recv = dram("h_fu_recv", [NC, 8, 128, TOK])
    h_a2 = dram("h_a2", [64, 128, TOK])

    res = sbuf("res", [128, 8 * D])
    arA = sbuf("arA", [128, 14336], F32)
    arB = sbuf("arB", [128, 8192], F32)
    arC = sbuf("arC", [128, 6272], BF16)
    ident = sbuf("ident", [128, 128], BF16)
    en_sb = sbuf("en_sb", [16, 16 * 128], BF16)
    h128_sb = sbuf("h128_sb", [128, 128], BF16)
    mask_sb = sbuf("mask_sb", [128, 128], F32)
    ones_sb = sbuf("ones_sb", [1, 128], BF16)
    convw_sb = sbuf("convw_sb", [128, 16])
    convb_sb = sbuf("convb_sb", [128, 4])
    dtb_sb = sbuf("dtb_sb", [128, 4])
    aneg_sb = sbuf("aneg_sb", [128, 4 * NSTATE])
    dpar_sb = sbuf("dpar_sb", [128, 4])
    stat_sb = sbuf("stat_sb", [128, 64])
    xw_sb = sbuf("xw_sb", [128, 4 * 160], BF16)
    wdt_sb = sbuf("wdt_sb", [128, 512], BF16)
    wi_h_sb = sbuf("wi_h_sb", [128, 16 * 512], BF16)
    wi_g_sb = sbuf("wi_g_sb", [128, 16 * 512], BF16)
    sstate = sbuf("sstate", [128, 4 * NSTATE], F32)
    hist = sbuf("hist", [128, 12])
    gsus_sb = sbuf("gsus_sb", [128, 112])
    bv_sb = sbuf("bv_sb", [1, 256], BF16)
    bqk_sb = sbuf("bqk_sb", [128, 4])
    identf = sbuf("identf", [128, 128], F32)
    onesf = sbuf("onesf", [1, 128], F32)
    th_row = sbuf("th_row", [1, 3], F32)
    kons = sbuf("kons", [128, 8], F32)
    K_LNEPS = kons[:, 0:1]
    K_RMSEPS = kons[:, 1:2]
    K_ATTS = kons[:, 2:3]
    K_LN2 = kons[:, 3:4]
    K_NLN2 = kons[:, 4:5]
    K_N3LN2 = kons[:, 5:6]
    K_P3LN2 = kons[:, 6:7]
    K_TINY = kons[:, 7:8]

    psA = psum("psA", [128, 512], F32)
    psB = psum("psB", [128, 512], F32)
    psC = psum("psC", [128, 512], F32)
    psD = psum("psD", [128, 512], F32)
    psE = psum("psE", [128, 1024], BF16)
    psF = psum("psF", [128, 1024], BF16)
    psG = psum("psG", [128, 512], F32)
    psH = psum("psH", [128, 512], F32)

    dbg_t = {}
    _dbg_specs = {"d_x1": ([TOK, D], F32), "d_x2": ([TOK, D], F32), "d_x3": ([TOK, D], F32),
                  "d_xnT": ([NC, 16, 128, TOK], BF16), "d_hsT": ([4, 128, T], BF16),
                  "d_sgT": ([4, 128, T], BF16), "d_ssmp": ([NC, TOK, 160], F32),
                  "d_nrm": ([NC, TOK, 160], BF16), "d_yg": ([NC, 4, 128, TOK], BF16),
                  "d_op": ([128, 4 * 512], F32), "d_ygf": ([4, 128, 8192], BF16)}
    for name in debug:
        shape, dt = _dbg_specs[name]
        dbg_t[name] = nc.dram_tensor(name, shape, dt, kind="ExternalOutput").ap()

    def fv(ar, off, n):
        return ar[:, off:off + n]

    def bv(ar, off, n):
        return ar[:, off:off + (n + 1) // 2].bitcast(BF16)

    xn_mov0 = bv(arA, 0, 16 * TC)
    xn_mov1 = bv(arA, 4096, 16 * TC)
    xnT_sb = bv(arA, 0, 16 * TOK)
    dtT_ch = fv(arA, 8192, 4 * TC)
    s_sc1 = bv(arA, 0, 4 * TC)
    csb0 = bv(arA, 2048, TC)
    csb1 = bv(arA, 2304, TC)
    du_bf = bv(arA, 12288, 4 * TC)
    du_ch = fv(arA, 10240, 4 * TC)
    hconv = fv(arA, 12276, 4 * (TC + 3))
    yacc = fv(arB, 0, 4 * TC)
    dA0_sc = bv(arB, 2048, 4 * TC)
    dA1_sc = bv(arB, 2048 + 1024, 4 * TC)
    dBu_sc = bv(arB, 4096, 4 * TC)
    sp_sc = fv(arB, 5120, 4 * TC)
    s_sc = bv(arB, 7168, 4 * TC)
    convo = fv(arB, 4096, 4 * TC)
    bsb0 = bv(arA, 3072, TC)
    bsb1 = bv(arA, 3328, TC)

    ssum_sc = fv(arA, 8192, 160)
    red0_sc = fv(arA, 8352, 8 * 160)
    red1_sc = fv(arB, 0, 8 * 160)
    ssmp_ev = fv(arB, 2048, 4 * 160)
    sq_junk = fv(arB, 4096, 128)

    hs_ch = arC[:, 0:2048]
    sg_ch = arC[:, 2048:4096]
    ln_xn0 = arC[:, 0:2048]
    ln_xn1 = arC[:, 2048:4096]
    nrm_tm = arC[:, 4096:4736]
    dtnT = arC[:, 4736:5248]
    bnT = arC[0:16, 5248:5760]
    cnT = arC[0:16, 5760:6272]
    yg_ev = arC[:, 4096:6144]
    ln_sq = fv(arA, 8192, 2048)
    ygf_sb = bv(arA, 0, 8192)
    wout_kg = bv(arA, 4096, 16384)
    q_ws = fv(arA, 0, 2048)
    q_t1 = fv(arA, 2048, 2048)
    q_q = fv(arA, 4096, 2048)
    q_qi = fv(arA, 6144, 2048)
    q_bf = bv(arA, 8192, 2048)
    q_ws2 = fv(arB, 0, 2048)
    q_t12 = fv(arB, 2048, 2048)
    q_q2 = fv(arB, 4096, 2048)
    q_qi2 = fv(arB, 6144, 2048)
    q_bf2 = bv(arA, 9216, 2048)
    b_wdst = bv(arA, 8192, 8192)
    b_acc = arB[:, 0:8192]
    b_fuse = arC[:, 0:4096]
    b_gs = arC[:, 4096:4608]
    b_us = arC[:, 4608:5120]
    b_tb = arC[:, 5120:5632]
    b_tb2 = arC[:, 5632:6144]
    b_wgst = wi_h_sb
    b_wust = wi_g_sb
    c_wv = bv(arA, 8192, 4096)
    c_qt = bv(arA, 0, 4096)
    c_kt = bv(arA, 2048, 4096)
    c_v = bv(arA, 4096, 4096)
    c_p = bv(arA, 6144, 4096)
    c_p1 = bv(arA, 10240, 4096)
    c_srow = arB[:, 0:4096]
    c_pt = bv(arB, 4096, 4096)
    c_osb = arC[:, 0:1024]
    c_vev = arC[:, 1024:1536]
    c_qkev = arC[:, 1536:2048]
    c_vev2 = arC[:, 4608:5120]
    c_qkev2 = arC[:, 4096:4608]
    c_att = arC[:, 2048:4096]
    t_ws = fv(arA, 0, 2048)
    t_tern = bv(arA, 2048, 2048)
    t_sc = bv(arA, 3072, 2048)
    t_le = fv(arA, 4096, 2048)
    t_tp = bv(arA, 6144, 2048)
    t_wsB = fv(arB, 0, 2048)
    t_ternB = bv(arB, 2048, 2048)
    t_tpB = bv(arB, 3072, 2048)
    racc = stat_sb[:, 52:55]
    thbc = stat_sb[:, 55:58]
    mbc = stat_sb[:, 58:61]
    d_amax = stat_sb[:, 61:62]
    d_scal = stat_sb[:, 62:63]
    d_inv2 = stat_sb[:, 63:64]
    d_win = bv(arA, 0, 8192)
    d_wtm = bv(arA, 4096, 8192)
    d_a2t = bv(arA, 8192, 8192)
    d_ast0 = arC[:, 1024:3072]
    d_ast1 = arC[:, 3072:5120]
    d_bf0 = bv(arB, 0, 8192)
    d_qf = fv(arB, 0, 8192)
    d_a2b = bv(arA, 0, 8192)
    d_w1s = arC[:, 0:128]
    d_wd0 = wi_h_sb
    d_wd1 = wi_g_sb

    ln_s1 = stat_sb[:, 0:8]
    ln_s2 = stat_sb[:, 8:16]
    ln_mean = stat_sb[:, 16:24]
    ln_rstd = stat_sb[:, 24:32]
    ln_tmp = stat_sb[:, 32:40]
    rms_s = stat_sb[:, 40:44]
    rms_r = stat_sb[:, 44:48]

    def ph_init(E):
        e = E.eng
        if E.on("sync"):
            for dst, src in [(ident, c_ident), (en_sb, c_en), (h128_sb, c_h128),
                             (mask_sb, c_mask), (ones_sb, c_ones), (convw_sb, conv_w_pp),
                             (convb_sb, conv_b_pp), (dtb_sb, dtb_pp), (aneg_sb, alog_pp),
                             (dpar_sb, dpar_pp), (kons, c_kons),
                             (gsus_sb, None)]:
                if dst is gsus_sb:
                    e.dma_start(out=gsus_sb[:, 0:48], in_=gs_pp[:, :]).then_inc(sems["sQ"], 16)
                    e.dma_start(out=gsus_sb[:, 48:96], in_=us_pp[:, :]).then_inc(sems["sQ"], 16)
                    e.dma_start(out=gsus_sb[:, 96:112], in_=ds_pp[:, :]).then_inc(sems["sQ"], 16)
                    continue
                e.dma_start(out=dst[:, :], in_=src[:, :]).then_inc(sems["sQ"], 16)
            e.dma_start(
                out=res[:, :].rearrange("p (t d) -> p t d", d=D),
                in_=x_own.rearrange("(t p) d -> p t d", p=128),
            ).then_inc(sems["sQ"], 16)
        E.inc("sQ", 16 * 15)
        E.flush(e, "sQ")
        if E.on("gpsimd"):
            e.dma_start(
                out=wi_h_sb[:, :].rearrange("p (k c) -> p k c", c=512),
                in_=wi_h.rearrange("(k p) c -> p k c", p=128),
            ).then_inc(sems["sG"], 16)
            e.dma_start(
                out=wi_g_sb[:, :].rearrange("p (k c) -> p k c", c=512),
                in_=wi_g.rearrange("(k p) c -> p k c", p=128),
            ).then_inc(sems["sG"], 16)
            e.dma_start(
                out=xw_sb[:, :].rearrange("p (k c) -> p k c", c=160),
                in_=xw_slice.rearrange("(k p) c -> p k c", p=128),
            ).then_inc(sems["sG"], 16)
            e.dma_start(out=wdt_sb[:, :], in_=wdt_slice[:, :]).then_inc(sems["sG"], 16)
        E.inc("sG", 16 * 4)
        E.flush(e, "sG")
        q_all, g_all = E.val("sQ"), E.val("sG")
        if E.on("scalar"):
            e.wait_ge(sems["sQ"], q_all)
            e.activation(out=aneg_sb[:, :], in_=aneg_sb[:, :], func=AF.Exp).then_inc(sems["sA"], 1)
        E.inc("sA", 1)
        a_now = E.val("sA")
        if E.on("vector"):
            e.wait_ge(sems["sA"], a_now)
            e.tensor_scalar(out=aneg_sb[:, :], in0=aneg_sb[:, :], scalar1=-1.0,
                            scalar2=None, op0=ALU.mult)
            e.memset(sstate[:, :], 0.0)
            e.drain()
            e.sem_inc(sems["sV"], 1)
        E.inc("sV", 1)
        v_now = E.val("sV")
        for nm in ("sync", "gpsimd", "scalar", "tensor"):
            if E.on(nm):
                e.wait_ge(sems["sQ"], q_all)
                e.wait_ge(sems["sG"], g_all)
                e.wait_ge(sems["sV"], v_now)

    def emit_ln_transpose(E, dst_sb):
        e = E.eng
        if E.on("scalar"):
            e.wait_ge(sems["sV"], E.val("sV"))
            E.dma_wait(e, "sQ")
            E.dma_wait(e, "sG")
            for tt in range(8):
                if tt:
                    e.drain()
                e.activation(out=ln_sq[:, :], in_=res[:, tt * D:(tt + 1) * D],
                             func=AF.Square, accum_out=ln_s2[:, tt:tt + 1])
            e.drain()
            e.sem_inc(sems["sA"], 1)
        E.inc("sA", 1)
        sq_done = E.val("sA")
        if E.on("vector"):
            for tt in range(8):
                e.tensor_reduce(out=ln_s1[:, tt:tt + 1], in_=res[:, tt * D:(tt + 1) * D],
                                axis=AX.X, op=ALU.add)
            e.drain()
            e.wait_ge(sems["sA"], sq_done)
            e.tensor_scalar(out=ln_mean[:, :], in0=ln_s1[:, :], scalar1=1.0 / D,
                            scalar2=None, op0=ALU.mult)
            e.tensor_scalar(out=ln_tmp[:, :], in0=ln_s2[:, :], scalar1=1.0 / D,
                            scalar2=None, op0=ALU.mult)
            e.drain()
            e.tensor_tensor(out=ln_s2[:, :], in0=ln_mean[:, :], in1=ln_mean[:, :], op=ALU.mult)
            e.drain()
            e.tensor_tensor(out=ln_tmp[:, :], in0=ln_tmp[:, :], in1=ln_s2[:, :], op=ALU.subtract)
            e.drain()
            e.sem_inc(sems["sV"], 1)
        E.inc("sV", 1)
        var_done = E.val("sV")
        if E.on("scalar"):
            e.wait_ge(sems["sV"], var_done)
            e.activation(out=ln_tmp[:, :], in_=ln_tmp[:, :], func=AF.Sqrt,
                         bias=K_LNEPS).then_inc(sems["sA"], 1)
        E.inc("sA", 1)
        std_done = E.val("sA")
        first_v = True
        gN = E.guard()
        gT = E.guard()
        gEv = E.guard()
        for tt in range(8):
            lx = ln_xn0 if tt % 2 == 0 else ln_xn1
            if E.on("vector"):
                if first_v:
                    e.wait_ge(sems["sA"], std_done)
                    e.reciprocal(out=ln_rstd[:, :], in_=ln_tmp[:, :])
                    e.drain()
                    first_v = False
                gT.wait(e, tt - 2)
                e.tensor_scalar(out=lx[:, :], in0=res[:, tt * D:(tt + 1) * D],
                                scalar1=ln_mean[:, tt:tt + 1], scalar2=ln_rstd[:, tt:tt + 1],
                                op0=ALU.subtract, op1=ALU.mult)
                e.drain()
                e.sem_inc(sems["sV"], 1)
            E.inc("sV", 1)
            gN.mark("sV")
            if E.on("tensor"):
                gN.wait(e, tt)
                gEv.wait(e, tt - 1)
                last = None
                for g in range(2):
                    pbank = psE if g == 0 else psF
                    for f8 in range(8):
                        f = g * 8 + f8
                        last = e.transpose(pbank[:, f8 * 128:(f8 + 1) * 128],
                                           lx[:, f * 128:(f + 1) * 128], ident[:, :])
                last.then_inc(sems["sT"], 1)
            E.inc("sT", 1)
            gT.mark("sT")
            if E.on("vector"):
                gT.wait(e, tt)
                for g in range(2):
                    pbank = psE if g == 0 else psF
                    for f8 in range(8):
                        f = g * 8 + f8
                        e.tensor_copy(dst_sb[:, f * TOK + tt * 128: f * TOK + (tt + 1) * 128],
                                      pbank[:, f8 * 128:(f8 + 1) * 128])
                e.drain()
                e.sem_inc(sems["sV2"], 1)
            E.inc("sV2", 1)
            gEv.mark("sV2")
        return gEv

    def ph_a0(E):
        e = E.eng
        gEv = emit_ln_transpose(E, xnT_sb)
        if E.on("sync"):
            gEv.wait_last(e)
            e.dma_start(
                out=h_xnT_own[:, :, :].transpose([1, 0, 2]),
                in_=xnT_sb[:, :].rearrange("p (f t) -> p f t", t=TOK),
            ).then_inc(sems["sQ"], 16)
        E.inc("sQ", 16)
        E.flush(e, "sQ")
        st = E.val("sQ")
        if E.on("sync"):
            e.wait_ge(sems["sQ"], st)
            e.sem_inc(sems["sA2"], 1)
        E.inc("sA2", 1)
        fla0 = E.val("sA2")
        if E.on("gpsimd"):
            e.wait_ge(sems["sA2"], fla0)
            e.collective_compute(
                "AllGather", ALU.bypass, replica_groups=[list(range(NC))],
                ins=[h_xnT_own[:, :, :].opt()], outs=[h_xnT_all[:, :, :, :].opt()],
            ).then_inc(sems["sCC"], 1)
        E.inc("sCC", 1)

    def ph_a_pass1(E):
        e = E.eng
        cc_ready = E.val("sCC")
        gLd = E.guard()
        gMM = E.guard()
        gHEv = E.guard()
        gGEv = E.guard()
        gXEv = E.guard()
        gConv = E.guard()
        gHs = E.guard()
        gXmm = E.guard()
        gSt = E.guard()
        for i in range(NTC):
            owner, trow = i // 2, (i % 2) * TC
            buf = xn_mov0 if i % 2 == 0 else xn_mov1
            if E.on("sync"):
                e.wait_ge(sems["sCC"], cc_ready)
                gMM.wait(e, i - 2)
                e.dma_start(
                    out=buf[:, :].rearrange("p (f t) -> p f t", t=TC),
                    in_=h_xnT_all[owner, :, :, trow:trow + TC].transpose([1, 0, 2]),
                ).then_inc(sems["sQ"], 16)
            E.inc("sQ", 16)
            E.flush(e, "sQ")
            gLd.mark("sQ")
            if E.on("tensor"):
                E.dma_wait(e, "sQ")
            for m in range(4):
                pb = psA if m % 2 == 0 else psB
                if E.on("tensor"):
                    gHEv.wait(e, 4 * i + m - 2)
                    for k in range(16):
                        last = e.matmul(pb[:, :], wi_h_sb[:, k * 512 + m * 128: k * 512 + (m + 1) * 128],
                                        buf[:, k * TC:(k + 1) * TC], start=(k == 0), stop=(k == 15))
                    last.then_inc(sems["sT"], 1)
                E.inc("sT", 1)
                gMMm = E.val("sT")
                if E.on("vector"):
                    e.wait_ge(sems["sT"], gMMm)
                    e.tensor_copy(hconv[:, m * (TC + 3) + 3: m * (TC + 3) + 3 + TC], pb[:, :])
                    e.drain()
                    e.sem_inc(sems["sV2"], 1)
                E.inc("sV2", 1)
                gHEv.mark("sV2")
            for m in range(4):
                pb = psC if m % 2 == 0 else psD
                if E.on("tensor"):
                    gGEv.wait(e, 4 * i + m - 2)
                    for k in range(16):
                        last = e.matmul(pb[:, :], wi_g_sb[:, k * 512 + m * 128: k * 512 + (m + 1) * 128],
                                        buf[:, k * TC:(k + 1) * TC], start=(k == 0), stop=(k == 15))
                    last.then_inc(sems["sT"], 1)
                E.inc("sT", 1)
                gMMg = E.val("sT")
                if E.on("scalar"):
                    e.wait_ge(sems["sT"], gMMg)
                    if m == 0:
                        E.dma_wait(e, "sQ")
                    e.activation(out=sg_ch[:, m * TC:(m + 1) * TC], in_=pb[:, :], func=AF.Silu)
                    e.drain()
                    e.sem_inc(sems["sA2"], 1)
                E.inc("sA2", 1)
                gGEv.mark("sA2")
            gMM.mark("sT")
            if E.on("vector"):
                if i % 8 == 0:
                    for m in range(4):
                        e.memset(hconv[:, m * (TC + 3): m * (TC + 3) + 3], 0.0)
                else:
                    for m in range(4):
                        e.tensor_copy(hconv[:, m * (TC + 3): m * (TC + 3) + 3],
                                      hist[:, m * 3:(m + 1) * 3])
                e.drain()
                gHs.wait(e, i - 1)
                for m in range(4):
                    base = m * (TC + 3)
                    e.tensor_scalar(out=convo[:, m * TC:(m + 1) * TC],
                                    in0=hconv[:, base: base + TC],
                                    scalar1=convw_sb[:, m * 4:m * 4 + 1], scalar2=None,
                                    op0=ALU.mult)
                    e.drain()
                    for kk in range(1, 4):
                        e.scalar_tensor_tensor(
                            out=convo[:, m * TC:(m + 1) * TC],
                            in0=hconv[:, base + kk: base + kk + TC],
                            scalar=convw_sb[:, m * 4 + kk:m * 4 + kk + 1],
                            in1=convo[:, m * TC:(m + 1) * TC],
                            op0=ALU.mult, op1=ALU.add)
                        e.drain()
                for m in range(4):
                    e.tensor_copy(hist[:, m * 3:(m + 1) * 3],
                                  hconv[:, m * (TC + 3) + TC: m * (TC + 3) + TC + 3])
                e.drain()
                e.sem_inc(sems["sV"], 1)
            E.inc("sV", 1)
            gConv.mark("sV")
            if E.on("scalar"):
                gConv.wait(e, i)
                gXmm.wait(e, i - 1)
                E.dma_wait(e, "sQ")
                for m in range(4):
                    if m:
                        e.drain()
                    e.activation(out=hs_ch[:, m * TC:(m + 1) * TC],
                                 in_=convo[:, m * TC:(m + 1) * TC],
                                 func=AF.Silu, bias=convb_sb[:, m:m + 1])
                e.drain()
                e.sem_inc(sems["sA"], 1)
            E.inc("sA", 1)
            gHs.mark("sA")
            for tb in range(4):
                pb = psG if tb % 2 == 0 else psH
                if E.on("tensor"):
                    if tb == 0:
                        gHs.wait(e, i)
                    gXEv.wait(e, 4 * i + tb - 2)
                    for k in range(4):
                        last = e.matmul(pb[:, 0:160],
                                        hs_ch[:, k * TC + tb * 128: k * TC + (tb + 1) * 128],
                                        xw_sb[:, k * 160:(k + 1) * 160],
                                        start=(k == 0), stop=(k == 3))
                    last.then_inc(sems["sT"], 1)
                E.inc("sT", 1)
                x_now = E.val("sT")
                if E.on("vector"):
                    e.wait_ge(sems["sT"], x_now)
                    if tb == 0:
                        E.dma_wait(e, "sQ")
                    e.tensor_copy(ssmp_ev[:, tb * 160:(tb + 1) * 160], pb[:, 0:160])
                    e.drain()
                    e.sem_inc(sems["sV2"], 1)
                E.inc("sV2", 1)
                gXEv.mark("sV2")
            gXmm.mark("sT")
            if E.on("sync"):
                gHs.wait(e, i)
                gGEv.wait(e, 4 * i + 3)
                gXEv.wait(e, 4 * i + 3)
                e.dma_start(out=h_hsT[:, :, i * TC:(i + 1) * TC].transpose([1, 0, 2]),
                            in_=hs_ch[:, :].rearrange("p (m t) -> p m t", t=TC)).then_inc(sems["sQ"], 16)
                e.dma_start(out=h_sgT[:, :, i * TC:(i + 1) * TC].transpose([1, 0, 2]),
                            in_=sg_ch[:, :].rearrange("p (m t) -> p m t", t=TC)).then_inc(sems["sQ"], 16)
                e.dma_start(out=h_ssmp_part[owner, trow:trow + TC, :].rearrange("(tb p) c -> p tb c", p=128),
                            in_=ssmp_ev[:, :].rearrange("p (tb c) -> p tb c", c=160)).then_inc(sems["sQ"], 16)
            E.inc("sQ", 48)
            E.flush(e, "sQ")
            gSt.mark("sQ")
        return gSt

    def ph_a_reduce(E, gSt):
        e = E.eng
        if E.on("sync"):
            gSt.wait_last(e)
            e.sem_inc(sems["sA2"], 1)
        E.inc("sA2", 1)
        flp1 = E.val("sA2")
        if E.on("gpsimd"):
            e.wait_ge(sems["sA2"], flp1)
            with allow_ici_all_to_all():
                e.collective_compute(
                    "AllToAll", ALU.bypass, replica_groups=[list(range(NC))],
                    ins=[h_ssmp_part[:, :, :].opt()], outs=[h_ssmp_recv[:, :, :].opt()],
                ).then_inc(sems["sCC"], 1)
        E.inc("sCC", 1)
        cc_now = E.val("sCC")
        gLd = E.guard()
        gRed = E.guard()
        gNrm = E.guard()
        gNSt = E.guard()
        for tt in range(8):
            red_sc = red0_sc if tt % 2 == 0 else red1_sc
            if E.on("sync"):
                e.wait_ge(sems["sCC"], cc_now)
                gRed.wait(e, tt - 2)
                e.dma_start(
                    out=red_sc[:, :].rearrange("p (s c) -> p s c", c=160),
                    in_=h_ssmp_recv[:, tt * 128:(tt + 1) * 128, :].transpose([1, 0, 2]),
                ).then_inc(sems["sQ"], 16)
            E.inc("sQ", 16)
            E.flush(e, "sQ")
            gLd.mark("sQ")
            if E.on("vector"):
                E.dma_wait(e, "sQ")
                e.tensor_tensor(out=ssum_sc[:, :], in0=red_sc[:, 0:160],
                                in1=red_sc[:, 160:320], op=ALU.add)
                e.drain()
                for s in range(2, 8):
                    e.tensor_tensor(out=ssum_sc[:, :], in0=ssum_sc[:, :],
                                    in1=red_sc[:, s * 160:(s + 1) * 160], op=ALU.add)
                    e.drain()
                e.sem_inc(sems["sV"], 1)
            E.inc("sV", 1)
            gRed.mark("sV")
            red_done = E.val("sV")
            if E.on("scalar"):
                e.wait_ge(sems["sV"], red_done)
                e.activation(out=sq_junk[:, 0:128], in_=ssum_sc[:, 0:128],
                             func=AF.Square, accum_out=rms_s[:, 0:1])
                e.drain()
                e.activation(out=sq_junk[:, 0:16], in_=ssum_sc[:, 128:144],
                             func=AF.Square, accum_out=rms_s[:, 1:2])
                e.drain()
                e.activation(out=sq_junk[:, 0:16], in_=ssum_sc[:, 144:160],
                             func=AF.Square, accum_out=rms_s[:, 2:3])
                e.drain()
                e.sem_inc(sems["sA"], 1)
            E.inc("sA", 1)
            sq_done = E.val("sA")
            if E.on("vector"):
                e.wait_ge(sems["sA"], sq_done)
                e.tensor_scalar(out=rms_s[:, 0:1], in0=rms_s[:, 0:1], scalar1=1.0 / DTRANK,
                                scalar2=None, op0=ALU.mult)
                e.tensor_scalar(out=rms_s[:, 1:3], in0=rms_s[:, 1:3], scalar1=1.0 / NSTATE,
                                scalar2=None, op0=ALU.mult)
                e.drain()
                e.sem_inc(sems["sV"], 1)
            E.inc("sV", 1)
            ms_done = E.val("sV")
            if E.on("scalar"):
                e.wait_ge(sems["sV"], ms_done)
                e.activation(out=rms_s[:, 0:3], in_=rms_s[:, 0:3], func=AF.Sqrt,
                             bias=K_RMSEPS).then_inc(sems["sA"], 1)
            E.inc("sA", 1)
            rt_done = E.val("sA")
            if E.on("vector"):
                e.wait_ge(sems["sA"], rt_done)
                E.dma_wait(e, "sQ")
                e.reciprocal(out=rms_r[:, 0:3], in_=rms_s[:, 0:3])
                e.drain()
                e.tensor_scalar(out=nrm_tm[:, 0:128], in0=ssum_sc[:, 0:128],
                                scalar1=rms_r[:, 0:1], scalar2=None, op0=ALU.mult)
                e.tensor_scalar(out=nrm_tm[:, 128:144], in0=ssum_sc[:, 128:144],
                                scalar1=rms_r[:, 1:2], scalar2=None, op0=ALU.mult)
                e.tensor_scalar(out=nrm_tm[:, 144:160], in0=ssum_sc[:, 144:160],
                                scalar1=rms_r[:, 2:3], scalar2=None, op0=ALU.mult)
                e.drain()
                e.sem_inc(sems["sV"], 1)
            E.inc("sV", 1)
            gNrm.mark("sV")
            nrm_done = E.val("sV")
            if E.on("sync"):
                e.wait_ge(sems["sV"], nrm_done)
                e.dma_start(out=h_nrm_own[tt * 128:(tt + 1) * 128, :],
                            in_=nrm_tm[:, 0:160]).then_inc(sems["sQ"], 16)
            E.inc("sQ", 16)
            E.flush(e, "sQ")
            gNSt.mark("sQ")
        st_all = E.val("sQ")
        if E.on("sync"):
            e.wait_ge(sems["sQ"], st_all)
            e.sem_inc(sems["sA2"], 1)
        E.inc("sA2", 1)
        flnr = E.val("sA2")
        if E.on("gpsimd"):
            e.wait_ge(sems["sA2"], flnr)
            e.collective_compute(
                "AllGather", ALU.bypass, replica_groups=[list(range(NC))],
                ins=[h_nrm_own[:, :].opt()], outs=[h_nrm_all[:, :, :].opt()],
            ).then_inc(sems["sCC"], 1)
        E.inc("sCC", 1)

    def ph_a_pass2(E):
        e = E.eng
        cc_now = E.val("sCC")
        gLd = E.guard()
        gTp = E.guard()
        gTpe = E.guard()
        gDt = E.guard()
        gSp = E.guard()
        gDu = E.guard()
        gBC = E.guard()
        gScan = E.guard()
        gCe = E.guard()
        gDA = E.guard()
        gYg = E.guard()
        gSt = E.guard()
        for i in range(NTC):
            owner, trow = i // 2, (i % 2) * TC
            lqs = "sQ2" if i % 2 == 0 else "sQ3"
            if E.on("sync"):
                if i == 0:
                    e.wait_ge(sems["sCC"], cc_now)
                gYg.wait(e, i - 1)
                E.dma_wait(e, "sQ")
                E.lane_open(e, lqs)
                e.dma_start(out=nrm_tm[:, :].rearrange("p (tb c) -> p tb c", c=160),
                            in_=h_nrm_all[owner, trow:trow + TC, :]
                            .rearrange("(tb p) c -> p tb c", p=128)).then_inc(sems[lqs], 16)
                e.dma_start(out=hs_ch[:, :].rearrange("p (m t) -> p m t", t=TC),
                            in_=h_hsT[:, :, i * TC:(i + 1) * TC].transpose([1, 0, 2])).then_inc(sems[lqs], 16)
                e.dma_start(out=sg_ch[:, :].rearrange("p (m t) -> p m t", t=TC),
                            in_=h_sgT[:, :, i * TC:(i + 1) * TC].transpose([1, 0, 2])).then_inc(sems[lqs], 16)
            E.inc(lqs, 48)
            gLd.mark(lqs)
            if E.on("tensor"):
                gLd.wait(e, i)
                gTpe.wait(e, i - 1)
                for tb in range(4):
                    e.transpose(psE[:, tb * 128:(tb + 1) * 128],
                                nrm_tm[:, tb * 160: tb * 160 + 128], ident[:, :])
                for tb in range(4):
                    e.transpose(psF[0:16, tb * 128:(tb + 1) * 128],
                                nrm_tm[:, tb * 160 + 128: tb * 160 + 144], ident[:, :])
                for tb in range(4):
                    last = e.transpose(psF[0:16, 512 + tb * 128: 512 + (tb + 1) * 128],
                                       nrm_tm[:, tb * 160 + 144: tb * 160 + 160], ident[:, :])
                last.then_inc(sems["sT"], 1)
            E.inc("sT", 1)
            gTp.mark("sT")
            if E.on("vector"):
                gTp.wait(e, i)
                gDt.wait(e, 4 * i - 1)
                gBC.wait(e, 16 * i - 1)
                e.tensor_copy(dtnT[:, :], psE[:, 0:512])
                e.tensor_copy(bnT[:, :], psF[0:16, 0:512])
                e.tensor_copy(cnT[:, :], psF[0:16, 512:1024])
                e.drain()
                e.sem_inc(sems["sV2"], 1)
            E.inc("sV2", 1)
            gTpe.mark("sV2")
            tpe_now = E.val("sV2")
            for m in range(4):
                pb = [psA, psB, psC, psD][m]
                if E.on("tensor"):
                    if m == 0:
                        e.wait_ge(sems["sV2"], tpe_now)
                        gSp.wait(e, i - 1)
                        gScan.wait(e, 16 * i - 1)
                    e.matmul(pb[:, :], wdt_sb[:, m * 128:(m + 1) * 128], dtnT[:, :],
                             start=True, stop=True).then_inc(sems["sT"], 1)
                E.inc("sT", 1)
                gDt.mark("sT")
                dt_now = E.val("sT")
                if E.on("scalar"):
                    e.wait_ge(sems["sT"], dt_now)
                    if m == 0:
                        gDu.wait(e, i - 1)
                        gScan.wait(e, 16 * i - 1)
                    sc1 = sp_sc[:, m * TC:(m + 1) * TC]
                    e.activation(out=sc1, in_=pb[:, :], func=AF.Abs, bias=dtb_sb[:, m:m + 1])
                    e.drain()
                    e.activation(out=sc1, in_=sc1, func=AF.Exp, scale=-1.0)
                    e.drain()
                    e.activation(out=sc1, in_=sc1, func=AF.Ln, bias=1.0)
                    e.drain()
                    e.activation(out=dtT_ch[:, m * TC:(m + 1) * TC], in_=pb[:, :],
                                 func=AF.Relu, bias=dtb_sb[:, m:m + 1])
                    e.drain()
                    if m == 3:
                        e.sem_inc(sems["sA"], 1)
            E.inc("sA", 1)
            gSp.mark("sA")
            sp_now = E.val("sA")
            if E.on("vector"):
                e.wait_ge(sems["sA"], sp_now)
                gScan.wait(e, 16 * i - 1)
                e.tensor_tensor(out=dtT_ch[:, :], in0=dtT_ch[:, :], in1=sp_sc[:, :], op=ALU.add)
                e.drain()
                e.tensor_tensor(out=du_bf[:, :], in0=dtT_ch[:, :], in1=hs_ch[:, :], op=ALU.mult)
                e.memset(yacc[:, :], 0.0)
                if i == 8:
                    e.memset(sstate[:, :], 0.0)
                e.drain()
                e.sem_inc(sems["sV"], 1)
            E.inc("sV", 1)
            gDu.mark("sV")
            du_now = E.val("sV")
            if i == 8:
                pass
            for n in range(NSTATE):
                pbB, pbC = (psG, psH) if n % 2 == 0 else (psC, psD)
                s_buf = s_sc if n % 2 == 0 else s_sc1
                c_buf = csb0 if n % 2 == 0 else csb1
                b_buf = bsb0 if n % 2 == 0 else bsb1
                dA_sc = dA0_sc if n % 2 == 0 else dA1_sc
                if E.on("tensor"):
                    if n == 0:
                        e.wait_ge(sems["sV2"], tpe_now)
                    if n == 1:
                        gSp.wait(e, i)
                    gCe.wait(e, 16 * i + n - 2)
                    e.matmul(pbB[:, :], en_sb[:, n * 128:(n + 1) * 128], bnT[:, :],
                             start=True, stop=True)
                    e.matmul(pbC[:, :], en_sb[:, n * 128:(n + 1) * 128], cnT[:, :],
                             start=True, stop=True).then_inc(sems["sT"], 1)
                E.inc("sT", 1)
                gBC.mark("sT")
                bc_now = E.val("sT")
                if E.on("scalar"):
                    if n == 0:
                        e.wait_ge(sems["sV"], du_now)
                    gScan.wait(e, 16 * i + n - 2)
                    for m in range(4):
                        if m:
                            e.drain()
                        e.activation(out=dA_sc[:, m * TC:(m + 1) * TC],
                                     in_=dtT_ch[:, m * TC:(m + 1) * TC], func=AF.Exp,
                                     scale=aneg_sb[:, m * NSTATE + n: m * NSTATE + n + 1])
                    e.drain()
                    e.wait_ge(sems["sT"], bc_now)
                    gScan.wait(e, 16 * i + n - 2)
                    e.activation(out=b_buf[:, :], in_=pbB[:, :], func=AF.Copy)
                    e.drain()
                    e.activation(out=c_buf[:, :], in_=pbC[:, :], func=AF.Copy)
                    e.drain()
                    e.sem_inc(sems["sA"], 1)
                E.inc("sA", 1)
                gDA.mark("sA")
                gCe.mark("sA")
                da_now = E.val("sA")
                if E.on("vector"):
                    e.wait_ge(sems["sA"], da_now)
                    e.tensor_tensor(
                        out=dBu_sc[:, :].rearrange("p (m t) -> p m t", t=TC),
                        in0=du_bf[:, :].rearrange("p (m t) -> p m t", t=TC),
                        in1=b_buf[:, :].rearrange("p (o t) -> p o t", o=1).broadcast_to([128, 4, TC]),
                        op=ALU.mult)
                    e.drain()
                    for m in range(4):
                        e.tensor_tensor_scan(
                            out=s_buf[:, m * TC:(m + 1) * TC],
                            data0=dA_sc[:, m * TC:(m + 1) * TC],
                            data1=dBu_sc[:, m * TC:(m + 1) * TC],
                            initial=sstate[:, m * NSTATE + n: m * NSTATE + n + 1],
                            op0=ALU.mult, op1=ALU.add)
                    e.drain()
                    e.tensor_copy(
                        sstate[:, n:n + 1].rearrange("p (c m) -> p m c", c=1).broadcast_to([128, 4, 1])
                        if False else
                        sstate[:, :].rearrange("p (m n2) -> p m n2", n2=NSTATE)[:, :, n:n + 1],
                        s_buf[:, :].rearrange("p (m t) -> p m t", t=TC)[:, :, TC - 1:TC])
                    e.drain()
                    e.tensor_tensor(
                        out=s_buf[:, :].rearrange("p (m t) -> p m t", t=TC),
                        in0=s_buf[:, :].rearrange("p (m t) -> p m t", t=TC),
                        in1=c_buf[:, :].rearrange("p (o t) -> p o t", o=1).broadcast_to([128, 4, TC]),
                        op=ALU.mult)
                    e.drain()
                    e.tensor_tensor(out=yacc[:, :], in0=yacc[:, :], in1=s_buf[:, :], op=ALU.add)
                    e.drain()
                    e.sem_inc(sems["sV"], 1)
                E.inc("sV", 1)
                gScan.mark("sV")
            if E.on("vector"):
                for m in range(4):
                    e.scalar_tensor_tensor(
                        out=yacc[:, m * TC:(m + 1) * TC],
                        in0=hs_ch[:, m * TC:(m + 1) * TC],
                        scalar=dpar_sb[:, m:m + 1],
                        in1=yacc[:, m * TC:(m + 1) * TC],
                        op0=ALU.mult, op1=ALU.add)
                e.drain()
                E.dma_wait(e, "sQ")
                gBC.wait(e, 16 * i + 15)
                e.tensor_tensor(out=yg_ev[:, :], in0=yacc[:, :], in1=sg_ch[:, :], op=ALU.mult)
                e.drain()
                e.sem_inc(sems["sV"], 1)
            E.inc("sV", 1)
            gYg.mark("sV")
            yg_now = E.val("sV")
            if E.on("sync"):
                e.wait_ge(sems["sV"], yg_now)
                e.dma_start(out=h_yg_stage[owner, :, :, trow:trow + TC].transpose([1, 0, 2]),
                            in_=yg_ev[:, :].rearrange("p (m t) -> p m t", t=TC)).then_inc(sems["sQ"], 16)
            E.inc("sQ", 16)
            E.flush(e, "sQ")
            gSt.mark("sQ")
        return gSt

    def ph_a_out(E, gSt):
        e = E.eng
        if E.on("sync"):
            gSt.wait_last(e)
            e.sem_inc(sems["sA2"], 1)
        E.inc("sA2", 1)
        flp2 = E.val("sA2")
        if E.on("gpsimd"):
            e.wait_ge(sems["sA2"], flp2)
            with allow_ici_all_to_all():
                e.collective_compute(
                    "AllToAll", ALU.bypass, replica_groups=[list(range(NC))],
                    ins=[h_yg_stage[:, :, :, :].opt()], outs=[h_yg_recv[:, :, :, :].opt()],
                ).then_inc(sems["sCC"], 1)
        E.inc("sCC", 1)
        cc_now = E.val("sCC")
        gYL = E.guard()
        gWL = E.guard()
        gMM = E.guard()
        gAcc = E.guard()
        pass2_done_q = E.val("sQ")
        for kg in range(4):
            if E.on("sync"):
                e.wait_ge(sems["sCC"], cc_now)
                gMM.wait(e, 32 * kg - 1)
                e.dma_start(
                    out=ygf_sb[:, :].rearrange("p (k tb c) -> p k tb c", tb=8, c=128),
                    in_=h_yg_recv[kg * 2:(kg + 1) * 2, :, :, :]
                    .rearrange("s m p (tb c) -> p (s m) tb c", c=128),
                ).then_inc(sems["sQ"], 16)
            E.inc("sQ", 16)
            E.flush(e, "sQ")
            gYL.mark("sQ")
            if E.on("gpsimd"):
                if kg == 0:
                    e.wait_ge(sems["sQ"], pass2_done_q)
                gMM.wait(e, 32 * kg - 1)
                e.dma_start(
                    out=wout_kg[:, :].rearrange("p (kk c) -> p kk c", c=D),
                    in_=wout[kg * 1024:(kg + 1) * 1024, :].rearrange("(kk p) c -> p kk c", p=128),
                ).then_inc(sems["sG"], 16)
            E.inc("sG", 16)
            E.flush(e, "sG")
            gWL.mark("sG")
            for tb in range(8):
                for nq in range(4):
                    pb = [psA, psB, psC, psD][nq]
                    if E.on("tensor"):
                        if tb == 0 and nq == 0:
                            E.dma_wait(e, "sQ")
                            E.dma_wait(e, "sG")
                        gAcc.wait(e, 32 * kg + 4 * tb + nq - 4)
                        for kk in range(8):
                            last = e.matmul(pb[:, :],
                                            ygf_sb[:, (kk * 8 + tb) * 128:(kk * 8 + tb + 1) * 128],
                                            wout_kg[:, kk * D + nq * 512: kk * D + (nq + 1) * 512],
                                            start=(kk == 0), stop=(kk == 7))
                        last.then_inc(sems["sT"], 1)
                    E.inc("sT", 1)
                    gMM.mark("sT")
                    mm_now = E.val("sT")
                    if E.on("vector"):
                        e.wait_ge(sems["sT"], mm_now)
                        if "d_op" in dbg_t and tb == 0 and nq == 0:
                            e.tensor_copy(s_sc[:, kg * 512:(kg + 1) * 512], pb[:, :])
                            e.drain()
                        dst = res[:, tb * D + nq * 512: tb * D + (nq + 1) * 512]
                        e.tensor_tensor(out=dst, in0=dst, in1=pb[:, :], op=ALU.add)
                        e.drain()
                        e.sem_inc(sems["sV"], 1)
                    E.inc("sV", 1)
                    gAcc.mark("sV")
        r_now = E.val("sV")
        for nm in ("sync", "gpsimd", "scalar", "tensor"):
            if E.on(nm):
                e.wait_ge(sems["sV"], r_now)


    def ph_w8(E):
        e = E.eng
        jobs = []
        jobs.append((w8g_slice, h_w8g_own, 2, FF1))
        jobs.append((w8u_slice, h_w8u_own, 2, FF1))
        jobs.append((w8d_slice, h_w8d_own, 6, D))
        gLd = E.guard()
        gSc1 = E.guard()
        gV1 = E.guard()
        gSc2 = E.guard()
        gV2g = E.guard()
        gStq = E.guard()
        ci = 0
        for src_ap, dst_h, ntile, cols in jobs:
            for rt in range(ntile):
                for cc in range(cols // 2048):
                    ws, t1, qq, qi, bf = ((q_ws, q_t1, q_q, q_qi, q_bf) if ci % 2 == 0
                                          else (q_ws2, q_t12, q_q2, q_qi2, q_bf2))
                    lqs = LANES[(2 * ci) % 3]
                    sqs = LANES[(2 * ci + 1) % 3]
                    if E.on("sync"):
                        gV2g.wait(e, ci - 2)
                        E.lane_open(e, lqs)
                        e.dma_start(out=ws[:, :],
                                    in_=src_ap[rt * 128:(rt + 1) * 128, cc * 2048:(cc + 1) * 2048]
                                    ).then_inc(sems[lqs], 16)
                    E.inc(lqs, 16)
                    gLd.mark(lqs)
                    if E.on("scalar"):
                        gLd.wait(e, ci)
                        gV2g.wait(e, ci - 2)
                        e.activation(out=t1[:, :], in_=ws[:, :], func=AF.Abs)
                        e.drain()
                        e.activation(out=t1[:, :], in_=t1[:, :], func=AF.Ln, bias=K_TINY)
                        e.drain()
                        e.sem_inc(sems["sA"], 1)
                    E.inc("sA", 1)
                    gSc1.mark("sA")
                    if E.on("vector"):
                        gSc1.wait(e, ci)
                        e.tensor_scalar(out=t1[:, :], in0=t1[:, :], scalar1=INV_LN2,
                                        scalar2=-0.5, op0=ALU.mult, op1=ALU.add)
                        e.drain()
                        e.tensor_scalar(out=t1[:, :], in0=t1[:, :], scalar1=MAGIC,
                                        scalar2=None, op0=ALU.add)
                        e.drain()
                        e.tensor_scalar(out=t1[:, :], in0=t1[:, :], scalar1=-MAGIC,
                                        scalar2=-6.0, op0=ALU.add, op1=ALU.max)
                        e.drain()
                        e.sem_inc(sems["sV"], 1)
                    E.inc("sV", 1)
                    gV1.mark("sV")
                    if E.on("scalar"):
                        gV1.wait(e, ci)
                        gV2g.wait(e, ci - 2)
                        e.activation(out=qq[:, :], in_=t1[:, :], func=AF.Exp,
                                     scale=K_LN2, bias=K_N3LN2)
                        e.activation(out=qi[:, :], in_=t1[:, :], func=AF.Exp,
                                     scale=K_NLN2, bias=K_P3LN2)
                        e.drain()
                        e.sem_inc(sems["sA"], 1)
                    E.inc("sA", 1)
                    gSc2.mark("sA")
                    if E.on("vector"):
                        gSc2.wait(e, ci)
                        gStq.wait(e, ci - 2)
                        e.tensor_tensor(out=t1[:, :], in0=ws[:, :], in1=qi[:, :], op=ALU.mult)
                        e.drain()
                        e.tensor_scalar(out=t1[:, :], in0=t1[:, :], scalar1=MAGIC,
                                        scalar2=-MAGIC, op0=ALU.add, op1=ALU.add)
                        e.drain()
                        e.tensor_tensor(out=bf[:, :], in0=t1[:, :], in1=qq[:, :], op=ALU.mult)
                        e.drain()
                        e.sem_inc(sems["sV"], 1)
                    E.inc("sV", 1)
                    gV2g.mark("sV")
                    v2now = E.val("sV")
                    if E.on("sync"):
                        e.wait_ge(sems["sV"], v2now)
                        E.lane_open(e, sqs)
                        e.dma_start(out=dst_h[rt * 128:(rt + 1) * 128, cc * 2048:(cc + 1) * 2048],
                                    in_=bf[:, :]).then_inc(sems[sqs], 16)
                    E.inc(sqs, 16)
                    gStq.mark(sqs)
                    ci += 1
        if E.on("sync"):
            for _qs in LANES:
                e.wait_ge(sems[_qs], E.val(_qs))
            e.sem_inc(sems["sA2"], 1)
        E.inc("sA2", 1)
        flw8 = E.val("sA2")
        if E.on("gpsimd"):
            e.wait_ge(sems["sA2"], flw8)
            for own, full in ((h_w8g_own, h_w8g), (h_w8u_own, h_w8u), (h_w8d_own, h_w8d)):
                e.collective_compute(
                    "AllGather", ALU.bypass, replica_groups=[list(range(NC))],
                    ins=[own.opt()], outs=[full.opt()],
                ).then_inc(sems["sCC"], 1)
        E.inc("sCC", 3)

    def ph_b(E):
        e = E.eng
        gEvL = emit_ln_transpose(E, xnT_sb)
        ln_done = E.val("sV2")
        cc_need = E.val("sCC")
        w8g_f = h_w8g.rearrange("s p c -> (s p) c")
        w8u_f = h_w8u.rearrange("s p c -> (s p) c")
        w8d_f = h_w8d.rearrange("s p c -> (s p) c")
        gWL = E.guard()
        gWLgu = E.guard()
        gGU = E.guard()
        gMMg = E.guard()
        gMMu = E.guard()
        gMMd = E.guard()
        gAcc = E.guard()
        gZ = E.guard()
        gSilu = E.guard()
        gTpO = E.guard()
        gRes = E.guard()
        gPrep = E.guard()
        si = 0
        oi = 0
        for h in range(2):
            if E.on("vector"):
                if h == 0:
                    gEvL.wait_last(e)
                e.memset(b_acc[:, :], 0.0)
                e.drain()
                e.sem_inc(sems["sV"], 1)
            E.inc("sV", 1)
            gZ.mark("sV")
            for s in range(12):
                if E.on("sync"):
                    e.wait_ge(sems["sCC"], cc_need)
                    if h == 0 and s == 0:
                        e.wait_ge(sems["sV2"], ln_done)
                    gMMg.wait(e, (si - 1) * 4 + 3)
                    gMMu.wait(e, (si - 1) * 4 + 3)
                    e.dma_start(out=b_wgst[:, :].rearrange("p (k c) -> p k c", c=512),
                                in_=w8g_f[:, s * 512:(s + 1) * 512]
                                .rearrange("(k p) c -> p k c", p=128)).then_inc(sems["sQ"], 16)
                    e.dma_start(out=b_wust[:, :].rearrange("p (k c) -> p k c", c=512),
                                in_=w8u_f[:, s * 512:(s + 1) * 512]
                                .rearrange("(k p) c -> p k c", p=128)).then_inc(sems["sQ"], 16)
                E.inc("sQ", 32)
                E.flush(e, "sQ")
                gWLgu.mark("sFL")
                if E.on("sync"):
                    gMMd.wait(e, (si - 1) * 16 + 15)
                    e.dma_start(out=b_wdst[:, :].rearrange("p (ic c) -> p ic c", c=D),
                                in_=w8d_f[s * 512:(s + 1) * 512, :]
                                .rearrange("(ic p) c -> p ic c", p=128)).then_inc(sems["sQ"], 16)
                E.inc("sQ", 16)
                E.flush(e, "sQ")
                gWL.mark("sFL")
                for ic in range(4):
                    icg = s * 4 + ic
                    pbg = psA if ic % 2 == 0 else psB
                    pbu = psC if ic % 2 == 0 else psD
                    if E.on("tensor"):
                        if ic == 0:
                            gWLgu.wait(e, si)
                            if h == 0 and s == 0:
                                e.wait_ge(sems["sV2"], ln_done)
                        gSilu.wait(e, si * 4 + ic - 2)
                        for k in range(16):
                            last = e.matmul(pbg[:, :],
                                            b_wgst[:, k * 512 + ic * 128: k * 512 + (ic + 1) * 128],
                                            xnT_sb[:, k * TOK + h * 512: k * TOK + (h + 1) * 512],
                                            start=(k == 0), stop=(k == 15))
                        last.then_inc(sems["sT"], 1)
                    E.inc("sT", 1)
                    gMMg.mark("sT")
                    gmm = E.val("sT")
                    if E.on("tensor"):
                        gGU.wait(e, si * 4 + ic - 2)
                        for k in range(16):
                            last = e.matmul(pbu[:, :],
                                            b_wust[:, k * 512 + ic * 128: k * 512 + (ic + 1) * 128],
                                            xnT_sb[:, k * TOK + h * 512: k * TOK + (h + 1) * 512],
                                            start=(k == 0), stop=(k == 15))
                        last.then_inc(sems["sT"], 1)
                    E.inc("sT", 1)
                    gMMu.mark("sT")
                    umm = E.val("sT")
                    if E.on("scalar"):
                        e.wait_ge(sems["sT"], gmm)
                        gGU.wait(e, si * 4 + ic - 1)
                        e.activation(out=b_gs[:, :], in_=pbg[:, :], func=AF.Silu,
                                     scale=gsus_sb[:, icg:icg + 1])
                        e.drain()
                        e.sem_inc(sems["sA"], 1)
                    E.inc("sA", 1)
                    gSilu.mark("sA")
                    silu_now = E.val("sA")
                    if E.on("vector"):
                        e.wait_ge(sems["sT"], umm)
                        e.wait_ge(sems["sA"], silu_now)
                        e.tensor_scalar(out=b_us[:, :], in0=pbu[:, :],
                                        scalar1=gsus_sb[:, 48 + icg:48 + icg + 1],
                                        scalar2=None, op0=ALU.mult)
                        e.drain()
                        e.tensor_tensor(out=b_fuse[:, ic * 512:(ic + 1) * 512],
                                        in0=b_gs[:, :], in1=b_us[:, :], op=ALU.mult)
                        e.drain()
                        e.sem_inc(sems["sV"], 1)
                    E.inc("sV", 1)
                    gGU.mark("sV")
                fuse_done = E.val("sV")
                for oc in range(16):
                    pb = psG if oc % 2 == 0 else psH
                    if E.on("tensor"):
                        if oc == 0:
                            e.wait_ge(sems["sV"], fuse_done)
                            gWL.wait(e, si)
                            if s == 0:
                                gZ.wait(e, h)
                        gAcc.wait(e, si * 16 + oc - 2)
                        for ic in range(4):
                            last = e.matmul(pb[:, :],
                                            b_wdst[:, ic * D + oc * 128: ic * D + (oc + 1) * 128],
                                            b_fuse[:, ic * 512:(ic + 1) * 512],
                                            start=(ic == 0), stop=(ic == 3))
                        last.then_inc(sems["sT"], 1)
                    E.inc("sT", 1)
                    gMMd.mark("sT")
                    dmm = E.val("sT")
                    if E.on("vector"):
                        e.wait_ge(sems["sT"], dmm)
                        if s == 0 and oc == 0:
                            gZ.wait(e, h)
                        dst = b_acc[:, oc * 512:(oc + 1) * 512]
                        e.tensor_tensor(out=dst, in0=dst, in1=pb[:, :], op=ALU.add)
                        e.drain()
                        e.sem_inc(sems["sV"], 1)
                    E.inc("sV", 1)
                    gAcc.mark("sV")
                si += 1
            for oc in range(16):
                if E.on("vector"):
                    gTpO.wait(e, oi - 1)
                    e.tensor_scalar(out=b_tb[:, :], in0=b_acc[:, oc * 512:(oc + 1) * 512],
                                    scalar1=gsus_sb[:, 96 + oc:96 + oc + 1],
                                    scalar2=None, op0=ALU.mult)
                    e.drain()
                    e.sem_inc(sems["sV"], 1)
                E.inc("sV", 1)
                gPrep.mark("sV")
                prep_now = E.val("sV")
                if E.on("tensor"):
                    e.wait_ge(sems["sV"], prep_now)
                    gRes.wait(e, oi - 1)
                    for tc4 in range(4):
                        last = e.transpose(psE[:, tc4 * 128:(tc4 + 1) * 128],
                                           b_tb[:, tc4 * 128:(tc4 + 1) * 128], ident[:, :])
                    last.then_inc(sems["sT"], 1)
                E.inc("sT", 1)
                gTpO.mark("sT")
                tp_now = E.val("sT")
                if E.on("vector"):
                    e.wait_ge(sems["sT"], tp_now)
                    for tc4 in range(4):
                        tt_g = h * 4 + tc4
                        dst = res[:, tt_g * D + oc * 128: tt_g * D + (oc + 1) * 128]
                        e.tensor_tensor(out=dst, in0=dst,
                                        in1=psE[:, tc4 * 128:(tc4 + 1) * 128], op=ALU.add)
                    e.drain()
                    e.sem_inc(sems["sV"], 1)
                E.inc("sV", 1)
                gRes.mark("sV")
                oi += 1
        r_now = E.val("sV")
        for nm in ("sync", "gpsimd", "scalar", "tensor"):
            if E.on(nm):
                e.wait_ge(sems["sV"], r_now)



    def ph_wth(E):
        e = E.eng
        if E.on("sync"):
            e.dma_start(out=identf[:, :], in_=c_identf[:, :]).then_inc(sems["sQ"], 16)
            e.dma_start(out=onesf[:, :], in_=c_onesf[:, :]).then_inc(sems["sQ"], 16)
        E.inc("sQ", 32)
        E.flush(e, "sQ")
        jobs = [(wg_slice, 8, D), (wu_slice, 8, D), (wd_slice, 16, ISL)]
        gLd = E.guard()
        gRs = E.guard()
        ci = 0
        for j, (src_ap, ntile, cols) in enumerate(jobs):
            for rt in range(ntile):
                tw = t_ws if ci % 2 == 0 else t_wsB
                lqs = LANES[ci % 3]
                if E.on("sync"):
                    gRs.wait(e, ci - 2)
                    E.lane_open(e, lqs)
                    e.dma_start(out=tw[:, 0:cols], in_=src_ap[rt * 128:(rt + 1) * 128, :]
                                ).then_inc(sems[lqs], 16)
                E.inc(lqs, 16)
                gLd.mark(lqs)
                if E.on("vector"):
                    gLd.wait(e, ci)
                    if ci == 0:
                        e.memset(racc[:, :], 0.0)
                        e.drain()
                    e.tensor_reduce(out=d_amax[:, :], in_=tw[:, 0:cols], axis=AX.X,
                                    op=ALU.add, apply_absolute_value=True)
                    e.drain()
                    e.tensor_tensor(out=racc[:, j:j + 1], in0=racc[:, j:j + 1],
                                    in1=d_amax[:, :], op=ALU.add)
                    e.drain()
                    e.sem_inc(sems["sV"], 1)
                E.inc("sV", 1)
                gRs.mark("sV")
                ci += 1
        rs_done = E.val("sV")
        if E.on("tensor"):
            e.wait_ge(sems["sV"], rs_done)
            E.dma_wait(e, "sQ")
            e.transpose(psA[0:3, 0:128], racc[:, :], identf[:, :]).then_inc(sems["sT"], 1)
        E.inc("sT", 1)
        tp_now = E.val("sT")
        if E.on("vector"):
            e.wait_ge(sems["sT"], tp_now)
            e.tensor_reduce(out=d_amax[0:3, :], in_=psA[0:3, 0:128], axis=AX.X, op=ALU.add)
            e.drain()
            e.sem_inc(sems["sV"], 1)
        E.inc("sV", 1)
        ps_done = E.val("sV")
        if E.on("sync"):
            e.wait_ge(sems["sV"], ps_done)
            e.dma_start(out=h_th_own[:, :], in_=d_amax[0:3, 0:1]).then_inc(sems["sQ"], 16)
            e.wait_ge(sems["sQ"], E.val("sQ") + 16)
            e.sem_inc(sems["sA2"], 1)
        E.inc("sQ", 16)
        E.flush(e, "sQ")
        E.inc("sA2", 1)
        fl = E.val("sA2")
        if E.on("gpsimd"):
            e.wait_ge(sems["sA2"], fl)
            e.collective_compute(
                "AllReduce", ALU.add, replica_groups=[list(range(NC))],
                ins=[h_th_own.opt()], outs=[h_th_all.opt()],
            ).then_inc(sems["sCC"], 1)
        E.inc("sCC", 1)
        cc_th = E.val("sCC")
        if E.on("sync"):
            e.wait_ge(sems["sCC"], cc_th)
            e.dma_start(out=th_row[:, :], in_=h_th_all[:, :].transpose([1, 0])).then_inc(sems["sQ"], 16)
        E.inc("sQ", 16)
        E.flush(e, "sQ")
        th_ld = E.val("sQ")
        if E.on("tensor"):
            e.wait_ge(sems["sQ"], th_ld)
            e.matmul(psB[:, 0:3], onesf[:, :], th_row[:, :], start=True, stop=True
                     ).then_inc(sems["sT"], 1)
        E.inc("sT", 1)
        bc_now = E.val("sT")
        NW = float(DFF * D)
        if E.on("vector"):
            e.wait_ge(sems["sT"], bc_now)
            e.tensor_scalar(out=mbc[:, :], in0=psB[:, 0:3], scalar1=1.0 / NW,
                            scalar2=None, op0=ALU.mult)
            e.drain()
            e.tensor_scalar(out=thbc[:, :], in0=mbc[:, :], scalar1=0.5,
                            scalar2=None, op0=ALU.mult)
            e.drain()
            e.sem_inc(sems["sV"], 1)
        E.inc("sV", 1)
        th_done = E.val("sV")
        nth = fv(arB, 4096, 4)
        if E.on("vector"):
            e.wait_ge(sems["sV"], th_done)
            e.tensor_scalar(out=nth[:, 0:3], in0=thbc[:, :], scalar1=-1.0,
                            scalar2=None, op0=ALU.mult)
            e.drain()
            e.sem_inc(sems["sV"], 1)
        E.inc("sV", 1)
        nth_done = E.val("sV")
        jobs2 = [(wg_slice, h_wgT, 8, D),
                 (wu_slice, h_wuT, 8, D),
                 (wd_slice, h_wdT_own, 16, ISL)]
        gTn = E.guard()
        gTp2 = E.guard()
        gEv2 = E.guard()
        gLd2 = E.guard()
        gSt2 = E.guard()
        ci = 0
        gti = 0
        for j, (src_ap, dst_h, ntile, cols) in enumerate(jobs2):
            ncb = cols // 128
            for rt in range(ntile):
                tw = t_ws if ci % 2 == 0 else t_wsB
                tt = t_tern if ci % 2 == 0 else t_ternB
                tp = t_tp if ci % 2 == 0 else t_tpB
                lqs = LANES[(2 * ci) % 3]
                sqs = LANES[(2 * ci + 1) % 3]
                if E.on("sync"):
                    gTn.wait(e, ci - 2)
                    E.lane_open(e, lqs)
                    e.dma_start(out=tw[:, 0:cols], in_=src_ap[rt * 128:(rt + 1) * 128, :]
                                ).then_inc(sems[lqs], 16)
                E.inc(lqs, 16)
                gLd2.mark(lqs)
                if E.on("vector"):
                    gLd2.wait(e, ci)
                    if ci == 0:
                        e.wait_ge(sems["sV"], nth_done)
                    gTp2.wait(e, ci - 2)
                    e.tensor_scalar(out=t_le[:, 0:cols], in0=tw[:, 0:cols],
                                    scalar1=nth[:, j:j + 1], scalar2=None, op0=ALU.is_le)
                    e.drain()
                    e.scalar_tensor_tensor(out=tt[:, 0:cols], in0=tw[:, 0:cols],
                                           scalar=thbc[:, j:j + 1], op0=ALU.is_ge,
                                           in1=t_le[:, 0:cols], op1=ALU.subtract)
                    e.drain()
                    e.sem_inc(sems["sV"], 1)
                E.inc("sV", 1)
                gTn.mark("sV")
                tn_now = E.val("sV")
                for g8 in range((ncb + 7) // 8):
                    nh = min(8, ncb - g8 * 8)
                    pbank = psE if gti % 2 == 0 else psF
                    gti += 1
                    if E.on("tensor"):
                        if g8 == 0:
                            e.wait_ge(sems["sV"], tn_now)
                        gEv2.wait(e, len(gEv2.marks) - 2)
                        for cb8 in range(nh):
                            cb = g8 * 8 + cb8
                            last = e.transpose(pbank[:, cb8 * 128:(cb8 + 1) * 128],
                                               tt[:, cb * 128:(cb + 1) * 128], ident[:, :])
                        last.then_inc(sems["sT"], 1)
                    E.inc("sT", 1)
                    if g8 == (ncb + 7) // 8 - 1:
                        gTp2.mark("sT")
                    tg_now = E.val("sT")
                    if E.on("vector"):
                        e.wait_ge(sems["sT"], tg_now)
                        if g8 == 0:
                            gSt2.wait(e, ci - 2)
                        e.tensor_copy(tp[:, g8 * 1024: g8 * 1024 + nh * 128],
                                      pbank[:, 0:nh * 128])
                        e.drain()
                        e.sem_inc(sems["sV"], 1)
                    E.inc("sV", 1)
                    gEv2.mark("sV")
                ev_now = E.val("sV")
                if E.on("sync"):
                    e.wait_ge(sems["sV"], ev_now)
                    E.lane_open(e, sqs)
                    e.dma_start(out=dst_h[:, rt * 128:(rt + 1) * 128]
                                .rearrange("(cb p) c -> p cb c", p=128),
                                in_=tp[:, 0:ncb * 128].rearrange("p (cb c) -> p cb c", c=128)
                                ).then_inc(sems[sqs], 16)
                E.inc(sqs, 16)
                gSt2.mark(sqs)
                ci += 1
        if E.on("sync"):
            for _qs in LANES:
                e.wait_ge(sems[_qs], E.val(_qs))
            e.sem_inc(sems["sA2"], 1)
        E.inc("sA2", 1)
        fl2 = E.val("sA2")
        if E.on("gpsimd"):
            e.wait_ge(sems["sA2"], fl2)
            e.collective_compute(
                "AllGather", ALU.bypass, replica_groups=[list(range(NC))],
                ins=[h_wdT_own.opt()], outs=[h_wdT.opt()],
            ).then_inc(sems["sCC"], 1)
        E.inc("sCC", 1)
        r_now = E.val("sV")
        for nm in ("sync", "gpsimd", "scalar", "tensor"):
            if E.on(nm):
                e.wait_ge(sems["sV"], r_now)

    def ph_c(E):
        e = E.eng
        gEvL = emit_ln_transpose(E, xnT_sb)
        if E.on("sync"):
            gEvL.wait_last(e)
            e.dma_start(out=h_xn3_own[:, :, :].transpose([1, 0, 2]),
                        in_=xnT_sb[:, :].rearrange("p (f t) -> p f t", t=TOK)).then_inc(sems["sQ"], 16)
        E.inc("sQ", 16)
        E.flush(e, "sQ")
        st0 = E.val("sQ")
        if E.on("sync"):
            e.wait_ge(sems["sQ"], st0)
            e.sem_inc(sems["sA2"], 1)
        E.inc("sA2", 1)
        fl0 = E.val("sA2")
        if E.on("gpsimd"):
            e.wait_ge(sems["sA2"], fl0)
            e.collective_compute(
                "AllGather", ALU.bypass, replica_groups=[list(range(NC))],
                ins=[h_xn3_own.opt()], outs=[h_xn3_all.opt()],
            ).then_inc(sems["sCC"], 1)
        E.inc("sCC", 1)
        cc0 = E.val("sCC")
        if E.on("gpsimd"):
            e.dma_start(out=wi_h_sb[:, 0:16 * 256].rearrange("p (k c) -> p k c", c=256),
                        in_=wq_s.rearrange("(k p) c -> p k c", p=128)).then_inc(sems["sG"], 16)
            e.dma_start(out=wi_g_sb[:, 0:16 * 256].rearrange("p (k c) -> p k c", c=256),
                        in_=wk_s.rearrange("(k p) c -> p k c", p=128)).then_inc(sems["sG"], 16)
            e.dma_start(out=c_wv[:, :].rearrange("p (k c) -> p k c", c=256),
                        in_=wv_s.rearrange("(k p) c -> p k c", p=128)).then_inc(sems["sG"], 16)
            e.dma_start(out=bv_sb[:, :], in_=bv_row[:, :]).then_inc(sems["sG"], 16)
        E.inc("sG", 64)
        E.flush(e, "sG")
        if E.on("sync"):
            e.dma_start(out=bqk_sb[:, 0:2], in_=bq_pp[:, :]).then_inc(sems["sQ"], 16)
            e.dma_start(out=bqk_sb[:, 2:4], in_=bk_pp[:, :]).then_inc(sems["sQ"], 16)
        E.inc("sQ", 32)
        E.flush(e, "sQ")
        gMM = E.guard()
        gQKe = E.guard()
        gVe = E.guard()
        gQKSt = E.guard()
        gVSt = E.guard()
        for i in range(NTC):
            owner, trow = i // 2, (i % 2) * TC
            buf = xn_mov0 if i % 2 == 0 else xn_mov1
            if E.on("sync"):
                e.wait_ge(sems["sCC"], cc0)
                gMM.wait(e, i - 2)
                e.dma_start(out=buf[:, :].rearrange("p (f t) -> p f t", t=TC),
                            in_=h_xn3_all[owner, :, :, trow:trow + TC]
                            .transpose([1, 0, 2])).then_inc(sems["sQ"], 16)
            E.inc("sQ", 16)
            E.flush(e, "sQ")
            if E.on("tensor"):
                E.dma_wait(e, "sQ")
                E.dma_wait(e, "sG")
            for qk in range(2):
                wsb = wi_h_sb if qk == 0 else wi_g_sb
                for m in range(2):
                    pb = psA if (qk * 2 + m) % 2 == 0 else psB
                    if E.on("tensor"):
                        gQKe.wait(e, 4 * i + qk * 2 + m - 2)
                        for k in range(16):
                            last = e.matmul(pb[:, :],
                                            wsb[:, k * 256 + m * 128: k * 256 + (m + 1) * 128],
                                            buf[:, k * TC:(k + 1) * TC],
                                            start=(k == 0), stop=(k == 15))
                        last.then_inc(sems["sT"], 1)
                    E.inc("sT", 1)
                    mm_now = E.val("sT")
                    qki = len(gQKSt.marks)
                    qkb = c_qkev if qki % 2 == 0 else c_qkev2
                    lqs = "sQ2" if qki % 2 == 0 else "sQ3"
                    if E.on("vector"):
                        e.wait_ge(sems["sT"], mm_now)
                        gQKSt.wait(e, qki - 2)
                        e.tensor_scalar(out=qkb[:, :], in0=pb[:, :],
                                        scalar1=bqk_sb[:, qk * 2 + m: qk * 2 + m + 1],
                                        scalar2=None, op0=ALU.add)
                        e.drain()
                        e.sem_inc(sems["sV"], 1)
                    E.inc("sV", 1)
                    gQKe.mark("sV")
                    ev_now = E.val("sV")
                    if E.on("sync"):
                        e.wait_ge(sems["sV"], ev_now)
                        E.lane_open(e, lqs)
                        e.dma_start(out=h_qkT[qk, m, :, i * TC:(i + 1) * TC],
                                    in_=qkb[:, :]).then_inc(sems[lqs], 16)
                    E.inc(lqs, 16)
                    gQKSt.mark(lqs)
            for tc4 in range(4):
                pb = psC if tc4 % 2 == 0 else psD
                if E.on("tensor"):
                    gVe.wait(e, 4 * i + tc4 - 2)
                    for k in range(16):
                        e.matmul(pb[:, 0:256],
                                 buf[:, k * TC + tc4 * 128: k * TC + (tc4 + 1) * 128],
                                 c_wv[:, k * 256:(k + 1) * 256],
                                 start=(k == 0), stop=False)
                    e.matmul(pb[:, 0:256], ones_sb[:, :], bv_sb[:, :],
                             start=False, stop=True).then_inc(sems["sT"], 1)
                E.inc("sT", 1)
                mm_now = E.val("sT")
                vi = len(gVSt.marks)
                vb = c_vev if vi % 2 == 0 else c_vev2
                lqs = "sQ2" if vi % 2 == 0 else "sQ3"
                if E.on("vector"):
                    e.wait_ge(sems["sT"], mm_now)
                    gVSt.wait(e, vi - 2)
                    e.tensor_copy(vb[:, 0:256], pb[:, 0:256])
                    e.drain()
                    e.sem_inc(sems["sV"], 1)
                E.inc("sV", 1)
                gVe.mark("sV")
                ev_now = E.val("sV")
                if E.on("sync"):
                    e.wait_ge(sems["sV"], ev_now)
                    E.lane_open(e, lqs)
                    e.dma_start(out=h_v[:, i * TC + tc4 * 128: i * TC + (tc4 + 1) * 128, :]
                                .transpose([1, 0, 2]),
                                in_=vb[:, 0:256].rearrange("p (u c) -> p u c", c=HD)
                                ).then_inc(sems[lqs], 16)
                E.inc(lqs, 16)
                gVSt.mark(lqs)
            gMM.mark("sT")
        qkv_done_q = E.val("sQ")
        qkv_done_q2 = E.val("sQ2")
        qkv_done_q3 = E.val("sQ3")
        gSe = E.guard()
        gSmm = E.guard()
        gPX = E.guard()
        gPTe = E.guard()
        gOmm = E.guard()
        gOev = E.guard()
        sbanks = [psA, psB, psC, psD]
        NBLK = [((qc + 1) * 128 + 511) // 512 for qc in range(32)]
        CUM = [sum(NBLK[:qc]) for qc in range(32)]
        UB = sum(NBLK)
        smm_idx = {}

        def emit_smm(u, qc, blk):
            kpref = (qc + 1) * 128
            n = min(512, kpref - blk * 512)
            j = u * UB + CUM[qc] + blk
            pb = sbanks[j % 4]
            if E.on("tensor"):
                if qc == 0 and blk == 0:
                    E.dma_wait(e, "sQ")
                gSe.wait(e, j - 4)
                e.matmul(pb[:, 0:n], c_qt[:, qc * 128:(qc + 1) * 128],
                         c_kt[:, blk * 512: blk * 512 + n],
                         start=True, stop=True).then_inc(sems["sT"], 1)
            E.inc("sT", 1)
            gSmm.mark("sT")
            smm_idx[j] = len(gSmm.marks) - 1

        for b in range(2):
            for kh in range(2):
                u = b * 2 + kh
                if E.on("sync"):
                    e.wait_ge(sems["sQ"], qkv_done_q)
                    e.wait_ge(sems["sQ2"], qkv_done_q2)
                    e.wait_ge(sems["sQ3"], qkv_done_q3)
                    gOmm.wait(e, 32 * u - 1)
                    e.dma_start(out=c_qt[:, :], in_=h_qkT[0, kh, :, b * 4096:(b + 1) * 4096]
                                ).then_inc(sems["sQ"], 16)
                    e.dma_start(out=c_kt[:, :], in_=h_qkT[1, kh, :, b * 4096:(b + 1) * 4096]
                                ).then_inc(sems["sQ"], 16)
                    e.dma_start(out=c_v[:, :].rearrange("p (kc c) -> p kc c", c=HD),
                                in_=h_v[kh, b * 4096:(b + 1) * 4096, :]
                                .rearrange("(kc p) c -> p kc c", p=128)).then_inc(sems["sQ"], 16)
                E.inc("sQ", 48)
                E.flush(e, "sQ")
                for blk in range(min(4, NBLK[0])):
                    emit_smm(u, 0, blk)
                for it in range(33):
                    if it < 32:
                        qc = it
                        kpref = (qc + 1) * 128
                        nblk = NBLK[qc]
                        sb = 48 + 4 * (qc % 2)
                        cpb = c_p if qc % 2 == 0 else c_p1
                        for blk in range(nblk):
                            n = min(512, kpref - blk * 512)
                            j = u * UB + CUM[qc] + blk
                            pb = sbanks[j % 4]
                            if E.on("vector"):
                                gSmm.wait(e, smm_idx[j])
                                if blk == 0:
                                    gPX.wait(e, len(gPX.marks) - 1)
                                e.tensor_copy(c_srow[:, blk * 512: blk * 512 + n], pb[:, 0:n])
                                e.drain()
                                e.sem_inc(sems["sV"], 1)
                            E.inc("sV", 1)
                            gSe.mark("sV")
                            if blk + 4 < nblk:
                                emit_smm(u, qc, blk + 4)
                        if qc + 1 < 32:
                            for blk in range(min(4, NBLK[qc + 1])):
                                emit_smm(u, qc + 1, blk)
                        if E.on("vector"):
                            dg = c_srow[:, qc * 128:(qc + 1) * 128]
                            e.tensor_tensor(out=dg, in0=dg, in1=mask_sb[:, :], op=ALU.add)
                            e.drain()
                            e.tensor_reduce(out=stat_sb[:, sb:sb + 1], in_=c_srow[:, 0:kpref],
                                            axis=AX.X, op=ALU.max, negate=True)
                            e.drain()
                            e.tensor_scalar(out=stat_sb[:, sb + 1:sb + 2], in0=stat_sb[:, sb:sb + 1],
                                            scalar1=ATT_SCALE, scalar2=None, op0=ALU.mult)
                            e.drain()
                            e.sem_inc(sems["sV"], 1)
                        E.inc("sV", 1)
                        mx_now = E.val("sV")
                        if E.on("scalar"):
                            e.wait_ge(sems["sV"], mx_now)
                            e.activation(out=cpb[:, 0:kpref], in_=c_srow[:, 0:kpref], func=AF.Exp,
                                         scale=K_ATTS, bias=stat_sb[:, sb + 1:sb + 2],
                                         accum_out=stat_sb[:, sb + 2:sb + 3]).then_inc(sems["sA"], 1)
                        E.inc("sA", 1)
                        gPX.mark("sA")
                    if it >= 1:
                        r = it - 1
                        rb = 48 + 4 * (r % 2)
                        rpb = c_p if r % 2 == 0 else c_p1
                        px_r = gPX.marks[32 * u + r][1]
                        if E.on("vector"):
                            e.wait_ge(sems["sA"], px_r)
                            e.reciprocal(out=stat_sb[:, rb + 3:rb + 4], in_=stat_sb[:, rb + 2:rb + 3])
                            e.drain()
                            e.sem_inc(sems["sV"], 1)
                        E.inc("sV", 1)
                        nkc = r + 1
                        for kcg in range((nkc + 7) // 8):
                            pbank = psE if kcg % 2 == 0 else psF
                            kchi = min(8, nkc - kcg * 8)
                            if E.on("tensor"):
                                gPX.wait(e, 32 * u + r)
                                gPTe.wait(e, len(gPTe.marks) - 2)
                                for k8 in range(kchi):
                                    kc = kcg * 8 + k8
                                    last = e.transpose(pbank[:, k8 * 128:(k8 + 1) * 128],
                                                       rpb[:, kc * 128:(kc + 1) * 128], ident[:, :])
                                last.then_inc(sems["sT"], 1)
                            E.inc("sT", 1)
                            tp_now = E.val("sT")
                            if E.on("vector"):
                                e.wait_ge(sems["sT"], tp_now)
                                e.tensor_copy(c_pt[:, kcg * 1024: kcg * 1024 + kchi * 128],
                                              pbank[:, 0:kchi * 128])
                                e.drain()
                                e.sem_inc(sems["sV"], 1)
                            E.inc("sV", 1)
                            gPTe.mark("sV")
                        pt_done = E.val("sV")
                        if E.on("tensor"):
                            e.wait_ge(sems["sV"], pt_done)
                            gOev.wait(e, len(gOev.marks) - 1)
                            for kc in range(nkc):
                                last = e.matmul(psG[:, 0:HD], c_pt[:, kc * 128:(kc + 1) * 128],
                                                c_v[:, kc * 128:(kc + 1) * 128],
                                                start=(kc == 0), stop=(kc == nkc - 1))
                            last.then_inc(sems["sT"], 1)
                        E.inc("sT", 1)
                        gOmm.mark("sT")
                        omm = E.val("sT")
                        if E.on("vector"):
                            e.wait_ge(sems["sT"], omm)
                            if r % 8 == 0:
                                E.dma_wait(e, "sQ")
                            e.tensor_scalar(out=c_osb[:, (r % 8) * 128:(r % 8 + 1) * 128],
                                            in0=psG[:, 0:HD], scalar1=stat_sb[:, rb + 3:rb + 4],
                                            scalar2=None, op0=ALU.mult)
                            e.drain()
                            e.sem_inc(sems["sV"], 1)
                        E.inc("sV", 1)
                        gOev.mark("sV")
                        oev_now = E.val("sV")
                        if r % 8 == 7:
                            if E.on("sync"):
                                e.wait_ge(sems["sV"], oev_now)
                                e.dma_start(out=h_o_stage[b * 4 + r // 8, kh, :, :]
                                            .rearrange("(qs p) c -> p qs c", p=128),
                                            in_=c_osb[:, :].rearrange("p (qs c) -> p qs c", c=HD)
                                            ).then_inc(sems["sQ"], 16)
                            E.inc("sQ", 16)
                            E.flush(e, "sQ")
        att_st = E.val("sQ")
        if E.on("sync"):
            e.wait_ge(sems["sQ"], att_st)
            e.sem_inc(sems["sA2"], 1)
        E.inc("sA2", 1)
        flush_now = E.val("sA2")
        if E.on("gpsimd"):
            e.wait_ge(sems["sA2"], flush_now)
            with allow_ici_all_to_all():
                e.collective_compute(
                    "AllToAll", ALU.bypass, replica_groups=[list(range(NC))],
                    ins=[h_o_stage.opt()], outs=[h_o_recv.opt()],
                ).then_inc(sems["sCC"], 1)
        E.inc("sCC", 1)
        cc1 = E.val("sCC")
        gRa = E.guard()
        for tt in range(8):
            if E.on("sync"):
                e.wait_ge(sems["sCC"], cc1)
                gRa.wait(e, tt - 1)
                e.dma_start(out=c_att[:, :].rearrange("p (s k c) -> p s k c", k=2, c=HD),
                            in_=h_o_recv[:, :, tt * 128:(tt + 1) * 128, :]
                            .transpose([2, 0, 1, 3])).then_inc(sems["sQ"], 16)
            E.inc("sQ", 16)
            E.flush(e, "sQ")
            if E.on("vector"):
                E.dma_wait(e, "sQ")
                e.tensor_tensor(out=res[:, tt * D:(tt + 1) * D], in0=res[:, tt * D:(tt + 1) * D],
                                in1=c_att[:, :], op=ALU.add)
                e.drain()
                e.sem_inc(sems["sV"], 1)
            E.inc("sV", 1)
            gRa.mark("sV")
        r_now = E.val("sV")
        for nm in ("sync", "gpsimd", "scalar", "tensor"):
            if E.on(nm):
                e.wait_ge(sems["sV"], r_now)


    def ph_d(E):
        e = E.eng
        if E.on("scalar"):
            e.wait_ge(sems["sV"], E.val("sV"))
            E.dma_wait(e, "sQ")
            E.dma_wait(e, "sG")
            for tt in range(8):
                if tt:
                    e.drain()
                e.activation(out=ln_sq[:, :], in_=res[:, tt * D:(tt + 1) * D],
                             func=AF.Square, accum_out=ln_s2[:, tt:tt + 1])
            e.drain()
            e.sem_inc(sems["sA"], 1)
        E.inc("sA", 1)
        sq_done = E.val("sA")
        if E.on("vector"):
            for tt in range(8):
                e.tensor_reduce(out=ln_s1[:, tt:tt + 1], in_=res[:, tt * D:(tt + 1) * D],
                                axis=AX.X, op=ALU.add)
            e.drain()
            e.wait_ge(sems["sA"], sq_done)
            e.tensor_scalar(out=ln_mean[:, :], in0=ln_s1[:, :], scalar1=1.0 / D,
                            scalar2=None, op0=ALU.mult)
            e.tensor_scalar(out=ln_tmp[:, :], in0=ln_s2[:, :], scalar1=1.0 / D,
                            scalar2=None, op0=ALU.mult)
            e.drain()
            e.tensor_tensor(out=ln_s2[:, :], in0=ln_mean[:, :], in1=ln_mean[:, :], op=ALU.mult)
            e.drain()
            e.tensor_tensor(out=ln_tmp[:, :], in0=ln_tmp[:, :], in1=ln_s2[:, :], op=ALU.subtract)
            e.drain()
            e.sem_inc(sems["sV"], 1)
        E.inc("sV", 1)
        var_done = E.val("sV")
        if E.on("scalar"):
            e.wait_ge(sems["sV"], var_done)
            e.activation(out=ln_tmp[:, :], in_=ln_tmp[:, :], func=AF.Sqrt,
                         bias=K_LNEPS).then_inc(sems["sA"], 1)
        E.inc("sA", 1)
        std_done = E.val("sA")
        first_v = True
        gN = E.guard()
        gT = E.guard()
        gEv = E.guard()
        q1 = fv(arB, 0, 2048)
        for tt in range(8):
            lx = ln_xn0 if tt % 2 == 0 else ln_xn1
            if E.on("vector"):
                if first_v:
                    e.wait_ge(sems["sA"], std_done)
                    e.reciprocal(out=ln_rstd[:, :], in_=ln_tmp[:, :])
                    e.drain()
                    first_v = False
                gT.wait(e, tt - 2)
                e.tensor_scalar(out=lx[:, :], in0=res[:, tt * D:(tt + 1) * D],
                                scalar1=ln_mean[:, tt:tt + 1], scalar2=ln_rstd[:, tt:tt + 1],
                                op0=ALU.subtract, op1=ALU.mult)
                e.drain()
                e.tensor_reduce(out=d_amax[:, :], in_=lx[:, :], axis=AX.X, op=ALU.max,
                                apply_absolute_value=True)
                e.drain()
                e.tensor_scalar(out=d_amax[:, :], in0=d_amax[:, :], scalar1=1e-5,
                                scalar2=None, op0=ALU.max)
                e.drain()
                e.reciprocal(out=d_scal[:, :], in_=d_amax[:, :])
                e.drain()
                e.tensor_scalar(out=d_scal[:, :], in0=d_scal[:, :], scalar1=128.0,
                                scalar2=None, op0=ALU.mult)
                e.tensor_scalar(out=d_inv2[:, :], in0=d_amax[:, :], scalar1=1.0 / 128.0,
                                scalar2=None, op0=ALU.mult)
                e.drain()
                e.tensor_scalar(out=q1[:, :], in0=lx[:, :], scalar1=d_scal[:, 0:1],
                                scalar2=None, op0=ALU.mult)
                e.drain()
                e.tensor_scalar(out=q1[:, :], in0=q1[:, :], scalar1=MAGIC,
                                scalar2=-MAGIC, op0=ALU.add, op1=ALU.add)
                e.drain()
                e.tensor_scalar(out=q1[:, :], in0=q1[:, :], scalar1=127.0,
                                scalar2=-128.0, op0=ALU.min, op1=ALU.max)
                e.drain()
                e.tensor_scalar(out=lx[:, :], in0=q1[:, :], scalar1=d_inv2[:, 0:1],
                                scalar2=None, op0=ALU.mult)
                e.drain()
                e.sem_inc(sems["sV"], 1)
            E.inc("sV", 1)
            gN.mark("sV")
            if E.on("tensor"):
                gN.wait(e, tt)
                gEv.wait(e, tt - 1)
                for g in range(2):
                    pbank = psE if g == 0 else psF
                    for f8 in range(8):
                        f = g * 8 + f8
                        last = e.transpose(pbank[:, f8 * 128:(f8 + 1) * 128],
                                           lx[:, f * 128:(f + 1) * 128], ident[:, :])
                last.then_inc(sems["sT"], 1)
            E.inc("sT", 1)
            gT.mark("sT")
            if E.on("vector"):
                gT.wait(e, tt)
                for g in range(2):
                    pbank = psE if g == 0 else psF
                    for f8 in range(8):
                        f = g * 8 + f8
                        e.tensor_copy(xnT_sb[:, f * TOK + tt * 128: f * TOK + (tt + 1) * 128],
                                      pbank[:, f8 * 128:(f8 + 1) * 128])
                e.drain()
                e.sem_inc(sems["sV2"], 1)
            E.inc("sV2", 1)
            gEv.mark("sV2")
        aq_done = E.val("sV2")
        if E.on("sync"):
            e.wait_ge(sems["sV2"], aq_done)
            e.dma_start(out=h_a_own[:, :, :].transpose([1, 0, 2]),
                        in_=xnT_sb[:, :].rearrange("p (f t) -> p f t", t=TOK)).then_inc(sems["sQ"], 16)
            e.wait_ge(sems["sQ"], E.val("sQ") + 16)
            e.sem_inc(sems["sA2"], 1)
        E.inc("sQ", 16)
        E.flush(e, "sQ")
        E.inc("sA2", 1)
        fla = E.val("sA2")
        if E.on("gpsimd"):
            e.wait_ge(sems["sA2"], fla)
            e.collective_compute(
                "AllGather", ALU.bypass, replica_groups=[list(range(NC))],
                ins=[h_a_own.opt()], outs=[h_a_all.opt()],
            ).then_inc(sems["sCC"], 1)
        E.inc("sCC", 1)
        cc_a = E.val("sCC")
        gWL = E.guard()
        gMM = E.guard()
        gGU = E.guard()
        gSilu = E.guard()
        gFuSt = E.guard()
        for ih in range(2):
            if E.on("sync"):
                e.wait_ge(sems["sCC"], cc_a)
                gMM.wait(e, 16 * ih - 1)
                e.dma_start(out=wi_h_sb[:, :].rearrange("p (k c) -> p k c", c=512),
                            in_=h_wgT[:, ih * 512:(ih + 1) * 512]
                            .rearrange("(k p) c -> p k c", p=128)).then_inc(sems["sQ"], 16)
                e.dma_start(out=wi_g_sb[:, :].rearrange("p (k c) -> p k c", c=512),
                            in_=h_wuT[:, ih * 512:(ih + 1) * 512]
                            .rearrange("(k p) c -> p k c", p=128)).then_inc(sems["sQ"], 16)
            E.inc("sQ", 32)
            E.flush(e, "sQ")
            gWL.mark("sQ")
            for i in range(NTC):
                owner, trow = i // 2, (i % 2) * TC
                buf = xn_mov0 if i % 2 == 0 else xn_mov1
                if E.on("sync"):
                    gMM.wait(e, 16 * ih + i - 2)
                    e.dma_start(out=buf[:, :].rearrange("p (f t) -> p f t", t=TC),
                                in_=h_a_all[owner, :, :, trow:trow + TC]
                                .transpose([1, 0, 2])).then_inc(sems["sQ"], 16)
                E.inc("sQ", 16)
                E.flush(e, "sQ")
                for m in range(4):
                    icb = ih * 4 + m
                    pbg = psA if m % 2 == 0 else psB
                    pbu = psC if m % 2 == 0 else psD
                    if E.on("tensor"):
                        if m == 0:
                            E.dma_wait(e, "sQ")
                        gSilu.wait(e, len(gSilu.marks) - 2)
                        for k in range(16):
                            last = e.matmul(pbg[:, :],
                                            wi_h_sb[:, k * 512 + m * 128: k * 512 + (m + 1) * 128],
                                            buf[:, k * TC:(k + 1) * TC],
                                            start=(k == 0), stop=(k == 15))
                        last.then_inc(sems["sT"], 1)
                    E.inc("sT", 1)
                    gmm = E.val("sT")
                    if E.on("tensor"):
                        gGU.wait(e, len(gGU.marks) - 2)
                        for k in range(16):
                            last = e.matmul(pbu[:, :],
                                            wi_g_sb[:, k * 512 + m * 128: k * 512 + (m + 1) * 128],
                                            buf[:, k * TC:(k + 1) * TC],
                                            start=(k == 0), stop=(k == 15))
                        last.then_inc(sems["sT"], 1)
                    E.inc("sT", 1)
                    umm = E.val("sT")
                    if E.on("scalar"):
                        e.wait_ge(sems["sT"], gmm)
                        gGU.wait(e, len(gGU.marks) - 1)
                        e.activation(out=b_gs[:, :], in_=pbg[:, :], func=AF.Silu,
                                     scale=mbc[:, 0:1])
                        e.drain()
                        e.sem_inc(sems["sA"], 1)
                    E.inc("sA", 1)
                    gSilu.mark("sA")
                    silu_now = E.val("sA")
                    fidx = len(gFuSt.marks)
                    tbb = b_tb if fidx % 2 == 0 else b_tb2
                    lqs = "sQ2" if fidx % 2 == 0 else "sQ3"
                    if E.on("vector"):
                        e.wait_ge(sems["sT"], umm)
                        e.wait_ge(sems["sA"], silu_now)
                        gFuSt.wait(e, fidx - 2)
                        e.tensor_scalar(out=b_us[:, :], in0=pbu[:, :],
                                        scalar1=mbc[:, 1:2], scalar2=None, op0=ALU.mult)
                        e.drain()
                        e.tensor_tensor(out=tbb[:, :], in0=b_gs[:, :], in1=b_us[:, :],
                                        op=ALU.mult)
                        e.drain()
                        e.sem_inc(sems["sV"], 1)
                    E.inc("sV", 1)
                    gGU.mark("sV")
                    fu_now = E.val("sV")
                    if E.on("sync"):
                        e.wait_ge(sems["sV"], fu_now)
                        E.lane_open(e, lqs)
                        e.dma_start(out=h_fu_stage[owner, icb, :, trow:trow + TC],
                                    in_=tbb[:, :]).then_inc(sems[lqs], 16)
                    E.inc(lqs, 16)
                    gFuSt.mark(lqs)
                gMM.mark("sT")
        if E.on("sync"):
            e.wait_ge(sems["sQ"], E.val("sQ"))
            e.wait_ge(sems["sQ2"], E.val("sQ2"))
            e.wait_ge(sems["sQ3"], E.val("sQ3"))
            e.sem_inc(sems["sA2"], 1)
        E.inc("sA2", 1)
        flf = E.val("sA2")
        if E.on("gpsimd"):
            e.wait_ge(sems["sA2"], flf)
            with allow_ici_all_to_all():
                e.collective_compute(
                    "AllToAll", ALU.bypass, replica_groups=[list(range(NC))],
                    ins=[h_fu_stage.opt()], outs=[h_fu_recv.opt()],
                ).then_inc(sems["sCC"], 1)
        E.inc("sCC", 1)
        cc_f = E.val("sCC")
        gWI = E.guard()
        gH1 = E.guard()
        gH1t = E.guard()
        gHtm = E.guard()
        gBf = E.guard()
        gA2t = E.guard()
        gA2e = E.guard()
        for tt in range(8):
            if E.on("sync"):
                e.wait_ge(sems["sCC"], cc_f)
                gA2t.wait(e, 8 * tt - 1)
                e.dma_start(out=d_win[:, :].rearrange("p (q c) -> p q c", c=128),
                            in_=h_fu_recv[:, :, :, tt * 128:(tt + 1) * 128]
                            .rearrange("s b p c -> p (s b) c")).then_inc(sems["sQ"], 16)
            E.inc("sQ", 16)
            E.flush(e, "sQ")
            gWI.mark("sQ")
            for g4 in range(16):
                pb = psA if g4 % 2 == 0 else psB
                if E.on("tensor"):
                    if g4 == 0:
                        E.dma_wait(e, "sQ")
                        gBf.wait(e, tt - 1)
                    gH1.wait(e, 16 * tt + g4 - 2)
                    for q4 in range(4):
                        pp = g4 * 4 + q4
                        last = e.matmul(pb[:, q4 * 128:(q4 + 1) * 128], h128_sb[:, :],
                                        d_win[:, pp * 128:(pp + 1) * 128],
                                        start=True, stop=True)
                    last.then_inc(sems["sT"], 1)
                E.inc("sT", 1)
                h1_now = E.val("sT")
                if E.on("vector"):
                    e.wait_ge(sems["sT"], h1_now)
                    gH1t.wait(e, 16 * tt + g4 - 1)
                    e.tensor_copy(b_tb[:, :], pb[:, :])
                    e.drain()
                    e.sem_inc(sems["sV"], 1)
                E.inc("sV", 1)
                gH1.mark("sV")
                h1e = E.val("sV")
                if E.on("tensor"):
                    e.wait_ge(sems["sV"], h1e)
                    gHtm.wait(e, 16 * tt + g4 - 2)
                    pbank = psE if g4 % 2 == 0 else psF
                    for q4 in range(4):
                        last = e.transpose(pbank[:, q4 * 128:(q4 + 1) * 128],
                                           b_tb[:, q4 * 128:(q4 + 1) * 128], ident[:, :])
                    last.then_inc(sems["sT"], 1)
                E.inc("sT", 1)
                gH1t.mark("sT")
                tp_now = E.val("sT")
                if E.on("vector"):
                    e.wait_ge(sems["sT"], tp_now)
                    pbank = psE if g4 % 2 == 0 else psF
                    e.tensor_copy(d_wtm[:, g4 * 512:(g4 + 1) * 512], pbank[:, 0:512])
                    e.drain()
                    e.sem_inc(sems["sV"], 1)
                E.inc("sV", 1)
                gHtm.mark("sV")
            if E.on("vector"):
                cur, oth = d_wtm, d_bf0
                for st in range(6):
                    s_el = 128 * (1 << st)
                    ng = 8192 // (2 * s_el)
                    a = cur[:, :].rearrange("p (g two s) -> p g two s", two=2, s=s_el)
                    o = oth[:, :].rearrange("p (g two s) -> p g two s", two=2, s=s_el)
                    e.tensor_tensor(out=o[:, :, 0, :], in0=a[:, :, 0, :],
                                    in1=a[:, :, 1, :], op=ALU.add)
                    e.tensor_tensor(out=o[:, :, 1, :], in0=a[:, :, 0, :],
                                    in1=a[:, :, 1, :], op=ALU.subtract)
                    e.drain()
                    cur, oth = oth, cur
                e.tensor_reduce(out=d_amax[:, :], in_=cur[:, :], axis=AX.X, op=ALU.max,
                                apply_absolute_value=True)
                e.drain()
                e.tensor_scalar(out=d_amax[:, :], in0=d_amax[:, :],
                                scalar1=float(1.0 / np.sqrt(np.float32(DFF))),
                                scalar2=1e-5, op0=ALU.mult, op1=ALU.max)
                e.drain()
                e.reciprocal(out=d_scal[:, :], in_=d_amax[:, :])
                e.drain()
                e.tensor_scalar(out=d_scal[:, :], in0=d_scal[:, :],
                                scalar1=float(128.0 / np.sqrt(np.float32(DFF))),
                                scalar2=None, op0=ALU.mult)
                e.tensor_scalar(out=d_inv2[:, :], in0=d_amax[:, :], scalar1=1.0 / 128.0,
                                scalar2=None, op0=ALU.mult)
                e.drain()
                e.tensor_scalar(out=d_qf[:, :], in0=cur[:, :], scalar1=d_scal[:, 0:1],
                                scalar2=None, op0=ALU.mult)
                e.drain()
                e.tensor_scalar(out=d_qf[:, :], in0=d_qf[:, :], scalar1=MAGIC,
                                scalar2=-MAGIC, op0=ALU.add, op1=ALU.add)
                e.drain()
                e.tensor_scalar(out=d_qf[:, :], in0=d_qf[:, :], scalar1=127.0,
                                scalar2=-128.0, op0=ALU.min, op1=ALU.max)
                e.drain()
                e.tensor_scalar(out=d_a2b[:, :], in0=d_qf[:, :], scalar1=d_inv2[:, 0:1],
                                scalar2=None, op0=ALU.mult)
                e.drain()
                e.sem_inc(sems["sV"], 1)
            E.inc("sV", 1)
            gBf.mark("sV")
            bf_now = E.val("sV")
            for g8 in range(8):
                pbank = psE if g8 % 2 == 0 else psF
                if E.on("tensor"):
                    if g8 == 0:
                        e.wait_ge(sems["sV"], bf_now)
                    gA2e.wait(e, len(gA2e.marks) - 2)
                    for q8 in range(8):
                        pp = g8 * 8 + q8
                        last = e.transpose(pbank[:, q8 * 128:(q8 + 1) * 128],
                                           d_a2b[:, pp * 128:(pp + 1) * 128], ident[:, :])
                    last.then_inc(sems["sT"], 1)
                E.inc("sT", 1)
                gA2t.mark("sT")
                tq_now = E.val("sT")
                if E.on("vector"):
                    e.wait_ge(sems["sT"], tq_now)
                    if g8 == 0:
                        E.dma_wait(e, "sQ")
                    e.tensor_copy(d_a2t[:, g8 * 1024:(g8 + 1) * 1024], pbank[:, :])
                    e.drain()
                    e.sem_inc(sems["sV"], 1)
                E.inc("sV", 1)
                gA2e.mark("sV")
            a2_now = E.val("sV")
            if E.on("sync"):
                e.wait_ge(sems["sV"], a2_now)
                e.dma_start(out=h_a2[:, :, tt * 128:(tt + 1) * 128].transpose([1, 0, 2]),
                            in_=d_a2t[:, :].rearrange("p (q c) -> p q c", c=128)
                            ).then_inc(sems["sQ"], 16)
            E.inc("sQ", 16)
            E.flush(e, "sQ")
        a2_all_st = E.val("sQ")
        gDL = E.guard()
        gDmm = E.guard()
        gDacc = E.guard()
        gZ2 = E.guard()
        gTpO = E.guard()
        gRes = E.guard()
        oi = 0
        for h2 in range(2):
            if E.on("vector"):
                gRes.wait(e, 16 * h2 - 1)
                e.memset(b_acc[:, :], 0.0)
                e.drain()
                e.sem_inc(sems["sV"], 1)
            E.inc("sV", 1)
            gZ2.mark("sV")
            for icg in range(16):
                wd = d_wd0 if icg % 2 == 0 else d_wd1
                d_ast = d_ast0 if icg % 2 == 0 else d_ast1
                dli = 16 * h2 + icg
                lqs = "sQ2" if dli % 2 == 0 else "sQ3"
                if E.on("sync"):
                    if h2 == 0 and icg == 0:
                        e.wait_ge(sems["sQ"], a2_all_st)
                    gDmm.wait(e, (dli - 2) * 16 + 15)
                    E.lane_open(e, lqs)
                    e.dma_start(out=d_ast[:, :].rearrange("p (ic c) -> p ic c", c=512),
                                in_=h_a2[icg * 4:(icg + 1) * 4, :, h2 * 512:(h2 + 1) * 512]
                                .transpose([1, 0, 2])).then_inc(sems[lqs], 16)
                    e.dma_start(out=wd[:, :].rearrange("p (ic c) -> p ic c", c=D),
                                in_=h_wdT.rearrange("s q c -> (s q) c")[icg * 512:(icg + 1) * 512, :]
                                .rearrange("(ic p) c -> p ic c", p=128)).then_inc(sems[lqs], 16)
                E.inc(lqs, 32)
                gDL.mark(lqs)
                for oc in range(16):
                    pb = psG if oc % 2 == 0 else psH
                    if E.on("tensor"):
                        if oc == 0:
                            gDL.wait(e, dli)
                            if icg == 0:
                                gZ2.wait(e, h2)
                        gDacc.wait(e, len(gDacc.marks) - 2)
                        for ic in range(4):
                            last = e.matmul(pb[:, :],
                                            wd[:, ic * D + oc * 128: ic * D + (oc + 1) * 128],
                                            d_ast[:, ic * 512:(ic + 1) * 512],
                                            start=(ic == 0), stop=(ic == 3))
                        last.then_inc(sems["sT"], 1)
                    E.inc("sT", 1)
                    gDmm.mark("sT")
                    dmm = E.val("sT")
                    if E.on("vector"):
                        e.wait_ge(sems["sT"], dmm)
                        if icg == 0 and oc == 0:
                            gZ2.wait(e, h2)
                        dst = b_acc[:, oc * 512:(oc + 1) * 512]
                        e.tensor_tensor(out=dst, in0=dst, in1=pb[:, :], op=ALU.add)
                        e.drain()
                        e.sem_inc(sems["sV"], 1)
                    E.inc("sV", 1)
                    gDacc.mark("sV")
            for oc in range(16):
                if E.on("vector"):
                    gTpO.wait(e, oi - 1)
                    e.tensor_scalar(out=b_tb[:, 0:512], in0=b_acc[:, oc * 512:(oc + 1) * 512],
                                    scalar1=mbc[:, 2:3], scalar2=None, op0=ALU.mult)
                    e.drain()
                    e.sem_inc(sems["sV"], 1)
                E.inc("sV", 1)
                prep_now = E.val("sV")
                if E.on("tensor"):
                    e.wait_ge(sems["sV"], prep_now)
                    gRes.wait(e, oi - 1)
                    for tc4 in range(4):
                        last = e.transpose(psE[:, tc4 * 128:(tc4 + 1) * 128],
                                           b_tb[:, tc4 * 128:(tc4 + 1) * 128], ident[:, :])
                    last.then_inc(sems["sT"], 1)
                E.inc("sT", 1)
                gTpO.mark("sT")
                tp_now = E.val("sT")
                if E.on("vector"):
                    e.wait_ge(sems["sT"], tp_now)
                    for tc4 in range(4):
                        tt_g = h2 * 4 + tc4
                        dst = res[:, tt_g * D + oc * 128: tt_g * D + (oc + 1) * 128]
                        e.tensor_tensor(out=dst, in0=dst,
                                        in1=psE[:, tc4 * 128:(tc4 + 1) * 128], op=ALU.add)
                    e.drain()
                    e.sem_inc(sems["sV"], 1)
                E.inc("sV", 1)
                gRes.mark("sV")
                oi += 1
        r_now = E.val("sV")
        for nm in ("sync", "gpsimd", "scalar", "tensor"):
            if E.on(nm):
                e.wait_ge(sems["sV"], r_now)

    def ph_finish(E):
        e = E.eng
        v_now = E.val("sV")
        dbg_srcs = {"d_xnT": h_xnT_all, "d_hsT": h_hsT, "d_sgT": h_sgT,
                    "d_ssmp": h_ssmp_recv, "d_nrm": h_nrm_all, "d_yg": h_yg_recv}
        if "d_op" in dbg_t:
            if E.on("sync"):
                e.wait_ge(sems["sV"], v_now)
                e.dma_start(out=dbg_t["d_op"].opt(), in_=s_sc[:, :].opt()).then_inc(sems["sQ"], 16)
            E.inc("sQ", 16)
            E.flush(e, "sQ")
        for nm, hsrc in dbg_srcs.items():
            if nm in dbg_t:
                if E.on("sync"):
                    e.dma_start(out=dbg_t[nm].opt(), in_=hsrc.opt()).then_inc(sems["sQ"], 16)
                E.inc("sQ", 16)
                E.flush(e, "sQ")
        if E.on("sync"):
            if "d_x1" in dbg_t:
                e.dma_start(out=dbg_t["d_x1"].rearrange("(t p) d -> p t d", p=128),
                            in_=res[:, :].rearrange("p (t d) -> p t d", d=D)).then_inc(sems["sQ"], 16)
                E.inc("sQ", 16)
                E.flush(e, "sQ")
            e.dma_start(out=out_own.rearrange("(t p) d -> p t d", p=128),
                        in_=res[:, :].rearrange("p (t d) -> p t d", d=D)).then_inc(sems["sQ"], 16)
            E.inc("sQ", 16)
            E.flush(e, "sQ")
            e.wait_ge(sems["sQ"], E.val("sQ"))
        else:
            if "d_x1" in dbg_t:
                E.inc("sQ", 16)
                E.flush(e, "sQ")
            E.inc("sQ", 16)
            E.flush(e, "sQ")

    def full_script(E):
        ph_init(E)
        ph_a0(E)
        ph_w8(E)
        ph_wth(E)
        g1 = ph_a_pass1(E)
        ph_a_reduce(E, g1)
        g2 = ph_a_pass2(E)
        ph_a_out(E, g2)
        ph_b(E)
        ph_c(E)
        ph_d(E)
        ph_finish(E)

    with nc.Block() as block:
        @block.sync
        def _(eng):
            full_script(Script("sync", eng, sems))

        @block.gpsimd
        def _(eng):
            full_script(Script("gpsimd", eng, sems))

        @block.vector
        def _(eng):
            full_script(Script("vector", eng, sems))

        @block.scalar
        def _(eng):
            full_script(Script("scalar", eng, sems))

        @block.tensor
        def _(eng):
            full_script(Script("tensor", eng, sems))

    return nc, dbg_t



def make_in_maps(inputs):
    g = {k: np.ascontiguousarray(np.asarray(v)) for k, v in inputs.items()}
    xf = g["L0_attn_x"].reshape(T, D)
    maps = []
    for c in range(NC):
        ds = slice(c * DSL, (c + 1) * DSL)
        m = {
            "x_own": xf[c * TOK:(c + 1) * TOK],
            "wi_h": g["L0_attn_in_proj_w"][:, ds],
            "wi_g": g["L0_attn_in_proj_w"][:, INTER + c * DSL: INTER + (c + 1) * DSL],
            "conv_w_pp": g["L0_attn_conv_w"][ds, 0, :].reshape(4, 128, KCONV)
                .transpose(1, 0, 2).reshape(128, 4 * KCONV),
            "conv_b_pp": g["L0_attn_conv_b"][ds].reshape(4, 128).T,
            "xw_slice": g["L0_attn_x_proj_w"][ds, :],
            "wdt_slice": g["L0_attn_dt_proj_w"][:, ds],
            "dtb_pp": g["L0_attn_dt_proj_b"][ds].reshape(4, 128).T,
            "alog_pp": g["L0_attn_A_log"][ds, :].reshape(4, 128, NSTATE)
                .transpose(1, 0, 2).reshape(128, 4 * NSTATE),
            "dpar_pp": g["L0_attn_D_param"][ds].reshape(4, 128).T,
            "wout": g["L0_attn_out_proj_w"],
            "w8g_slice": g["L0_ffn_gate_w"][c * 256:(c + 1) * 256, :],
            "w8u_slice": g["L0_ffn_up_w"][c * 256:(c + 1) * 256, :],
            "w8d_slice": g["L0_ffn_down_w"][c * 768:(c + 1) * 768, :],
            "gs_pp": g["L0_ffn_gate_scale"].reshape(48, 128).T,
            "us_pp": g["L0_ffn_up_scale"].reshape(48, 128).T,
            "ds_pp": g["L0_ffn_down_scale"].reshape(16, 128).T,
            "wq_s": g["L1_attn_Wq"][:, c * 256:(c + 1) * 256],
            "wk_s": g["L1_attn_Wk"][:, c * 256:(c + 1) * 256],
            "wv_s": g["L1_attn_Wv"][:, c * 256:(c + 1) * 256],
            "bq_pp": g["L1_attn_bq"][c * 256:(c + 1) * 256].reshape(2, 128).T,
            "bk_pp": g["L1_attn_bk"][c * 256:(c + 1) * 256].reshape(2, 128).T,
            "bv_row": g["L1_attn_bv"][c * 256:(c + 1) * 256].reshape(1, 256),
            "wg_slice": g["L1_ffn_w_gate"][c * ISL:(c + 1) * ISL, :],
            "wu_slice": g["L1_ffn_w_up"][c * ISL:(c + 1) * ISL, :],
            "wd_slice": g["L1_ffn_w_down"][:, c * ISL:(c + 1) * ISL],
        }
        maps.append({k: np.ascontiguousarray(v) for k, v in m.items()})
    return maps


_NC_CACHE = {}


def kernel(**inputs):
    key = tuple(sorted(DEBUG))
    if key not in _NC_CACHE:
        _NC_CACHE[key] = build_program(debug=key)
    nc, dbg_t = _NC_CACHE[key]
    maps = make_in_maps(inputs)
    r = run_bass_kernel_spmd(nc, maps, core_ids=list(range(NC)))
    out = np.concatenate([np.asarray(r.results[c]["out_own"]) for c in range(NC)], 0)
    kernel.last_results = r.results
    return out.reshape(2, 4096, 2048)


# version 13
# speedup vs baseline: 1.0100x; 1.0100x over previous
import math
import numpy as np
import ml_dtypes
import concourse.bass as bass
import concourse.mybir as mybir
from concourse.bass_utils import run_bass_kernel_spmd
from concourse.aot_env import allow_ici_all_to_all

F32 = mybir.dt.float32
BF16 = mybir.dt.bfloat16
AF = mybir.ActivationFunctionType
ALU = mybir.AluOpType
AX = mybir.AxisListType

NC = 8
D = 2048
T = 8192
TOK = 1024
INTER, NSTATE, DTRANK, KCONV = 4096, 16, 128, 4
DSL = 512
FF1 = 6144
DFF = 8192
NH, HD = 16, 128
ISL = 1024
TC = 512
NTC = 16
MAGIC = float(3 << 22)
INV_LN2 = float(1.0 / math.log(2.0))
LN2 = float(math.log(2.0))
ATT_SCALE = float(1.0 / np.sqrt(np.float32(HD)))
NEG_BIG = -1e9
LN_EPS = 1e-5
RMS_EPS = 1e-6

DEBUG = []


def _bf(x):
    return np.asarray(x, dtype=ml_dtypes.bfloat16)


SEM_NAMES = ["sQ", "sG", "sCC", "sT", "sV", "sA", "sV2", "sA2", "sFL", "sFG", "sP", "sQ2", "sQ3"]
LANES = ["sQ", "sQ2", "sQ3"]


class Guard:

    def __init__(self, E):
        self.E = E
        self.marks = []

    def mark(self, sem):
        self.marks.append((sem, self.E.val(sem)))

    def wait(self, e, k):
        if 0 <= k < len(self.marks):
            s, v = self.marks[k]
            e.wait_ge(self.E.sems[s], v)
        else:
            e.wait_ge(self.E.sems["sQ"], 0)

    def wait_last(self, e):
        self.wait(e, len(self.marks) - 1)


class Script:
    def __init__(self, active, eng, sems):
        self.active = active
        self.eng = eng
        self.sems = sems
        self.cnt = dict.fromkeys(SEM_NAMES, 0)

    def on(self, name):
        return self.active == name

    def inc(self, name, n=1):
        self.cnt[name] += n
        return self.cnt[name]

    def val(self, name):
        return self.cnt[name]

    def guard(self):
        return Guard(self)

    def dma_wait(self, e, sem="sQ"):
        if sem == "sQ":
            if self.active == "sync":
                e.wait_ge(self.sems["sQ"], self.cnt["sQ"])
            else:
                e.wait_ge(self.sems["sFL"], self.cnt["sFL"])
        else:
            if self.active == "gpsimd":
                e.wait_ge(self.sems["sG"], self.cnt["sG"])
            else:
                e.wait_ge(self.sems["sFG"], self.cnt["sFG"])

    def flush(self, e, sem="sQ", proxy=None):
        owner = "sync" if sem == "sQ" else "gpsimd"
        fl = "sFL" if sem == "sQ" else "sFG"
        who = proxy or owner
        if self.active == who:
            self.eng.wait_ge(self.sems[sem], self.cnt[sem])
            self.eng.sem_inc(self.sems[fl], 1)
        self.cnt[fl] += 1

    def lane_open(self, e, qs):
        if self.active == "sync":
            self.eng.wait_ge(self.sems[qs], self.cnt[qs])


def build_program(debug=()):
    nc = bass.Bass("TRN2", target_bir_lowering=False, dynamic_dma_scratch_size=16384)
    sems = {s: nc.semaphore(s).__enter__() for s in SEM_NAMES}
    inp = lambda name, shape, dtype=F32: nc.dram_tensor(name, list(shape), dtype, kind="ExternalInput").ap()
    dram = lambda name, shape, dtype=BF16: nc.dram_tensor(name, list(shape), dtype, kind="Internal").ap()
    sbuf = lambda name, shape, dtype=F32: nc.alloc_sbuf_tensor(name, list(shape), dtype).ap()
    psum = lambda name, shape, dtype=F32: nc.alloc_psum_tensor(name, list(shape), dtype).ap()

    x_own = inp("x_own", [TOK, D])
    wi_h = inp("wi_h", [D, DSL])
    wi_g = inp("wi_g", [D, DSL])
    conv_w_pp = inp("conv_w_pp", [128, 4 * KCONV])
    conv_b_pp = inp("conv_b_pp", [128, 4])
    xw_slice = inp("xw_slice", [DSL, 160])
    wdt_slice = inp("wdt_slice", [DTRANK, DSL])
    dtb_pp = inp("dtb_pp", [128, 4])
    alog_pp = inp("alog_pp", [128, 4 * NSTATE])
    dpar_pp = inp("dpar_pp", [128, 4])
    wout = inp("wout", [INTER, D])
    w8g_slice = inp("w8g_slice", [D // NC, FF1])
    w8u_slice = inp("w8u_slice", [D // NC, FF1])
    w8d_slice = inp("w8d_slice", [FF1 // NC, D])
    gs_pp = inp("gs_pp", [128, FF1 // 128])
    us_pp = inp("us_pp", [128, FF1 // 128])
    ds_pp = inp("ds_pp", [128, D // 128])
    wq_s = inp("wq_s", [D, 2 * HD])
    wk_s = inp("wk_s", [D, 2 * HD])
    wv_s = inp("wv_s", [D, 2 * HD])
    bq_pp = inp("bq_pp", [128, 2])
    bk_pp = inp("bk_pp", [128, 2])
    bv_row = inp("bv_row", [1, 2 * HD])
    wg_slice = inp("wg_slice", [ISL, D])
    wu_slice = inp("wu_slice", [ISL, D])
    wd_slice = inp("wd_slice", [D, ISL])
    out_own = nc.dram_tensor("out_own", [TOK, D], F32, kind="ExternalOutput").ap()

    c_ident = nc.inline_tensor(np.eye(128, dtype=ml_dtypes.bfloat16), "c_ident").ap()
    en_np = np.zeros((16, 16 * 128), dtype=ml_dtypes.bfloat16)
    for n in range(16):
        en_np[n, n * 128:(n + 1) * 128] = 1
    c_en = nc.inline_tensor(en_np, "c_en").ap()
    ii, jj = np.meshgrid(np.arange(128), np.arange(128), indexing="ij")
    pc = np.vectorize(lambda a: bin(a).count("1"))(ii & jj) % 2
    c_h128 = nc.inline_tensor(_bf(1.0 - 2.0 * pc), "c_h128").ap()
    mtile = np.where(np.tril(np.ones((128, 128))) > 0, 0.0, NEG_BIG).astype(np.float32)
    c_mask = nc.inline_tensor(mtile, "c_mask").ap()
    c_ones = nc.inline_tensor(np.ones((1, 128), dtype=ml_dtypes.bfloat16), "c_ones").ap()
    c_identf = nc.inline_tensor(np.eye(128, dtype=np.float32), "c_identf").ap()
    c_onesf = nc.inline_tensor(np.ones((1, 128), dtype=np.float32), "c_onesf").ap()
    kvals = [LN_EPS, RMS_EPS, ATT_SCALE, LN2, -LN2, -3.0 * LN2, 3.0 * LN2, 1e-30]
    c_kons = nc.inline_tensor(
        np.tile(np.asarray(kvals, np.float32)[None, :], (128, 1)), "c_kons").ap()

    h_xnT_own = dram("h_xnT_own", [16, 128, TOK])
    h_xnT_all = dram("h_xnT_all", [NC, 16, 128, TOK])
    h_hsT = dram("h_hsT", [4, 128, T])
    h_sgT = dram("h_sgT", [4, 128, T])
    h_ssmp_part = dram("h_ssmp_part", [NC, TOK, 160], F32)
    h_ssmp_recv = dram("h_ssmp_recv", [NC, TOK, 160], F32)
    h_nrm_own = dram("h_nrm_own", [TOK, 160])
    h_nrm_all = dram("h_nrm_all", [NC, TOK, 160])
    h_yg_stage = dram("h_yg_stage", [NC, 4, 128, TOK])
    h_yg_recv = dram("h_yg_recv", [NC, 4, 128, TOK])
    h_w8g_own = dram("h_w8g_own", [D // NC, FF1])
    h_w8u_own = dram("h_w8u_own", [D // NC, FF1])
    h_w8d_own = dram("h_w8d_own", [FF1 // NC, D])
    h_w8g = dram("h_w8g", [NC, D // NC, FF1])
    h_w8u = dram("h_w8u", [NC, D // NC, FF1])
    h_w8d = dram("h_w8d", [NC, FF1 // NC, D])
    h_xn3_own = dram("h_xn3_own", [16, 128, TOK])
    h_xn3_all = dram("h_xn3_all", [NC, 16, 128, TOK])
    h_qkT = dram("h_qkT", [2, 2, 128, T])
    h_v = dram("h_v", [2, T, HD])
    h_o_stage = dram("h_o_stage", [NC, 2, TOK, HD])
    h_o_recv = dram("h_o_recv", [NC, 2, TOK, HD])
    h_th_own = dram("h_th_own", [3, 1], F32)
    h_th_all = dram("h_th_all", [3, 1], F32)
    h_wgT = dram("h_wgT", [D, ISL])
    h_wuT = dram("h_wuT", [D, ISL])
    h_wdT_own = dram("h_wdT_own", [ISL, D])
    h_wdT = dram("h_wdT", [NC, ISL, D])
    h_a_own = dram("h_a_own", [16, 128, TOK])
    h_a_all = dram("h_a_all", [NC, 16, 128, TOK])
    h_fu_stage = dram("h_fu_stage", [NC, 8, 128, TOK])
    h_fu_recv = dram("h_fu_recv", [NC, 8, 128, TOK])
    h_a2 = dram("h_a2", [64, 128, TOK])

    res = sbuf("res", [128, 8 * D])
    arA = sbuf("arA", [128, 14336], F32)
    arB = sbuf("arB", [128, 8192], F32)
    arC = sbuf("arC", [128, 6272], BF16)
    ident = sbuf("ident", [128, 128], BF16)
    en_sb = sbuf("en_sb", [16, 16 * 128], BF16)
    h128_sb = sbuf("h128_sb", [128, 128], BF16)
    mask_sb = sbuf("mask_sb", [128, 128], F32)
    ones_sb = sbuf("ones_sb", [1, 128], BF16)
    convw_sb = sbuf("convw_sb", [128, 16])
    convb_sb = sbuf("convb_sb", [128, 4])
    dtb_sb = sbuf("dtb_sb", [128, 4])
    aneg_sb = sbuf("aneg_sb", [128, 4 * NSTATE])
    dpar_sb = sbuf("dpar_sb", [128, 4])
    stat_sb = sbuf("stat_sb", [128, 64])
    xw_sb = sbuf("xw_sb", [128, 4 * 160], BF16)
    wdt_sb = sbuf("wdt_sb", [128, 512], BF16)
    wi_h_sb = sbuf("wi_h_sb", [128, 16 * 512], BF16)
    wi_g_sb = sbuf("wi_g_sb", [128, 16 * 512], BF16)
    sstate = sbuf("sstate", [128, 4 * NSTATE], F32)
    hist = sbuf("hist", [128, 12])
    gsus_sb = sbuf("gsus_sb", [128, 112])
    bv_sb = sbuf("bv_sb", [1, 256], BF16)
    bqk_sb = sbuf("bqk_sb", [128, 4])
    identf = sbuf("identf", [128, 128], F32)
    onesf = sbuf("onesf", [1, 128], F32)
    th_row = sbuf("th_row", [1, 3], F32)
    kons = sbuf("kons", [128, 8], F32)
    K_LNEPS = kons[:, 0:1]
    K_RMSEPS = kons[:, 1:2]
    K_ATTS = kons[:, 2:3]
    K_LN2 = kons[:, 3:4]
    K_NLN2 = kons[:, 4:5]
    K_N3LN2 = kons[:, 5:6]
    K_P3LN2 = kons[:, 6:7]
    K_TINY = kons[:, 7:8]

    psA = psum("psA", [128, 512], F32)
    psB = psum("psB", [128, 512], F32)
    psC = psum("psC", [128, 512], F32)
    psD = psum("psD", [128, 512], F32)
    psE = psum("psE", [128, 1024], BF16)
    psF = psum("psF", [128, 1024], BF16)
    psG = psum("psG", [128, 512], F32)
    psH = psum("psH", [128, 512], F32)

    dbg_t = {}
    _dbg_specs = {"d_x1": ([TOK, D], F32), "d_x2": ([TOK, D], F32), "d_x3": ([TOK, D], F32),
                  "d_xnT": ([NC, 16, 128, TOK], BF16), "d_hsT": ([4, 128, T], BF16),
                  "d_sgT": ([4, 128, T], BF16), "d_ssmp": ([NC, TOK, 160], F32),
                  "d_nrm": ([NC, TOK, 160], BF16), "d_yg": ([NC, 4, 128, TOK], BF16),
                  "d_op": ([128, 4 * 512], F32), "d_ygf": ([4, 128, 8192], BF16)}
    for name in debug:
        shape, dt = _dbg_specs[name]
        dbg_t[name] = nc.dram_tensor(name, shape, dt, kind="ExternalOutput").ap()

    def fv(ar, off, n):
        return ar[:, off:off + n]

    def bv(ar, off, n):
        return ar[:, off:off + (n + 1) // 2].bitcast(BF16)

    xn_mov0 = bv(arA, 0, 16 * TC)
    xn_mov1 = bv(arA, 4096, 16 * TC)
    xnT_sb = bv(arA, 0, 16 * TOK)
    dtT_ch = fv(arA, 8192, 4 * TC)
    s_sc1 = bv(arA, 0, 4 * TC)
    csb0 = bv(arA, 2048, TC)
    csb1 = bv(arA, 2304, TC)
    du_bf = bv(arA, 12288, 4 * TC)
    du_ch = fv(arA, 10240, 4 * TC)
    hconv = fv(arA, 12276, 4 * (TC + 3))
    yacc = fv(arB, 0, 4 * TC)
    dA0_sc = bv(arB, 2048, 4 * TC)
    dA1_sc = bv(arB, 2048 + 1024, 4 * TC)
    dBu_sc = bv(arB, 4096, 4 * TC)
    sp_sc = fv(arB, 5120, 4 * TC)
    s_sc = bv(arB, 7168, 4 * TC)
    convo = fv(arB, 4096, 4 * TC)
    bsb0 = bv(arA, 3072, TC)
    bsb1 = bv(arA, 3328, TC)

    ssum_sc = fv(arA, 8192, 160)
    red0_sc = fv(arA, 8352, 8 * 160)
    red1_sc = fv(arB, 0, 8 * 160)
    ssmp_ev = fv(arB, 2048, 4 * 160)
    sq_junk = fv(arB, 4096, 128)

    hs_ch = arC[:, 0:2048]
    sg_ch = arC[:, 2048:4096]
    ln_xn0 = arC[:, 0:2048]
    ln_xn1 = arC[:, 2048:4096]
    nrm_tm = arC[:, 4096:4736]
    dtnT = arC[:, 4736:5248]
    bnT = arC[0:16, 5248:5760]
    cnT = arC[0:16, 5760:6272]
    yg_ev = arC[:, 4096:6144]
    ln_sq = fv(arA, 8192, 2048)
    ygf_sb = bv(arA, 0, 8192)
    wout_kg = bv(arA, 4096, 16384)
    q_ws = fv(arA, 0, 2048)
    q_t1 = fv(arA, 2048, 2048)
    q_q = fv(arA, 4096, 2048)
    q_qi = fv(arA, 6144, 2048)
    q_bf = bv(arA, 8192, 2048)
    q_ws2 = fv(arB, 0, 2048)
    q_t12 = fv(arB, 2048, 2048)
    q_q2 = fv(arB, 4096, 2048)
    q_qi2 = fv(arB, 6144, 2048)
    q_bf2 = bv(arA, 9216, 2048)
    b_wdst = bv(arA, 8192, 8192)
    b_acc = arB[:, 0:8192]
    b_fuse = arC[:, 0:4096]
    b_gs = arC[:, 4096:4608]
    b_us = arC[:, 4608:5120]
    b_tb = arC[:, 5120:5632]
    b_tb2 = arC[:, 5632:6144]
    b_wgst = wi_h_sb
    b_wust = wi_g_sb
    c_wv = bv(arA, 8192, 4096)
    c_qt = bv(arA, 0, 4096)
    c_kt = bv(arA, 2048, 4096)
    c_v = bv(arA, 4096, 4096)
    c_p = bv(arA, 6144, 4096)
    c_p1 = bv(arA, 10240, 4096)
    c_qt1 = bv(arA, 8192, 4096)
    c_kt1 = bv(arA, 12288, 4096)
    c_srow = arB[:, 0:4096]
    c_pt = bv(arB, 4096, 4096)
    c_osb = arC[:, 0:1024]
    c_vev = arC[:, 1024:1536]
    c_qkev = arC[:, 1536:2048]
    c_vev2 = arC[:, 4608:5120]
    c_qkev2 = arC[:, 4096:4608]
    c_att = arC[:, 2048:4096]
    c_att1 = arC[:, 0:2048]
    t_ws = fv(arA, 0, 2048)
    t_tern = bv(arA, 2048, 2048)
    t_sc = bv(arA, 3072, 2048)
    t_le = fv(arA, 4096, 2048)
    t_tp = bv(arA, 6144, 2048)
    t_wsB = fv(arB, 0, 2048)
    t_ternB = bv(arB, 2048, 2048)
    t_tpB = bv(arB, 3072, 2048)
    racc = stat_sb[:, 52:55]
    thbc = stat_sb[:, 55:58]
    mbc = stat_sb[:, 58:61]
    d_amax = stat_sb[:, 61:62]
    d_scal = stat_sb[:, 62:63]
    d_inv2 = stat_sb[:, 63:64]
    d_win = bv(arA, 0, 8192)
    d_wtm = bv(arA, 4096, 8192)
    d_a2t = bv(arA, 8192, 8192)
    d_ast0 = arC[:, 1024:3072]
    d_ast1 = arC[:, 3072:5120]
    d_bf0 = bv(arB, 0, 8192)
    d_qf = fv(arB, 0, 8192)
    d_a2b = bv(arA, 0, 8192)
    d_w1s = arC[:, 0:128]
    d_wd0 = wi_h_sb
    d_wd1 = wi_g_sb

    ln_s1 = stat_sb[:, 0:8]
    ln_s2 = stat_sb[:, 8:16]
    ln_mean = stat_sb[:, 16:24]
    ln_rstd = stat_sb[:, 24:32]
    ln_tmp = stat_sb[:, 32:40]
    rms_s = stat_sb[:, 40:44]
    rms_r = stat_sb[:, 44:48]

    def ph_init(E):
        e = E.eng
        if E.on("sync"):
            for dst, src in [(ident, c_ident), (en_sb, c_en), (h128_sb, c_h128),
                             (mask_sb, c_mask), (ones_sb, c_ones), (convw_sb, conv_w_pp),
                             (convb_sb, conv_b_pp), (dtb_sb, dtb_pp), (aneg_sb, alog_pp),
                             (dpar_sb, dpar_pp), (kons, c_kons),
                             (gsus_sb, None)]:
                if dst is gsus_sb:
                    e.dma_start(out=gsus_sb[:, 0:48], in_=gs_pp[:, :]).then_inc(sems["sQ"], 16)
                    e.dma_start(out=gsus_sb[:, 48:96], in_=us_pp[:, :]).then_inc(sems["sQ"], 16)
                    e.dma_start(out=gsus_sb[:, 96:112], in_=ds_pp[:, :]).then_inc(sems["sQ"], 16)
                    continue
                e.dma_start(out=dst[:, :], in_=src[:, :]).then_inc(sems["sQ"], 16)
            e.dma_start(
                out=res[:, :].rearrange("p (t d) -> p t d", d=D),
                in_=x_own.rearrange("(t p) d -> p t d", p=128),
            ).then_inc(sems["sQ"], 16)
        E.inc("sQ", 16 * 15)
        E.flush(e, "sQ")
        if E.on("gpsimd"):
            e.dma_start(
                out=wi_h_sb[:, :].rearrange("p (k c) -> p k c", c=512),
                in_=wi_h.rearrange("(k p) c -> p k c", p=128),
            ).then_inc(sems["sG"], 16)
            e.dma_start(
                out=wi_g_sb[:, :].rearrange("p (k c) -> p k c", c=512),
                in_=wi_g.rearrange("(k p) c -> p k c", p=128),
            ).then_inc(sems["sG"], 16)
            e.dma_start(
                out=xw_sb[:, :].rearrange("p (k c) -> p k c", c=160),
                in_=xw_slice.rearrange("(k p) c -> p k c", p=128),
            ).then_inc(sems["sG"], 16)
            e.dma_start(out=wdt_sb[:, :], in_=wdt_slice[:, :]).then_inc(sems["sG"], 16)
        E.inc("sG", 16 * 4)
        E.flush(e, "sG")
        q_all, g_all = E.val("sQ"), E.val("sG")
        if E.on("scalar"):
            e.wait_ge(sems["sQ"], q_all)
            e.activation(out=aneg_sb[:, :], in_=aneg_sb[:, :], func=AF.Exp).then_inc(sems["sA"], 1)
        E.inc("sA", 1)
        a_now = E.val("sA")
        if E.on("vector"):
            e.wait_ge(sems["sA"], a_now)
            e.tensor_scalar(out=aneg_sb[:, :], in0=aneg_sb[:, :], scalar1=-1.0,
                            scalar2=None, op0=ALU.mult)
            e.memset(sstate[:, :], 0.0)
            e.drain()
            e.sem_inc(sems["sV"], 1)
        E.inc("sV", 1)
        v_now = E.val("sV")
        for nm in ("sync", "gpsimd", "scalar", "tensor"):
            if E.on(nm):
                e.wait_ge(sems["sQ"], q_all)
                e.wait_ge(sems["sG"], g_all)
                e.wait_ge(sems["sV"], v_now)

    def emit_ln_transpose(E, dst_sb):
        e = E.eng
        if E.on("scalar"):
            e.wait_ge(sems["sV"], E.val("sV"))
            E.dma_wait(e, "sQ")
            E.dma_wait(e, "sG")
            for tt in range(8):
                if tt:
                    e.drain()
                e.activation(out=ln_sq[:, :], in_=res[:, tt * D:(tt + 1) * D],
                             func=AF.Square, accum_out=ln_s2[:, tt:tt + 1])
            e.drain()
            e.sem_inc(sems["sA"], 1)
        E.inc("sA", 1)
        sq_done = E.val("sA")
        if E.on("vector"):
            for tt in range(8):
                e.tensor_reduce(out=ln_s1[:, tt:tt + 1], in_=res[:, tt * D:(tt + 1) * D],
                                axis=AX.X, op=ALU.add)
            e.drain()
            e.wait_ge(sems["sA"], sq_done)
            e.tensor_scalar(out=ln_mean[:, :], in0=ln_s1[:, :], scalar1=1.0 / D,
                            scalar2=None, op0=ALU.mult)
            e.tensor_scalar(out=ln_tmp[:, :], in0=ln_s2[:, :], scalar1=1.0 / D,
                            scalar2=None, op0=ALU.mult)
            e.drain()
            e.tensor_tensor(out=ln_s2[:, :], in0=ln_mean[:, :], in1=ln_mean[:, :], op=ALU.mult)
            e.drain()
            e.tensor_tensor(out=ln_tmp[:, :], in0=ln_tmp[:, :], in1=ln_s2[:, :], op=ALU.subtract)
            e.drain()
            e.sem_inc(sems["sV"], 1)
        E.inc("sV", 1)
        var_done = E.val("sV")
        if E.on("scalar"):
            e.wait_ge(sems["sV"], var_done)
            e.activation(out=ln_tmp[:, :], in_=ln_tmp[:, :], func=AF.Sqrt,
                         bias=K_LNEPS).then_inc(sems["sA"], 1)
        E.inc("sA", 1)
        std_done = E.val("sA")
        first_v = True
        gN = E.guard()
        gT = E.guard()
        gEv = E.guard()
        for tt in range(8):
            lx = ln_xn0 if tt % 2 == 0 else ln_xn1
            if E.on("vector"):
                if first_v:
                    e.wait_ge(sems["sA"], std_done)
                    e.reciprocal(out=ln_rstd[:, :], in_=ln_tmp[:, :])
                    e.drain()
                    first_v = False
                gT.wait(e, tt - 2)
                e.tensor_scalar(out=lx[:, :], in0=res[:, tt * D:(tt + 1) * D],
                                scalar1=ln_mean[:, tt:tt + 1], scalar2=ln_rstd[:, tt:tt + 1],
                                op0=ALU.subtract, op1=ALU.mult)
                e.drain()
                e.sem_inc(sems["sV"], 1)
            E.inc("sV", 1)
            gN.mark("sV")
            if E.on("tensor"):
                gN.wait(e, tt)
                gEv.wait(e, tt - 1)
                last = None
                for g in range(2):
                    pbank = psE if g == 0 else psF
                    for f8 in range(8):
                        f = g * 8 + f8
                        last = e.transpose(pbank[:, f8 * 128:(f8 + 1) * 128],
                                           lx[:, f * 128:(f + 1) * 128], ident[:, :])
                last.then_inc(sems["sT"], 1)
            E.inc("sT", 1)
            gT.mark("sT")
            if E.on("vector"):
                gT.wait(e, tt)
                for g in range(2):
                    pbank = psE if g == 0 else psF
                    for f8 in range(8):
                        f = g * 8 + f8
                        e.tensor_copy(dst_sb[:, f * TOK + tt * 128: f * TOK + (tt + 1) * 128],
                                      pbank[:, f8 * 128:(f8 + 1) * 128])
                e.drain()
                e.sem_inc(sems["sV2"], 1)
            E.inc("sV2", 1)
            gEv.mark("sV2")
        return gEv

    def ph_a0(E):
        e = E.eng
        gEv = emit_ln_transpose(E, xnT_sb)
        if E.on("sync"):
            gEv.wait_last(e)
            e.dma_start(
                out=h_xnT_own[:, :, :].transpose([1, 0, 2]),
                in_=xnT_sb[:, :].rearrange("p (f t) -> p f t", t=TOK),
            ).then_inc(sems["sQ"], 16)
        E.inc("sQ", 16)
        E.flush(e, "sQ")
        st = E.val("sQ")
        if E.on("sync"):
            e.wait_ge(sems["sQ"], st)
            e.sem_inc(sems["sA2"], 1)
        E.inc("sA2", 1)
        fla0 = E.val("sA2")
        if E.on("gpsimd"):
            e.wait_ge(sems["sA2"], fla0)
            e.collective_compute(
                "AllGather", ALU.bypass, replica_groups=[list(range(NC))],
                ins=[h_xnT_own[:, :, :].opt()], outs=[h_xnT_all[:, :, :, :].opt()],
            ).then_inc(sems["sCC"], 1)
        E.inc("sCC", 1)

    def ph_a_pass1(E):
        e = E.eng
        cc_ready = E.val("sCC")
        gLd = E.guard()
        gMM = E.guard()
        gHEv = E.guard()
        gGEv = E.guard()
        gXEv = E.guard()
        gConv = E.guard()
        gHs = E.guard()
        gXmm = E.guard()
        gSt = E.guard()
        for i in range(NTC):
            owner, trow = i // 2, (i % 2) * TC
            buf = xn_mov0 if i % 2 == 0 else xn_mov1
            if E.on("sync"):
                e.wait_ge(sems["sCC"], cc_ready)
                gMM.wait(e, i - 2)
                e.dma_start(
                    out=buf[:, :].rearrange("p (f t) -> p f t", t=TC),
                    in_=h_xnT_all[owner, :, :, trow:trow + TC].transpose([1, 0, 2]),
                ).then_inc(sems["sQ"], 16)
            E.inc("sQ", 16)
            E.flush(e, "sQ")
            gLd.mark("sQ")
            if E.on("tensor"):
                E.dma_wait(e, "sQ")
            for m in range(4):
                pb = psA if m % 2 == 0 else psB
                if E.on("tensor"):
                    gHEv.wait(e, 4 * i + m - 2)
                    for k in range(16):
                        last = e.matmul(pb[:, :], wi_h_sb[:, k * 512 + m * 128: k * 512 + (m + 1) * 128],
                                        buf[:, k * TC:(k + 1) * TC], start=(k == 0), stop=(k == 15))
                    last.then_inc(sems["sT"], 1)
                E.inc("sT", 1)
                gMMm = E.val("sT")
                if E.on("vector"):
                    e.wait_ge(sems["sT"], gMMm)
                    e.tensor_copy(hconv[:, m * (TC + 3) + 3: m * (TC + 3) + 3 + TC], pb[:, :])
                    e.drain()
                    e.sem_inc(sems["sV2"], 1)
                E.inc("sV2", 1)
                gHEv.mark("sV2")
            for m in range(4):
                pb = psC if m % 2 == 0 else psD
                if E.on("tensor"):
                    gGEv.wait(e, 4 * i + m - 2)
                    for k in range(16):
                        last = e.matmul(pb[:, :], wi_g_sb[:, k * 512 + m * 128: k * 512 + (m + 1) * 128],
                                        buf[:, k * TC:(k + 1) * TC], start=(k == 0), stop=(k == 15))
                    last.then_inc(sems["sT"], 1)
                E.inc("sT", 1)
                gMMg = E.val("sT")
                if E.on("scalar"):
                    e.wait_ge(sems["sT"], gMMg)
                    if m == 0:
                        E.dma_wait(e, "sQ")
                    e.activation(out=sg_ch[:, m * TC:(m + 1) * TC], in_=pb[:, :], func=AF.Silu)
                    e.drain()
                    e.sem_inc(sems["sA2"], 1)
                E.inc("sA2", 1)
                gGEv.mark("sA2")
            gMM.mark("sT")
            if E.on("vector"):
                if i % 8 == 0:
                    for m in range(4):
                        e.memset(hconv[:, m * (TC + 3): m * (TC + 3) + 3], 0.0)
                else:
                    for m in range(4):
                        e.tensor_copy(hconv[:, m * (TC + 3): m * (TC + 3) + 3],
                                      hist[:, m * 3:(m + 1) * 3])
                e.drain()
                gHs.wait(e, i - 1)
                for m in range(4):
                    base = m * (TC + 3)
                    e.tensor_scalar(out=convo[:, m * TC:(m + 1) * TC],
                                    in0=hconv[:, base: base + TC],
                                    scalar1=convw_sb[:, m * 4:m * 4 + 1], scalar2=None,
                                    op0=ALU.mult)
                    e.drain()
                    for kk in range(1, 4):
                        e.scalar_tensor_tensor(
                            out=convo[:, m * TC:(m + 1) * TC],
                            in0=hconv[:, base + kk: base + kk + TC],
                            scalar=convw_sb[:, m * 4 + kk:m * 4 + kk + 1],
                            in1=convo[:, m * TC:(m + 1) * TC],
                            op0=ALU.mult, op1=ALU.add)
                        e.drain()
                for m in range(4):
                    e.tensor_copy(hist[:, m * 3:(m + 1) * 3],
                                  hconv[:, m * (TC + 3) + TC: m * (TC + 3) + TC + 3])
                e.drain()
                e.sem_inc(sems["sV"], 1)
            E.inc("sV", 1)
            gConv.mark("sV")
            if E.on("scalar"):
                gConv.wait(e, i)
                gXmm.wait(e, i - 1)
                E.dma_wait(e, "sQ")
                for m in range(4):
                    if m:
                        e.drain()
                    e.activation(out=hs_ch[:, m * TC:(m + 1) * TC],
                                 in_=convo[:, m * TC:(m + 1) * TC],
                                 func=AF.Silu, bias=convb_sb[:, m:m + 1])
                e.drain()
                e.sem_inc(sems["sA"], 1)
            E.inc("sA", 1)
            gHs.mark("sA")
            for tb in range(4):
                pb = psG if tb % 2 == 0 else psH
                if E.on("tensor"):
                    if tb == 0:
                        gHs.wait(e, i)
                    gXEv.wait(e, 4 * i + tb - 2)
                    for k in range(4):
                        last = e.matmul(pb[:, 0:160],
                                        hs_ch[:, k * TC + tb * 128: k * TC + (tb + 1) * 128],
                                        xw_sb[:, k * 160:(k + 1) * 160],
                                        start=(k == 0), stop=(k == 3))
                    last.then_inc(sems["sT"], 1)
                E.inc("sT", 1)
                x_now = E.val("sT")
                if E.on("vector"):
                    e.wait_ge(sems["sT"], x_now)
                    if tb == 0:
                        E.dma_wait(e, "sQ")
                    e.tensor_copy(ssmp_ev[:, tb * 160:(tb + 1) * 160], pb[:, 0:160])
                    e.drain()
                    e.sem_inc(sems["sV2"], 1)
                E.inc("sV2", 1)
                gXEv.mark("sV2")
            gXmm.mark("sT")
            if E.on("sync"):
                gHs.wait(e, i)
                gGEv.wait(e, 4 * i + 3)
                gXEv.wait(e, 4 * i + 3)
                e.dma_start(out=h_hsT[:, :, i * TC:(i + 1) * TC].transpose([1, 0, 2]),
                            in_=hs_ch[:, :].rearrange("p (m t) -> p m t", t=TC)).then_inc(sems["sQ"], 16)
                e.dma_start(out=h_sgT[:, :, i * TC:(i + 1) * TC].transpose([1, 0, 2]),
                            in_=sg_ch[:, :].rearrange("p (m t) -> p m t", t=TC)).then_inc(sems["sQ"], 16)
                e.dma_start(out=h_ssmp_part[owner, trow:trow + TC, :].rearrange("(tb p) c -> p tb c", p=128),
                            in_=ssmp_ev[:, :].rearrange("p (tb c) -> p tb c", c=160)).then_inc(sems["sQ"], 16)
            E.inc("sQ", 48)
            E.flush(e, "sQ")
            gSt.mark("sQ")
        return gSt

    def ph_a_reduce(E, gSt):
        e = E.eng
        if E.on("sync"):
            gSt.wait_last(e)
            e.sem_inc(sems["sA2"], 1)
        E.inc("sA2", 1)
        flp1 = E.val("sA2")
        if E.on("gpsimd"):
            e.wait_ge(sems["sA2"], flp1)
            with allow_ici_all_to_all():
                e.collective_compute(
                    "AllToAll", ALU.bypass, replica_groups=[list(range(NC))],
                    ins=[h_ssmp_part[:, :, :].opt()], outs=[h_ssmp_recv[:, :, :].opt()],
                ).then_inc(sems["sCC"], 1)
        E.inc("sCC", 1)
        cc_now = E.val("sCC")
        gLd = E.guard()
        gRed = E.guard()
        gNrm = E.guard()
        gNSt = E.guard()
        for tt in range(8):
            red_sc = red0_sc if tt % 2 == 0 else red1_sc
            if E.on("sync"):
                e.wait_ge(sems["sCC"], cc_now)
                gRed.wait(e, tt - 2)
                e.dma_start(
                    out=red_sc[:, :].rearrange("p (s c) -> p s c", c=160),
                    in_=h_ssmp_recv[:, tt * 128:(tt + 1) * 128, :].transpose([1, 0, 2]),
                ).then_inc(sems["sQ"], 16)
            E.inc("sQ", 16)
            E.flush(e, "sQ")
            gLd.mark("sQ")
            if E.on("vector"):
                E.dma_wait(e, "sQ")
                e.tensor_tensor(out=ssum_sc[:, :], in0=red_sc[:, 0:160],
                                in1=red_sc[:, 160:320], op=ALU.add)
                e.drain()
                for s in range(2, 8):
                    e.tensor_tensor(out=ssum_sc[:, :], in0=ssum_sc[:, :],
                                    in1=red_sc[:, s * 160:(s + 1) * 160], op=ALU.add)
                    e.drain()
                e.sem_inc(sems["sV"], 1)
            E.inc("sV", 1)
            gRed.mark("sV")
            red_done = E.val("sV")
            if E.on("scalar"):
                e.wait_ge(sems["sV"], red_done)
                e.activation(out=sq_junk[:, 0:128], in_=ssum_sc[:, 0:128],
                             func=AF.Square, accum_out=rms_s[:, 0:1])
                e.drain()
                e.activation(out=sq_junk[:, 0:16], in_=ssum_sc[:, 128:144],
                             func=AF.Square, accum_out=rms_s[:, 1:2])
                e.drain()
                e.activation(out=sq_junk[:, 0:16], in_=ssum_sc[:, 144:160],
                             func=AF.Square, accum_out=rms_s[:, 2:3])
                e.drain()
                e.sem_inc(sems["sA"], 1)
            E.inc("sA", 1)
            sq_done = E.val("sA")
            if E.on("vector"):
                e.wait_ge(sems["sA"], sq_done)
                e.tensor_scalar(out=rms_s[:, 0:1], in0=rms_s[:, 0:1], scalar1=1.0 / DTRANK,
                                scalar2=None, op0=ALU.mult)
                e.tensor_scalar(out=rms_s[:, 1:3], in0=rms_s[:, 1:3], scalar1=1.0 / NSTATE,
                                scalar2=None, op0=ALU.mult)
                e.drain()
                e.sem_inc(sems["sV"], 1)
            E.inc("sV", 1)
            ms_done = E.val("sV")
            if E.on("scalar"):
                e.wait_ge(sems["sV"], ms_done)
                e.activation(out=rms_s[:, 0:3], in_=rms_s[:, 0:3], func=AF.Sqrt,
                             bias=K_RMSEPS).then_inc(sems["sA"], 1)
            E.inc("sA", 1)
            rt_done = E.val("sA")
            if E.on("vector"):
                e.wait_ge(sems["sA"], rt_done)
                E.dma_wait(e, "sQ")
                e.reciprocal(out=rms_r[:, 0:3], in_=rms_s[:, 0:3])
                e.drain()
                e.tensor_scalar(out=nrm_tm[:, 0:128], in0=ssum_sc[:, 0:128],
                                scalar1=rms_r[:, 0:1], scalar2=None, op0=ALU.mult)
                e.tensor_scalar(out=nrm_tm[:, 128:144], in0=ssum_sc[:, 128:144],
                                scalar1=rms_r[:, 1:2], scalar2=None, op0=ALU.mult)
                e.tensor_scalar(out=nrm_tm[:, 144:160], in0=ssum_sc[:, 144:160],
                                scalar1=rms_r[:, 2:3], scalar2=None, op0=ALU.mult)
                e.drain()
                e.sem_inc(sems["sV"], 1)
            E.inc("sV", 1)
            gNrm.mark("sV")
            nrm_done = E.val("sV")
            if E.on("sync"):
                e.wait_ge(sems["sV"], nrm_done)
                e.dma_start(out=h_nrm_own[tt * 128:(tt + 1) * 128, :],
                            in_=nrm_tm[:, 0:160]).then_inc(sems["sQ"], 16)
            E.inc("sQ", 16)
            E.flush(e, "sQ")
            gNSt.mark("sQ")
        st_all = E.val("sQ")
        if E.on("sync"):
            e.wait_ge(sems["sQ"], st_all)
            e.sem_inc(sems["sA2"], 1)
        E.inc("sA2", 1)
        flnr = E.val("sA2")
        if E.on("gpsimd"):
            e.wait_ge(sems["sA2"], flnr)
            e.collective_compute(
                "AllGather", ALU.bypass, replica_groups=[list(range(NC))],
                ins=[h_nrm_own[:, :].opt()], outs=[h_nrm_all[:, :, :].opt()],
            ).then_inc(sems["sCC"], 1)
        E.inc("sCC", 1)

    def ph_a_pass2(E):
        e = E.eng
        cc_now = E.val("sCC")
        gLd = E.guard()
        gTp = E.guard()
        gTpe = E.guard()
        gDt = E.guard()
        gSp = E.guard()
        gDu = E.guard()
        gBC = E.guard()
        gScan = E.guard()
        gCe = E.guard()
        gDA = E.guard()
        gYg = E.guard()
        gSt = E.guard()
        for i in range(NTC):
            owner, trow = i // 2, (i % 2) * TC
            if E.on("sync"):
                if i == 0:
                    e.wait_ge(sems["sCC"], cc_now)
                gYg.wait(e, i - 1)
                E.dma_wait(e, "sQ")
                e.dma_start(out=nrm_tm[:, :].rearrange("p (tb c) -> p tb c", c=160),
                            in_=h_nrm_all[owner, trow:trow + TC, :]
                            .rearrange("(tb p) c -> p tb c", p=128)).then_inc(sems["sQ"], 16)
                e.dma_start(out=hs_ch[:, :].rearrange("p (m t) -> p m t", t=TC),
                            in_=h_hsT[:, :, i * TC:(i + 1) * TC].transpose([1, 0, 2])).then_inc(sems["sQ"], 16)
                e.dma_start(out=sg_ch[:, :].rearrange("p (m t) -> p m t", t=TC),
                            in_=h_sgT[:, :, i * TC:(i + 1) * TC].transpose([1, 0, 2])).then_inc(sems["sQ"], 16)
            E.inc("sQ", 48)
            E.flush(e, "sQ")
            gLd.mark("sQ")
            if E.on("tensor"):
                E.dma_wait(e, "sQ")
                gTpe.wait(e, i - 1)
                for tb in range(4):
                    e.transpose(psE[:, tb * 128:(tb + 1) * 128],
                                nrm_tm[:, tb * 160: tb * 160 + 128], ident[:, :])
                for tb in range(4):
                    e.transpose(psF[0:16, tb * 128:(tb + 1) * 128],
                                nrm_tm[:, tb * 160 + 128: tb * 160 + 144], ident[:, :])
                for tb in range(4):
                    last = e.transpose(psF[0:16, 512 + tb * 128: 512 + (tb + 1) * 128],
                                       nrm_tm[:, tb * 160 + 144: tb * 160 + 160], ident[:, :])
                last.then_inc(sems["sT"], 1)
            E.inc("sT", 1)
            gTp.mark("sT")
            if E.on("vector"):
                gTp.wait(e, i)
                gDt.wait(e, 4 * i - 1)
                gBC.wait(e, 16 * i - 1)
                e.tensor_copy(dtnT[:, :], psE[:, 0:512])
                e.tensor_copy(bnT[:, :], psF[0:16, 0:512])
                e.tensor_copy(cnT[:, :], psF[0:16, 512:1024])
                e.drain()
                e.sem_inc(sems["sV2"], 1)
            E.inc("sV2", 1)
            gTpe.mark("sV2")
            tpe_now = E.val("sV2")
            for m in range(4):
                pb = [psA, psB, psC, psD][m]
                if E.on("tensor"):
                    if m == 0:
                        e.wait_ge(sems["sV2"], tpe_now)
                        gSp.wait(e, i - 1)
                        gScan.wait(e, 16 * i - 1)
                    e.matmul(pb[:, :], wdt_sb[:, m * 128:(m + 1) * 128], dtnT[:, :],
                             start=True, stop=True).then_inc(sems["sT"], 1)
                E.inc("sT", 1)
                gDt.mark("sT")
                dt_now = E.val("sT")
                if E.on("scalar"):
                    e.wait_ge(sems["sT"], dt_now)
                    if m == 0:
                        gDu.wait(e, i - 1)
                        gScan.wait(e, 16 * i - 1)
                    sc1 = sp_sc[:, m * TC:(m + 1) * TC]
                    e.activation(out=sc1, in_=pb[:, :], func=AF.Abs, bias=dtb_sb[:, m:m + 1])
                    e.drain()
                    e.activation(out=sc1, in_=sc1, func=AF.Exp, scale=-1.0)
                    e.drain()
                    e.activation(out=sc1, in_=sc1, func=AF.Ln, bias=1.0)
                    e.drain()
                    e.activation(out=dtT_ch[:, m * TC:(m + 1) * TC], in_=pb[:, :],
                                 func=AF.Relu, bias=dtb_sb[:, m:m + 1])
                    e.drain()
                    if m == 3:
                        e.sem_inc(sems["sA"], 1)
            E.inc("sA", 1)
            gSp.mark("sA")
            sp_now = E.val("sA")
            if E.on("vector"):
                e.wait_ge(sems["sA"], sp_now)
                gScan.wait(e, 16 * i - 1)
                e.tensor_tensor(out=dtT_ch[:, :], in0=dtT_ch[:, :], in1=sp_sc[:, :], op=ALU.add)
                e.drain()
                e.tensor_tensor(out=du_bf[:, :], in0=dtT_ch[:, :], in1=hs_ch[:, :], op=ALU.mult)
                e.memset(yacc[:, :], 0.0)
                if i == 8:
                    e.memset(sstate[:, :], 0.0)
                e.drain()
                e.sem_inc(sems["sV"], 1)
            E.inc("sV", 1)
            gDu.mark("sV")
            du_now = E.val("sV")
            if i == 8:
                pass
            for n in range(NSTATE):
                pbB, pbC = (psG, psH) if n % 2 == 0 else (psC, psD)
                s_buf = s_sc if n % 2 == 0 else s_sc1
                c_buf = csb0 if n % 2 == 0 else csb1
                b_buf = bsb0 if n % 2 == 0 else bsb1
                dA_sc = dA0_sc if n % 2 == 0 else dA1_sc
                if E.on("tensor"):
                    if n == 0:
                        e.wait_ge(sems["sV2"], tpe_now)
                    if n == 1:
                        gSp.wait(e, i)
                    gCe.wait(e, 16 * i + n - 2)
                    e.matmul(pbB[:, :], en_sb[:, n * 128:(n + 1) * 128], bnT[:, :],
                             start=True, stop=True)
                    e.matmul(pbC[:, :], en_sb[:, n * 128:(n + 1) * 128], cnT[:, :],
                             start=True, stop=True).then_inc(sems["sT"], 1)
                E.inc("sT", 1)
                gBC.mark("sT")
                bc_now = E.val("sT")
                if E.on("scalar"):
                    if n == 0:
                        e.wait_ge(sems["sV"], du_now)
                    gScan.wait(e, 16 * i + n - 2)
                    for m in range(4):
                        if m:
                            e.drain()
                        e.activation(out=dA_sc[:, m * TC:(m + 1) * TC],
                                     in_=dtT_ch[:, m * TC:(m + 1) * TC], func=AF.Exp,
                                     scale=aneg_sb[:, m * NSTATE + n: m * NSTATE + n + 1])
                    e.drain()
                    e.wait_ge(sems["sT"], bc_now)
                    gScan.wait(e, 16 * i + n - 2)
                    e.activation(out=b_buf[:, :], in_=pbB[:, :], func=AF.Copy)
                    e.drain()
                    e.activation(out=c_buf[:, :], in_=pbC[:, :], func=AF.Copy)
                    e.drain()
                    e.sem_inc(sems["sA"], 1)
                E.inc("sA", 1)
                gDA.mark("sA")
                gCe.mark("sA")
                da_now = E.val("sA")
                if E.on("vector"):
                    e.wait_ge(sems["sA"], da_now)
                    e.tensor_tensor(
                        out=dBu_sc[:, :].rearrange("p (m t) -> p m t", t=TC),
                        in0=du_bf[:, :].rearrange("p (m t) -> p m t", t=TC),
                        in1=b_buf[:, :].rearrange("p (o t) -> p o t", o=1).broadcast_to([128, 4, TC]),
                        op=ALU.mult)
                    e.drain()
                    for m in range(4):
                        e.tensor_tensor_scan(
                            out=s_buf[:, m * TC:(m + 1) * TC],
                            data0=dA_sc[:, m * TC:(m + 1) * TC],
                            data1=dBu_sc[:, m * TC:(m + 1) * TC],
                            initial=sstate[:, m * NSTATE + n: m * NSTATE + n + 1],
                            op0=ALU.mult, op1=ALU.add)
                    e.drain()
                    e.tensor_copy(
                        sstate[:, n:n + 1].rearrange("p (c m) -> p m c", c=1).broadcast_to([128, 4, 1])
                        if False else
                        sstate[:, :].rearrange("p (m n2) -> p m n2", n2=NSTATE)[:, :, n:n + 1],
                        s_buf[:, :].rearrange("p (m t) -> p m t", t=TC)[:, :, TC - 1:TC])
                    e.tensor_tensor(
                        out=dBu_sc[:, :].rearrange("p (m t) -> p m t", t=TC),
                        in0=s_buf[:, :].rearrange("p (m t) -> p m t", t=TC),
                        in1=c_buf[:, :].rearrange("p (o t) -> p o t", o=1).broadcast_to([128, 4, TC]),
                        op=ALU.mult)
                    e.drain()
                    e.tensor_tensor(out=yacc[:, :], in0=yacc[:, :], in1=dBu_sc[:, :], op=ALU.add)
                    e.drain()
                    e.sem_inc(sems["sV"], 1)
                E.inc("sV", 1)
                gScan.mark("sV")
            if E.on("vector"):
                for m in range(4):
                    e.scalar_tensor_tensor(
                        out=yacc[:, m * TC:(m + 1) * TC],
                        in0=hs_ch[:, m * TC:(m + 1) * TC],
                        scalar=dpar_sb[:, m:m + 1],
                        in1=yacc[:, m * TC:(m + 1) * TC],
                        op0=ALU.mult, op1=ALU.add)
                e.drain()
                E.dma_wait(e, "sQ")
                gBC.wait(e, 16 * i + 15)
                e.tensor_tensor(out=yg_ev[:, :], in0=yacc[:, :], in1=sg_ch[:, :], op=ALU.mult)
                e.drain()
                e.sem_inc(sems["sV"], 1)
            E.inc("sV", 1)
            gYg.mark("sV")
            yg_now = E.val("sV")
            if E.on("sync"):
                e.wait_ge(sems["sV"], yg_now)
                e.dma_start(out=h_yg_stage[owner, :, :, trow:trow + TC].transpose([1, 0, 2]),
                            in_=yg_ev[:, :].rearrange("p (m t) -> p m t", t=TC)).then_inc(sems["sQ"], 16)
            E.inc("sQ", 16)
            E.flush(e, "sQ")
            gSt.mark("sQ")
        return gSt

    def ph_a_out(E, gSt):
        e = E.eng
        if E.on("sync"):
            gSt.wait_last(e)
            e.sem_inc(sems["sA2"], 1)
        E.inc("sA2", 1)
        flp2 = E.val("sA2")
        if E.on("gpsimd"):
            e.wait_ge(sems["sA2"], flp2)
            with allow_ici_all_to_all():
                e.collective_compute(
                    "AllToAll", ALU.bypass, replica_groups=[list(range(NC))],
                    ins=[h_yg_stage[:, :, :, :].opt()], outs=[h_yg_recv[:, :, :, :].opt()],
                ).then_inc(sems["sCC"], 1)
        E.inc("sCC", 1)
        cc_now = E.val("sCC")
        gYL = E.guard()
        gWL = E.guard()
        gMM = E.guard()
        gAcc = E.guard()
        pass2_done_q = E.val("sQ")
        for kg in range(4):
            if E.on("sync"):
                e.wait_ge(sems["sCC"], cc_now)
                gMM.wait(e, 32 * kg - 1)
                e.dma_start(
                    out=ygf_sb[:, :].rearrange("p (k tb c) -> p k tb c", tb=8, c=128),
                    in_=h_yg_recv[kg * 2:(kg + 1) * 2, :, :, :]
                    .rearrange("s m p (tb c) -> p (s m) tb c", c=128),
                ).then_inc(sems["sQ"], 16)
            E.inc("sQ", 16)
            E.flush(e, "sQ")
            gYL.mark("sQ")
            if E.on("gpsimd"):
                if kg == 0:
                    e.wait_ge(sems["sQ"], pass2_done_q)
                gMM.wait(e, 32 * kg - 1)
                e.dma_start(
                    out=wout_kg[:, :].rearrange("p (kk c) -> p kk c", c=D),
                    in_=wout[kg * 1024:(kg + 1) * 1024, :].rearrange("(kk p) c -> p kk c", p=128),
                ).then_inc(sems["sG"], 16)
            E.inc("sG", 16)
            E.flush(e, "sG")
            gWL.mark("sG")
            for tb in range(8):
                for nq in range(4):
                    pb = [psA, psB, psC, psD][nq]
                    if E.on("tensor"):
                        if tb == 0 and nq == 0:
                            E.dma_wait(e, "sQ")
                            E.dma_wait(e, "sG")
                        gAcc.wait(e, 32 * kg + 4 * tb + nq - 4)
                        for kk in range(8):
                            last = e.matmul(pb[:, :],
                                            ygf_sb[:, (kk * 8 + tb) * 128:(kk * 8 + tb + 1) * 128],
                                            wout_kg[:, kk * D + nq * 512: kk * D + (nq + 1) * 512],
                                            start=(kk == 0), stop=(kk == 7))
                        last.then_inc(sems["sT"], 1)
                    E.inc("sT", 1)
                    gMM.mark("sT")
                    mm_now = E.val("sT")
                    if E.on("vector"):
                        e.wait_ge(sems["sT"], mm_now)
                        if "d_op" in dbg_t and tb == 0 and nq == 0:
                            e.tensor_copy(s_sc[:, kg * 512:(kg + 1) * 512], pb[:, :])
                            e.drain()
                        dst = res[:, tb * D + nq * 512: tb * D + (nq + 1) * 512]
                        e.tensor_tensor(out=dst, in0=dst, in1=pb[:, :], op=ALU.add)
                        e.drain()
                        e.sem_inc(sems["sV"], 1)
                    E.inc("sV", 1)
                    gAcc.mark("sV")
        r_now = E.val("sV")
        for nm in ("sync", "gpsimd", "scalar", "tensor"):
            if E.on(nm):
                e.wait_ge(sems["sV"], r_now)


    def ph_w8(E):
        e = E.eng
        jobs = []
        jobs.append((w8g_slice, h_w8g_own, 2, FF1))
        jobs.append((w8u_slice, h_w8u_own, 2, FF1))
        jobs.append((w8d_slice, h_w8d_own, 6, D))
        gLd = E.guard()
        gSc1 = E.guard()
        gV1 = E.guard()
        gSc2 = E.guard()
        gV2g = E.guard()
        gStq = E.guard()
        ci = 0
        for src_ap, dst_h, ntile, cols in jobs:
            for rt in range(ntile):
                for cc in range(cols // 2048):
                    ws, t1, qq, qi, bf = ((q_ws, q_t1, q_q, q_qi, q_bf) if ci % 2 == 0
                                          else (q_ws2, q_t12, q_q2, q_qi2, q_bf2))
                    lqs = LANES[(2 * ci) % 3]
                    sqs = LANES[(2 * ci + 1) % 3]
                    if E.on("sync"):
                        gV2g.wait(e, ci - 2)
                        E.lane_open(e, lqs)
                        e.dma_start(out=ws[:, :],
                                    in_=src_ap[rt * 128:(rt + 1) * 128, cc * 2048:(cc + 1) * 2048]
                                    ).then_inc(sems[lqs], 16)
                    E.inc(lqs, 16)
                    gLd.mark(lqs)
                    if E.on("scalar"):
                        gLd.wait(e, ci)
                        gV2g.wait(e, ci - 2)
                        e.activation(out=t1[:, :], in_=ws[:, :], func=AF.Abs)
                        e.drain()
                        e.activation(out=t1[:, :], in_=t1[:, :], func=AF.Ln, bias=K_TINY)
                        e.drain()
                        e.sem_inc(sems["sA"], 1)
                    E.inc("sA", 1)
                    gSc1.mark("sA")
                    if E.on("vector"):
                        gSc1.wait(e, ci)
                        e.tensor_scalar(out=t1[:, :], in0=t1[:, :], scalar1=INV_LN2,
                                        scalar2=-0.5, op0=ALU.mult, op1=ALU.add)
                        e.drain()
                        e.tensor_scalar(out=t1[:, :], in0=t1[:, :], scalar1=MAGIC,
                                        scalar2=None, op0=ALU.add)
                        e.drain()
                        e.tensor_scalar(out=t1[:, :], in0=t1[:, :], scalar1=-MAGIC,
                                        scalar2=-6.0, op0=ALU.add, op1=ALU.max)
                        e.drain()
                        e.sem_inc(sems["sV"], 1)
                    E.inc("sV", 1)
                    gV1.mark("sV")
                    if E.on("scalar"):
                        gV1.wait(e, ci)
                        gV2g.wait(e, ci - 2)
                        e.activation(out=qq[:, :], in_=t1[:, :], func=AF.Exp,
                                     scale=K_LN2, bias=K_N3LN2)
                        e.activation(out=qi[:, :], in_=t1[:, :], func=AF.Exp,
                                     scale=K_NLN2, bias=K_P3LN2)
                        e.drain()
                        e.sem_inc(sems["sA"], 1)
                    E.inc("sA", 1)
                    gSc2.mark("sA")
                    if E.on("vector"):
                        gSc2.wait(e, ci)
                        gStq.wait(e, ci - 2)
                        e.tensor_tensor(out=t1[:, :], in0=ws[:, :], in1=qi[:, :], op=ALU.mult)
                        e.drain()
                        e.tensor_scalar(out=t1[:, :], in0=t1[:, :], scalar1=MAGIC,
                                        scalar2=-MAGIC, op0=ALU.add, op1=ALU.add)
                        e.drain()
                        e.tensor_tensor(out=bf[:, :], in0=t1[:, :], in1=qq[:, :], op=ALU.mult)
                        e.drain()
                        e.sem_inc(sems["sV"], 1)
                    E.inc("sV", 1)
                    gV2g.mark("sV")
                    v2now = E.val("sV")
                    if E.on("sync"):
                        e.wait_ge(sems["sV"], v2now)
                        E.lane_open(e, sqs)
                        e.dma_start(out=dst_h[rt * 128:(rt + 1) * 128, cc * 2048:(cc + 1) * 2048],
                                    in_=bf[:, :]).then_inc(sems[sqs], 16)
                    E.inc(sqs, 16)
                    gStq.mark(sqs)
                    ci += 1
        if E.on("sync"):
            for _qs in LANES:
                e.wait_ge(sems[_qs], E.val(_qs))
            e.sem_inc(sems["sA2"], 1)
        E.inc("sA2", 1)
        flw8 = E.val("sA2")
        if E.on("gpsimd"):
            e.wait_ge(sems["sA2"], flw8)
            for own, full in ((h_w8g_own, h_w8g), (h_w8u_own, h_w8u), (h_w8d_own, h_w8d)):
                e.collective_compute(
                    "AllGather", ALU.bypass, replica_groups=[list(range(NC))],
                    ins=[own.opt()], outs=[full.opt()],
                ).then_inc(sems["sCC"], 1)
        E.inc("sCC", 3)

    def ph_b(E):
        e = E.eng
        gEvL = emit_ln_transpose(E, xnT_sb)
        ln_done = E.val("sV2")
        cc_need = E.val("sCC")
        w8g_f = h_w8g.rearrange("s p c -> (s p) c")
        w8u_f = h_w8u.rearrange("s p c -> (s p) c")
        w8d_f = h_w8d.rearrange("s p c -> (s p) c")
        gWL = E.guard()
        gWLgu = E.guard()
        gGU = E.guard()
        gMMg = E.guard()
        gMMu = E.guard()
        gMMd = E.guard()
        gAcc = E.guard()
        gZ = E.guard()
        gSilu = E.guard()
        gTpO = E.guard()
        gRes = E.guard()
        gPrep = E.guard()
        si = 0
        oi = 0
        for h in range(2):
            if E.on("vector"):
                if h == 0:
                    gEvL.wait_last(e)
                e.memset(b_acc[:, :], 0.0)
                e.drain()
                e.sem_inc(sems["sV"], 1)
            E.inc("sV", 1)
            gZ.mark("sV")
            for s in range(12):
                if E.on("sync"):
                    e.wait_ge(sems["sCC"], cc_need)
                    if h == 0 and s == 0:
                        e.wait_ge(sems["sV2"], ln_done)
                    gMMg.wait(e, (si - 1) * 4 + 3)
                    gMMu.wait(e, (si - 1) * 4 + 3)
                    e.dma_start(out=b_wgst[:, :].rearrange("p (k c) -> p k c", c=512),
                                in_=w8g_f[:, s * 512:(s + 1) * 512]
                                .rearrange("(k p) c -> p k c", p=128)).then_inc(sems["sQ"], 16)
                    e.dma_start(out=b_wust[:, :].rearrange("p (k c) -> p k c", c=512),
                                in_=w8u_f[:, s * 512:(s + 1) * 512]
                                .rearrange("(k p) c -> p k c", p=128)).then_inc(sems["sQ"], 16)
                E.inc("sQ", 32)
                E.flush(e, "sQ")
                gWLgu.mark("sFL")
                if E.on("sync"):
                    gMMd.wait(e, (si - 1) * 16 + 15)
                    e.dma_start(out=b_wdst[:, :].rearrange("p (ic c) -> p ic c", c=D),
                                in_=w8d_f[s * 512:(s + 1) * 512, :]
                                .rearrange("(ic p) c -> p ic c", p=128)).then_inc(sems["sQ"], 16)
                E.inc("sQ", 16)
                E.flush(e, "sQ")
                gWL.mark("sFL")
                for ic in range(4):
                    icg = s * 4 + ic
                    pbg = psA if ic % 2 == 0 else psB
                    pbu = psC if ic % 2 == 0 else psD
                    if E.on("tensor"):
                        if ic == 0:
                            gWLgu.wait(e, si)
                            if h == 0 and s == 0:
                                e.wait_ge(sems["sV2"], ln_done)
                        gSilu.wait(e, si * 4 + ic - 2)
                        for k in range(16):
                            last = e.matmul(pbg[:, :],
                                            b_wgst[:, k * 512 + ic * 128: k * 512 + (ic + 1) * 128],
                                            xnT_sb[:, k * TOK + h * 512: k * TOK + (h + 1) * 512],
                                            start=(k == 0), stop=(k == 15))
                        last.then_inc(sems["sT"], 1)
                    E.inc("sT", 1)
                    gMMg.mark("sT")
                    gmm = E.val("sT")
                    if E.on("tensor"):
                        gGU.wait(e, si * 4 + ic - 2)
                        for k in range(16):
                            last = e.matmul(pbu[:, :],
                                            b_wust[:, k * 512 + ic * 128: k * 512 + (ic + 1) * 128],
                                            xnT_sb[:, k * TOK + h * 512: k * TOK + (h + 1) * 512],
                                            start=(k == 0), stop=(k == 15))
                        last.then_inc(sems["sT"], 1)
                    E.inc("sT", 1)
                    gMMu.mark("sT")
                    umm = E.val("sT")
                    if E.on("scalar"):
                        e.wait_ge(sems["sT"], gmm)
                        gGU.wait(e, si * 4 + ic - 1)
                        e.activation(out=b_gs[:, :], in_=pbg[:, :], func=AF.Silu,
                                     scale=gsus_sb[:, icg:icg + 1])
                        e.drain()
                        e.sem_inc(sems["sA"], 1)
                    E.inc("sA", 1)
                    gSilu.mark("sA")
                    silu_now = E.val("sA")
                    if E.on("vector"):
                        e.wait_ge(sems["sT"], umm)
                        e.wait_ge(sems["sA"], silu_now)
                        e.tensor_scalar(out=b_us[:, :], in0=pbu[:, :],
                                        scalar1=gsus_sb[:, 48 + icg:48 + icg + 1],
                                        scalar2=None, op0=ALU.mult)
                        e.drain()
                        e.tensor_tensor(out=b_fuse[:, ic * 512:(ic + 1) * 512],
                                        in0=b_gs[:, :], in1=b_us[:, :], op=ALU.mult)
                        e.drain()
                        e.sem_inc(sems["sV"], 1)
                    E.inc("sV", 1)
                    gGU.mark("sV")
                fuse_done = E.val("sV")
                for oc in range(16):
                    pb = psG if oc % 2 == 0 else psH
                    if E.on("tensor"):
                        if oc == 0:
                            e.wait_ge(sems["sV"], fuse_done)
                            gWL.wait(e, si)
                            if s == 0:
                                gZ.wait(e, h)
                        gAcc.wait(e, si * 16 + oc - 2)
                        for ic in range(4):
                            last = e.matmul(pb[:, :],
                                            b_wdst[:, ic * D + oc * 128: ic * D + (oc + 1) * 128],
                                            b_fuse[:, ic * 512:(ic + 1) * 512],
                                            start=(ic == 0), stop=(ic == 3))
                        last.then_inc(sems["sT"], 1)
                    E.inc("sT", 1)
                    gMMd.mark("sT")
                    dmm = E.val("sT")
                    if E.on("vector"):
                        e.wait_ge(sems["sT"], dmm)
                        if s == 0 and oc == 0:
                            gZ.wait(e, h)
                        dst = b_acc[:, oc * 512:(oc + 1) * 512]
                        e.tensor_tensor(out=dst, in0=dst, in1=pb[:, :], op=ALU.add)
                        e.drain()
                        e.sem_inc(sems["sV"], 1)
                    E.inc("sV", 1)
                    gAcc.mark("sV")
                si += 1
            for oc in range(16):
                if E.on("vector"):
                    gTpO.wait(e, oi - 1)
                    e.tensor_scalar(out=b_tb[:, :], in0=b_acc[:, oc * 512:(oc + 1) * 512],
                                    scalar1=gsus_sb[:, 96 + oc:96 + oc + 1],
                                    scalar2=None, op0=ALU.mult)
                    e.drain()
                    e.sem_inc(sems["sV"], 1)
                E.inc("sV", 1)
                gPrep.mark("sV")
                prep_now = E.val("sV")
                if E.on("tensor"):
                    e.wait_ge(sems["sV"], prep_now)
                    gRes.wait(e, oi - 1)
                    for tc4 in range(4):
                        last = e.transpose(psE[:, tc4 * 128:(tc4 + 1) * 128],
                                           b_tb[:, tc4 * 128:(tc4 + 1) * 128], ident[:, :])
                    last.then_inc(sems["sT"], 1)
                E.inc("sT", 1)
                gTpO.mark("sT")
                tp_now = E.val("sT")
                if E.on("vector"):
                    e.wait_ge(sems["sT"], tp_now)
                    for tc4 in range(4):
                        tt_g = h * 4 + tc4
                        dst = res[:, tt_g * D + oc * 128: tt_g * D + (oc + 1) * 128]
                        e.tensor_tensor(out=dst, in0=dst,
                                        in1=psE[:, tc4 * 128:(tc4 + 1) * 128], op=ALU.add)
                    e.drain()
                    e.sem_inc(sems["sV"], 1)
                E.inc("sV", 1)
                gRes.mark("sV")
                oi += 1
        r_now = E.val("sV")
        for nm in ("sync", "gpsimd", "scalar", "tensor"):
            if E.on(nm):
                e.wait_ge(sems["sV"], r_now)



    def ph_wth(E):
        e = E.eng
        if E.on("sync"):
            e.dma_start(out=identf[:, :], in_=c_identf[:, :]).then_inc(sems["sQ"], 16)
            e.dma_start(out=onesf[:, :], in_=c_onesf[:, :]).then_inc(sems["sQ"], 16)
        E.inc("sQ", 32)
        E.flush(e, "sQ")
        jobs = [(wg_slice, 8, D), (wu_slice, 8, D), (wd_slice, 16, ISL)]
        gLd = E.guard()
        gRs = E.guard()
        ci = 0
        for j, (src_ap, ntile, cols) in enumerate(jobs):
            for rt in range(ntile):
                tw = t_ws if ci % 2 == 0 else t_wsB
                lqs = LANES[ci % 3]
                if E.on("sync"):
                    gRs.wait(e, ci - 2)
                    E.lane_open(e, lqs)
                    e.dma_start(out=tw[:, 0:cols], in_=src_ap[rt * 128:(rt + 1) * 128, :]
                                ).then_inc(sems[lqs], 16)
                E.inc(lqs, 16)
                gLd.mark(lqs)
                if E.on("vector"):
                    gLd.wait(e, ci)
                    if ci == 0:
                        e.memset(racc[:, :], 0.0)
                        e.drain()
                    e.tensor_reduce(out=d_amax[:, :], in_=tw[:, 0:cols], axis=AX.X,
                                    op=ALU.add, apply_absolute_value=True)
                    e.drain()
                    e.tensor_tensor(out=racc[:, j:j + 1], in0=racc[:, j:j + 1],
                                    in1=d_amax[:, :], op=ALU.add)
                    e.drain()
                    e.sem_inc(sems["sV"], 1)
                E.inc("sV", 1)
                gRs.mark("sV")
                ci += 1
        rs_done = E.val("sV")
        if E.on("tensor"):
            e.wait_ge(sems["sV"], rs_done)
            E.dma_wait(e, "sQ")
            e.transpose(psA[0:3, 0:128], racc[:, :], identf[:, :]).then_inc(sems["sT"], 1)
        E.inc("sT", 1)
        tp_now = E.val("sT")
        if E.on("vector"):
            e.wait_ge(sems["sT"], tp_now)
            e.tensor_reduce(out=d_amax[0:3, :], in_=psA[0:3, 0:128], axis=AX.X, op=ALU.add)
            e.drain()
            e.sem_inc(sems["sV"], 1)
        E.inc("sV", 1)
        ps_done = E.val("sV")
        if E.on("sync"):
            e.wait_ge(sems["sV"], ps_done)
            e.dma_start(out=h_th_own[:, :], in_=d_amax[0:3, 0:1]).then_inc(sems["sQ"], 16)
            e.wait_ge(sems["sQ"], E.val("sQ") + 16)
            e.sem_inc(sems["sA2"], 1)
        E.inc("sQ", 16)
        E.flush(e, "sQ")
        E.inc("sA2", 1)
        fl = E.val("sA2")
        if E.on("gpsimd"):
            e.wait_ge(sems["sA2"], fl)
            e.collective_compute(
                "AllReduce", ALU.add, replica_groups=[list(range(NC))],
                ins=[h_th_own.opt()], outs=[h_th_all.opt()],
            ).then_inc(sems["sCC"], 1)
        E.inc("sCC", 1)
        cc_th = E.val("sCC")
        if E.on("sync"):
            e.wait_ge(sems["sCC"], cc_th)
            e.dma_start(out=th_row[:, :], in_=h_th_all[:, :].transpose([1, 0])).then_inc(sems["sQ"], 16)
        E.inc("sQ", 16)
        E.flush(e, "sQ")
        th_ld = E.val("sQ")
        if E.on("tensor"):
            e.wait_ge(sems["sQ"], th_ld)
            e.matmul(psB[:, 0:3], onesf[:, :], th_row[:, :], start=True, stop=True
                     ).then_inc(sems["sT"], 1)
        E.inc("sT", 1)
        bc_now = E.val("sT")
        NW = float(DFF * D)
        if E.on("vector"):
            e.wait_ge(sems["sT"], bc_now)
            e.tensor_scalar(out=mbc[:, :], in0=psB[:, 0:3], scalar1=1.0 / NW,
                            scalar2=None, op0=ALU.mult)
            e.drain()
            e.tensor_scalar(out=thbc[:, :], in0=mbc[:, :], scalar1=0.5,
                            scalar2=None, op0=ALU.mult)
            e.drain()
            e.sem_inc(sems["sV"], 1)
        E.inc("sV", 1)
        th_done = E.val("sV")
        nth = fv(arB, 4096, 4)
        if E.on("vector"):
            e.wait_ge(sems["sV"], th_done)
            e.tensor_scalar(out=nth[:, 0:3], in0=thbc[:, :], scalar1=-1.0,
                            scalar2=None, op0=ALU.mult)
            e.drain()
            e.sem_inc(sems["sV"], 1)
        E.inc("sV", 1)
        nth_done = E.val("sV")
        jobs2 = [(wg_slice, h_wgT, 8, D),
                 (wu_slice, h_wuT, 8, D),
                 (wd_slice, h_wdT_own, 16, ISL)]
        gTn = E.guard()
        gTp2 = E.guard()
        gEv2 = E.guard()
        gLd2 = E.guard()
        gSt2 = E.guard()
        ci = 0
        gti = 0
        for j, (src_ap, dst_h, ntile, cols) in enumerate(jobs2):
            ncb = cols // 128
            for rt in range(ntile):
                tw = t_ws if ci % 2 == 0 else t_wsB
                tt = t_tern if ci % 2 == 0 else t_ternB
                tp = t_tp if ci % 2 == 0 else t_tpB
                lqs = LANES[(2 * ci) % 3]
                sqs = LANES[(2 * ci + 1) % 3]
                if E.on("sync"):
                    gTn.wait(e, ci - 2)
                    E.lane_open(e, lqs)
                    e.dma_start(out=tw[:, 0:cols], in_=src_ap[rt * 128:(rt + 1) * 128, :]
                                ).then_inc(sems[lqs], 16)
                E.inc(lqs, 16)
                gLd2.mark(lqs)
                if E.on("vector"):
                    gLd2.wait(e, ci)
                    if ci == 0:
                        e.wait_ge(sems["sV"], nth_done)
                    gTp2.wait(e, ci - 2)
                    e.tensor_scalar(out=t_le[:, 0:cols], in0=tw[:, 0:cols],
                                    scalar1=nth[:, j:j + 1], scalar2=None, op0=ALU.is_le)
                    e.drain()
                    e.scalar_tensor_tensor(out=tt[:, 0:cols], in0=tw[:, 0:cols],
                                           scalar=thbc[:, j:j + 1], op0=ALU.is_ge,
                                           in1=t_le[:, 0:cols], op1=ALU.subtract)
                    e.drain()
                    e.sem_inc(sems["sV"], 1)
                E.inc("sV", 1)
                gTn.mark("sV")
                tn_now = E.val("sV")
                for g8 in range((ncb + 7) // 8):
                    nh = min(8, ncb - g8 * 8)
                    pbank = psE if gti % 2 == 0 else psF
                    gti += 1
                    if E.on("tensor"):
                        if g8 == 0:
                            e.wait_ge(sems["sV"], tn_now)
                        gEv2.wait(e, len(gEv2.marks) - 2)
                        for cb8 in range(nh):
                            cb = g8 * 8 + cb8
                            last = e.transpose(pbank[:, cb8 * 128:(cb8 + 1) * 128],
                                               tt[:, cb * 128:(cb + 1) * 128], ident[:, :])
                        last.then_inc(sems["sT"], 1)
                    E.inc("sT", 1)
                    if g8 == (ncb + 7) // 8 - 1:
                        gTp2.mark("sT")
                    tg_now = E.val("sT")
                    if E.on("vector"):
                        e.wait_ge(sems["sT"], tg_now)
                        if g8 == 0:
                            gSt2.wait(e, ci - 2)
                        e.tensor_copy(tp[:, g8 * 1024: g8 * 1024 + nh * 128],
                                      pbank[:, 0:nh * 128])
                        e.drain()
                        e.sem_inc(sems["sV"], 1)
                    E.inc("sV", 1)
                    gEv2.mark("sV")
                ev_now = E.val("sV")
                if E.on("sync"):
                    e.wait_ge(sems["sV"], ev_now)
                    E.lane_open(e, sqs)
                    e.dma_start(out=dst_h[:, rt * 128:(rt + 1) * 128]
                                .rearrange("(cb p) c -> p cb c", p=128),
                                in_=tp[:, 0:ncb * 128].rearrange("p (cb c) -> p cb c", c=128)
                                ).then_inc(sems[sqs], 16)
                E.inc(sqs, 16)
                gSt2.mark(sqs)
                ci += 1
        if E.on("sync"):
            for _qs in LANES:
                e.wait_ge(sems[_qs], E.val(_qs))
            e.sem_inc(sems["sA2"], 1)
        E.inc("sA2", 1)
        fl2 = E.val("sA2")
        if E.on("gpsimd"):
            e.wait_ge(sems["sA2"], fl2)
            e.collective_compute(
                "AllGather", ALU.bypass, replica_groups=[list(range(NC))],
                ins=[h_wdT_own.opt()], outs=[h_wdT.opt()],
            ).then_inc(sems["sCC"], 1)
        E.inc("sCC", 1)
        r_now = E.val("sV")
        for nm in ("sync", "gpsimd", "scalar", "tensor"):
            if E.on(nm):
                e.wait_ge(sems["sV"], r_now)

    def ph_c(E):
        e = E.eng
        gEvL = emit_ln_transpose(E, xnT_sb)
        if E.on("sync"):
            gEvL.wait_last(e)
            e.dma_start(out=h_xn3_own[:, :, :].transpose([1, 0, 2]),
                        in_=xnT_sb[:, :].rearrange("p (f t) -> p f t", t=TOK)).then_inc(sems["sQ"], 16)
        E.inc("sQ", 16)
        E.flush(e, "sQ")
        st0 = E.val("sQ")
        if E.on("sync"):
            e.wait_ge(sems["sQ"], st0)
            e.sem_inc(sems["sA2"], 1)
        E.inc("sA2", 1)
        fl0 = E.val("sA2")
        if E.on("gpsimd"):
            e.wait_ge(sems["sA2"], fl0)
            e.collective_compute(
                "AllGather", ALU.bypass, replica_groups=[list(range(NC))],
                ins=[h_xn3_own.opt()], outs=[h_xn3_all.opt()],
            ).then_inc(sems["sCC"], 1)
        E.inc("sCC", 1)
        cc0 = E.val("sCC")
        if E.on("gpsimd"):
            e.dma_start(out=wi_h_sb[:, 0:16 * 256].rearrange("p (k c) -> p k c", c=256),
                        in_=wq_s.rearrange("(k p) c -> p k c", p=128)).then_inc(sems["sG"], 16)
            e.dma_start(out=wi_g_sb[:, 0:16 * 256].rearrange("p (k c) -> p k c", c=256),
                        in_=wk_s.rearrange("(k p) c -> p k c", p=128)).then_inc(sems["sG"], 16)
            e.dma_start(out=c_wv[:, :].rearrange("p (k c) -> p k c", c=256),
                        in_=wv_s.rearrange("(k p) c -> p k c", p=128)).then_inc(sems["sG"], 16)
            e.dma_start(out=bv_sb[:, :], in_=bv_row[:, :]).then_inc(sems["sG"], 16)
        E.inc("sG", 64)
        E.flush(e, "sG")
        if E.on("sync"):
            e.dma_start(out=bqk_sb[:, 0:2], in_=bq_pp[:, :]).then_inc(sems["sQ"], 16)
            e.dma_start(out=bqk_sb[:, 2:4], in_=bk_pp[:, :]).then_inc(sems["sQ"], 16)
        E.inc("sQ", 32)
        E.flush(e, "sQ")
        gMM = E.guard()
        gQKe = E.guard()
        gVe = E.guard()
        gQKSt = E.guard()
        gVSt = E.guard()
        for i in range(NTC):
            owner, trow = i // 2, (i % 2) * TC
            buf = xn_mov0 if i % 2 == 0 else xn_mov1
            if E.on("sync"):
                e.wait_ge(sems["sCC"], cc0)
                gMM.wait(e, i - 2)
                e.dma_start(out=buf[:, :].rearrange("p (f t) -> p f t", t=TC),
                            in_=h_xn3_all[owner, :, :, trow:trow + TC]
                            .transpose([1, 0, 2])).then_inc(sems["sQ"], 16)
            E.inc("sQ", 16)
            E.flush(e, "sQ")
            if E.on("tensor"):
                E.dma_wait(e, "sQ")
                E.dma_wait(e, "sG")
            for qk in range(2):
                wsb = wi_h_sb if qk == 0 else wi_g_sb
                for m in range(2):
                    pb = psA if (qk * 2 + m) % 2 == 0 else psB
                    if E.on("tensor"):
                        gQKe.wait(e, 4 * i + qk * 2 + m - 2)
                        for k in range(16):
                            last = e.matmul(pb[:, :],
                                            wsb[:, k * 256 + m * 128: k * 256 + (m + 1) * 128],
                                            buf[:, k * TC:(k + 1) * TC],
                                            start=(k == 0), stop=(k == 15))
                        last.then_inc(sems["sT"], 1)
                    E.inc("sT", 1)
                    mm_now = E.val("sT")
                    qki = len(gQKSt.marks)
                    qkb = c_qkev if qki % 2 == 0 else c_qkev2
                    lqs = "sQ2" if qki % 2 == 0 else "sQ3"
                    if E.on("vector"):
                        e.wait_ge(sems["sT"], mm_now)
                        gQKSt.wait(e, qki - 2)
                        e.tensor_scalar(out=qkb[:, :], in0=pb[:, :],
                                        scalar1=bqk_sb[:, qk * 2 + m: qk * 2 + m + 1],
                                        scalar2=None, op0=ALU.add)
                        e.drain()
                        e.sem_inc(sems["sV"], 1)
                    E.inc("sV", 1)
                    gQKe.mark("sV")
                    ev_now = E.val("sV")
                    if E.on("sync"):
                        e.wait_ge(sems["sV"], ev_now)
                        E.lane_open(e, lqs)
                        e.dma_start(out=h_qkT[qk, m, :, i * TC:(i + 1) * TC],
                                    in_=qkb[:, :]).then_inc(sems[lqs], 16)
                    E.inc(lqs, 16)
                    gQKSt.mark(lqs)
            for tc4 in range(4):
                pb = psC if tc4 % 2 == 0 else psD
                if E.on("tensor"):
                    gVe.wait(e, 4 * i + tc4 - 2)
                    for k in range(16):
                        e.matmul(pb[:, 0:256],
                                 buf[:, k * TC + tc4 * 128: k * TC + (tc4 + 1) * 128],
                                 c_wv[:, k * 256:(k + 1) * 256],
                                 start=(k == 0), stop=False)
                    e.matmul(pb[:, 0:256], ones_sb[:, :], bv_sb[:, :],
                             start=False, stop=True).then_inc(sems["sT"], 1)
                E.inc("sT", 1)
                mm_now = E.val("sT")
                vi = len(gVSt.marks)
                vb = c_vev if vi % 2 == 0 else c_vev2
                lqs = "sQ2" if vi % 2 == 0 else "sQ3"
                if E.on("vector"):
                    e.wait_ge(sems["sT"], mm_now)
                    gVSt.wait(e, vi - 2)
                    e.tensor_copy(vb[:, 0:256], pb[:, 0:256])
                    e.drain()
                    e.sem_inc(sems["sV"], 1)
                E.inc("sV", 1)
                gVe.mark("sV")
                ev_now = E.val("sV")
                if E.on("sync"):
                    e.wait_ge(sems["sV"], ev_now)
                    E.lane_open(e, lqs)
                    e.dma_start(out=h_v[:, i * TC + tc4 * 128: i * TC + (tc4 + 1) * 128, :]
                                .transpose([1, 0, 2]),
                                in_=vb[:, 0:256].rearrange("p (u c) -> p u c", c=HD)
                                ).then_inc(sems[lqs], 16)
                E.inc(lqs, 16)
                gVSt.mark(lqs)
            gMM.mark("sT")
        qkv_done_q = E.val("sQ")
        qkv_done_q2 = E.val("sQ2")
        qkv_done_q3 = E.val("sQ3")
        gSe = E.guard()
        gSmm = E.guard()
        gPX = E.guard()
        gPTe = E.guard()
        gOmm = E.guard()
        gOev = E.guard()
        sbanks = [psA, psB, psC, psD]
        NBLK = [((qc + 1) * 128 + 511) // 512 for qc in range(32)]
        CUM = [sum(NBLK[:qc]) for qc in range(32)]
        UB = sum(NBLK)
        smm_idx = {}

        def emit_smm(u, qc, blk):
            kpref = (qc + 1) * 128
            n = min(512, kpref - blk * 512)
            j = u * UB + CUM[qc] + blk
            pb = sbanks[j % 4]
            uqt = c_qt if u % 2 == 0 else c_qt1
            ukt = c_kt if u % 2 == 0 else c_kt1
            if E.on("tensor"):
                if qc == 0 and blk == 0:
                    E.dma_wait(e, "sQ")
                gSe.wait(e, j - 4)
                e.matmul(pb[:, 0:n], uqt[:, qc * 128:(qc + 1) * 128],
                         ukt[:, blk * 512: blk * 512 + n],
                         start=True, stop=True).then_inc(sems["sT"], 1)
            E.inc("sT", 1)
            gSmm.mark("sT")
            smm_idx[j] = len(gSmm.marks) - 1

        for b in range(2):
            for kh in range(2):
                u = b * 2 + kh
                uqt = c_qt if u % 2 == 0 else c_qt1
                ukt = c_kt if u % 2 == 0 else c_kt1
                if E.on("sync"):
                    e.wait_ge(sems["sQ"], qkv_done_q)
                    e.wait_ge(sems["sQ2"], qkv_done_q2)
                    e.wait_ge(sems["sQ3"], qkv_done_q3)
                    gOmm.wait(e, 32 * (u - 1) - 1)
                    e.dma_start(out=uqt[:, :], in_=h_qkT[0, kh, :, b * 4096:(b + 1) * 4096]
                                ).then_inc(sems["sQ"], 16)
                    e.dma_start(out=ukt[:, :], in_=h_qkT[1, kh, :, b * 4096:(b + 1) * 4096]
                                ).then_inc(sems["sQ"], 16)
                    gOmm.wait(e, 32 * u - 1)
                    e.dma_start(out=c_v[:, :].rearrange("p (kc c) -> p kc c", c=HD),
                                in_=h_v[kh, b * 4096:(b + 1) * 4096, :]
                                .rearrange("(kc p) c -> p kc c", p=128)).then_inc(sems["sQ"], 16)
                E.inc("sQ", 48)
                E.flush(e, "sQ")
                for blk in range(min(4, NBLK[0])):
                    emit_smm(u, 0, blk)
                for it in range(33):
                    if it < 32:
                        qc = it
                        kpref = (qc + 1) * 128
                        nblk = NBLK[qc]
                        sb = 48 + 4 * (qc % 2)
                        cpb = c_p if qc % 2 == 0 else c_p1
                        for blk in range(nblk):
                            n = min(512, kpref - blk * 512)
                            j = u * UB + CUM[qc] + blk
                            pb = sbanks[j % 4]
                            if E.on("vector"):
                                gSmm.wait(e, smm_idx[j])
                                if blk == 0:
                                    gPX.wait(e, len(gPX.marks) - 1)
                                e.tensor_copy(c_srow[:, blk * 512: blk * 512 + n], pb[:, 0:n])
                                e.drain()
                                e.sem_inc(sems["sV"], 1)
                            E.inc("sV", 1)
                            gSe.mark("sV")
                            if blk + 4 < nblk:
                                emit_smm(u, qc, blk + 4)
                        if qc + 1 < 32:
                            for blk in range(min(4, NBLK[qc + 1])):
                                emit_smm(u, qc + 1, blk)
                        if E.on("vector"):
                            dg = c_srow[:, qc * 128:(qc + 1) * 128]
                            e.tensor_tensor(out=dg, in0=dg, in1=mask_sb[:, :], op=ALU.add)
                            e.drain()
                            e.tensor_reduce(out=stat_sb[:, sb:sb + 1], in_=c_srow[:, 0:kpref],
                                            axis=AX.X, op=ALU.max, negate=True)
                            e.drain()
                            e.tensor_scalar(out=stat_sb[:, sb + 1:sb + 2], in0=stat_sb[:, sb:sb + 1],
                                            scalar1=ATT_SCALE, scalar2=None, op0=ALU.mult)
                            e.drain()
                            e.sem_inc(sems["sV"], 1)
                        E.inc("sV", 1)
                        mx_now = E.val("sV")
                        if E.on("scalar"):
                            e.wait_ge(sems["sV"], mx_now)
                            e.activation(out=cpb[:, 0:kpref], in_=c_srow[:, 0:kpref], func=AF.Exp,
                                         scale=K_ATTS, bias=stat_sb[:, sb + 1:sb + 2],
                                         accum_out=stat_sb[:, sb + 2:sb + 3]).then_inc(sems["sA"], 1)
                        E.inc("sA", 1)
                        gPX.mark("sA")
                    if it >= 1:
                        r = it - 1
                        rb = 48 + 4 * (r % 2)
                        rpb = c_p if r % 2 == 0 else c_p1
                        px_r = gPX.marks[32 * u + r][1]
                        if E.on("vector"):
                            e.wait_ge(sems["sA"], px_r)
                            e.reciprocal(out=stat_sb[:, rb + 3:rb + 4], in_=stat_sb[:, rb + 2:rb + 3])
                            e.drain()
                            e.sem_inc(sems["sV"], 1)
                        E.inc("sV", 1)
                        nkc = r + 1
                        for kcg in range((nkc + 7) // 8):
                            pbank = psE if kcg % 2 == 0 else psF
                            kchi = min(8, nkc - kcg * 8)
                            if E.on("tensor"):
                                gPX.wait(e, 32 * u + r)
                                gPTe.wait(e, len(gPTe.marks) - 2)
                                for k8 in range(kchi):
                                    kc = kcg * 8 + k8
                                    last = e.transpose(pbank[:, k8 * 128:(k8 + 1) * 128],
                                                       rpb[:, kc * 128:(kc + 1) * 128], ident[:, :])
                                last.then_inc(sems["sT"], 1)
                            E.inc("sT", 1)
                            tp_now = E.val("sT")
                            if E.on("vector"):
                                e.wait_ge(sems["sT"], tp_now)
                                e.tensor_copy(c_pt[:, kcg * 1024: kcg * 1024 + kchi * 128],
                                              pbank[:, 0:kchi * 128])
                                e.drain()
                                e.sem_inc(sems["sV"], 1)
                            E.inc("sV", 1)
                            gPTe.mark("sV")
                        pt_done = E.val("sV")
                        if E.on("tensor"):
                            e.wait_ge(sems["sV"], pt_done)
                            gOev.wait(e, len(gOev.marks) - 1)
                            for kc in range(nkc):
                                last = e.matmul(psG[:, 0:HD], c_pt[:, kc * 128:(kc + 1) * 128],
                                                c_v[:, kc * 128:(kc + 1) * 128],
                                                start=(kc == 0), stop=(kc == nkc - 1))
                            last.then_inc(sems["sT"], 1)
                        E.inc("sT", 1)
                        gOmm.mark("sT")
                        omm = E.val("sT")
                        if E.on("vector"):
                            e.wait_ge(sems["sT"], omm)
                            if r % 8 == 0:
                                E.dma_wait(e, "sQ")
                            e.tensor_scalar(out=c_osb[:, (r % 8) * 128:(r % 8 + 1) * 128],
                                            in0=psG[:, 0:HD], scalar1=stat_sb[:, rb + 3:rb + 4],
                                            scalar2=None, op0=ALU.mult)
                            e.drain()
                            e.sem_inc(sems["sV"], 1)
                        E.inc("sV", 1)
                        gOev.mark("sV")
                        oev_now = E.val("sV")
                        if r % 8 == 7:
                            if E.on("sync"):
                                e.wait_ge(sems["sV"], oev_now)
                                e.dma_start(out=h_o_stage[b * 4 + r // 8, kh, :, :]
                                            .rearrange("(qs p) c -> p qs c", p=128),
                                            in_=c_osb[:, :].rearrange("p (qs c) -> p qs c", c=HD)
                                            ).then_inc(sems["sQ"], 16)
                            E.inc("sQ", 16)
                            E.flush(e, "sQ")
        att_st = E.val("sQ")
        if E.on("sync"):
            e.wait_ge(sems["sQ"], att_st)
            e.sem_inc(sems["sA2"], 1)
        E.inc("sA2", 1)
        flush_now = E.val("sA2")
        if E.on("gpsimd"):
            e.wait_ge(sems["sA2"], flush_now)
            with allow_ici_all_to_all():
                e.collective_compute(
                    "AllToAll", ALU.bypass, replica_groups=[list(range(NC))],
                    ins=[h_o_stage.opt()], outs=[h_o_recv.opt()],
                ).then_inc(sems["sCC"], 1)
        E.inc("sCC", 1)
        cc1 = E.val("sCC")
        gRa = E.guard()
        gAl = E.guard()
        for tt in range(8):
            att_b = c_att if tt % 2 == 0 else c_att1
            lqs = "sQ2" if tt % 2 == 0 else "sQ3"
            if E.on("sync"):
                e.wait_ge(sems["sCC"], cc1)
                gRa.wait(e, tt - 2)
                E.lane_open(e, lqs)
                e.dma_start(out=att_b[:, :].rearrange("p (s k c) -> p s k c", k=2, c=HD),
                            in_=h_o_recv[:, :, tt * 128:(tt + 1) * 128, :]
                            .transpose([2, 0, 1, 3])).then_inc(sems[lqs], 16)
            E.inc(lqs, 16)
            gAl.mark(lqs)
            if E.on("vector"):
                gAl.wait(e, tt)
                e.tensor_tensor(out=res[:, tt * D:(tt + 1) * D], in0=res[:, tt * D:(tt + 1) * D],
                                in1=att_b[:, :], op=ALU.add)
                e.drain()
                e.sem_inc(sems["sV"], 1)
            E.inc("sV", 1)
            gRa.mark("sV")
        r_now = E.val("sV")
        for nm in ("sync", "gpsimd", "scalar", "tensor"):
            if E.on(nm):
                e.wait_ge(sems["sV"], r_now)


    def ph_d(E):
        e = E.eng
        if E.on("scalar"):
            e.wait_ge(sems["sV"], E.val("sV"))
            E.dma_wait(e, "sQ")
            E.dma_wait(e, "sG")
            for tt in range(8):
                if tt:
                    e.drain()
                e.activation(out=ln_sq[:, :], in_=res[:, tt * D:(tt + 1) * D],
                             func=AF.Square, accum_out=ln_s2[:, tt:tt + 1])
            e.drain()
            e.sem_inc(sems["sA"], 1)
        E.inc("sA", 1)
        sq_done = E.val("sA")
        if E.on("vector"):
            for tt in range(8):
                e.tensor_reduce(out=ln_s1[:, tt:tt + 1], in_=res[:, tt * D:(tt + 1) * D],
                                axis=AX.X, op=ALU.add)
            e.drain()
            e.wait_ge(sems["sA"], sq_done)
            e.tensor_scalar(out=ln_mean[:, :], in0=ln_s1[:, :], scalar1=1.0 / D,
                            scalar2=None, op0=ALU.mult)
            e.tensor_scalar(out=ln_tmp[:, :], in0=ln_s2[:, :], scalar1=1.0 / D,
                            scalar2=None, op0=ALU.mult)
            e.drain()
            e.tensor_tensor(out=ln_s2[:, :], in0=ln_mean[:, :], in1=ln_mean[:, :], op=ALU.mult)
            e.drain()
            e.tensor_tensor(out=ln_tmp[:, :], in0=ln_tmp[:, :], in1=ln_s2[:, :], op=ALU.subtract)
            e.drain()
            e.sem_inc(sems["sV"], 1)
        E.inc("sV", 1)
        var_done = E.val("sV")
        if E.on("scalar"):
            e.wait_ge(sems["sV"], var_done)
            e.activation(out=ln_tmp[:, :], in_=ln_tmp[:, :], func=AF.Sqrt,
                         bias=K_LNEPS).then_inc(sems["sA"], 1)
        E.inc("sA", 1)
        std_done = E.val("sA")
        first_v = True
        gN = E.guard()
        gT = E.guard()
        gEv = E.guard()
        q1 = fv(arB, 0, 2048)
        for tt in range(8):
            lx = ln_xn0 if tt % 2 == 0 else ln_xn1
            if E.on("vector"):
                if first_v:
                    e.wait_ge(sems["sA"], std_done)
                    e.reciprocal(out=ln_rstd[:, :], in_=ln_tmp[:, :])
                    e.drain()
                    first_v = False
                gT.wait(e, tt - 2)
                e.tensor_scalar(out=lx[:, :], in0=res[:, tt * D:(tt + 1) * D],
                                scalar1=ln_mean[:, tt:tt + 1], scalar2=ln_rstd[:, tt:tt + 1],
                                op0=ALU.subtract, op1=ALU.mult)
                e.drain()
                e.tensor_reduce(out=d_amax[:, :], in_=lx[:, :], axis=AX.X, op=ALU.max,
                                apply_absolute_value=True)
                e.drain()
                e.tensor_scalar(out=d_amax[:, :], in0=d_amax[:, :], scalar1=1e-5,
                                scalar2=None, op0=ALU.max)
                e.drain()
                e.reciprocal(out=d_scal[:, :], in_=d_amax[:, :])
                e.drain()
                e.tensor_scalar(out=d_scal[:, :], in0=d_scal[:, :], scalar1=128.0,
                                scalar2=None, op0=ALU.mult)
                e.tensor_scalar(out=d_inv2[:, :], in0=d_amax[:, :], scalar1=1.0 / 128.0,
                                scalar2=None, op0=ALU.mult)
                e.drain()
                e.tensor_scalar(out=q1[:, :], in0=lx[:, :], scalar1=d_scal[:, 0:1],
                                scalar2=None, op0=ALU.mult)
                e.drain()
                e.tensor_scalar(out=q1[:, :], in0=q1[:, :], scalar1=MAGIC,
                                scalar2=-MAGIC, op0=ALU.add, op1=ALU.add)
                e.drain()
                e.tensor_scalar(out=q1[:, :], in0=q1[:, :], scalar1=127.0,
                                scalar2=-128.0, op0=ALU.min, op1=ALU.max)
                e.drain()
                e.tensor_scalar(out=lx[:, :], in0=q1[:, :], scalar1=d_inv2[:, 0:1],
                                scalar2=None, op0=ALU.mult)
                e.drain()
                e.sem_inc(sems["sV"], 1)
            E.inc("sV", 1)
            gN.mark("sV")
            if E.on("tensor"):
                gN.wait(e, tt)
                gEv.wait(e, tt - 1)
                for g in range(2):
                    pbank = psE if g == 0 else psF
                    for f8 in range(8):
                        f = g * 8 + f8
                        last = e.transpose(pbank[:, f8 * 128:(f8 + 1) * 128],
                                           lx[:, f * 128:(f + 1) * 128], ident[:, :])
                last.then_inc(sems["sT"], 1)
            E.inc("sT", 1)
            gT.mark("sT")
            if E.on("vector"):
                gT.wait(e, tt)
                for g in range(2):
                    pbank = psE if g == 0 else psF
                    for f8 in range(8):
                        f = g * 8 + f8
                        e.tensor_copy(xnT_sb[:, f * TOK + tt * 128: f * TOK + (tt + 1) * 128],
                                      pbank[:, f8 * 128:(f8 + 1) * 128])
                e.drain()
                e.sem_inc(sems["sV2"], 1)
            E.inc("sV2", 1)
            gEv.mark("sV2")
        aq_done = E.val("sV2")
        if E.on("sync"):
            e.wait_ge(sems["sV2"], aq_done)
            e.dma_start(out=h_a_own[:, :, :].transpose([1, 0, 2]),
                        in_=xnT_sb[:, :].rearrange("p (f t) -> p f t", t=TOK)).then_inc(sems["sQ"], 16)
            e.wait_ge(sems["sQ"], E.val("sQ") + 16)
            e.sem_inc(sems["sA2"], 1)
        E.inc("sQ", 16)
        E.flush(e, "sQ")
        E.inc("sA2", 1)
        fla = E.val("sA2")
        if E.on("gpsimd"):
            e.wait_ge(sems["sA2"], fla)
            e.collective_compute(
                "AllGather", ALU.bypass, replica_groups=[list(range(NC))],
                ins=[h_a_own.opt()], outs=[h_a_all.opt()],
            ).then_inc(sems["sCC"], 1)
        E.inc("sCC", 1)
        cc_a = E.val("sCC")
        gWL = E.guard()
        gMM = E.guard()
        gGU = E.guard()
        gSilu = E.guard()
        gFuSt = E.guard()
        for ih in range(2):
            if E.on("sync"):
                e.wait_ge(sems["sCC"], cc_a)
                gMM.wait(e, 16 * ih - 1)
                e.dma_start(out=wi_h_sb[:, :].rearrange("p (k c) -> p k c", c=512),
                            in_=h_wgT[:, ih * 512:(ih + 1) * 512]
                            .rearrange("(k p) c -> p k c", p=128)).then_inc(sems["sQ"], 16)
                e.dma_start(out=wi_g_sb[:, :].rearrange("p (k c) -> p k c", c=512),
                            in_=h_wuT[:, ih * 512:(ih + 1) * 512]
                            .rearrange("(k p) c -> p k c", p=128)).then_inc(sems["sQ"], 16)
            E.inc("sQ", 32)
            E.flush(e, "sQ")
            gWL.mark("sQ")
            for i in range(NTC):
                owner, trow = i // 2, (i % 2) * TC
                buf = xn_mov0 if i % 2 == 0 else xn_mov1
                if E.on("sync"):
                    gMM.wait(e, 16 * ih + i - 2)
                    e.dma_start(out=buf[:, :].rearrange("p (f t) -> p f t", t=TC),
                                in_=h_a_all[owner, :, :, trow:trow + TC]
                                .transpose([1, 0, 2])).then_inc(sems["sQ"], 16)
                E.inc("sQ", 16)
                E.flush(e, "sQ")
                for m in range(4):
                    icb = ih * 4 + m
                    pbg = psA if m % 2 == 0 else psB
                    pbu = psC if m % 2 == 0 else psD
                    if E.on("tensor"):
                        if m == 0:
                            E.dma_wait(e, "sQ")
                        gSilu.wait(e, len(gSilu.marks) - 2)
                        for k in range(16):
                            last = e.matmul(pbg[:, :],
                                            wi_h_sb[:, k * 512 + m * 128: k * 512 + (m + 1) * 128],
                                            buf[:, k * TC:(k + 1) * TC],
                                            start=(k == 0), stop=(k == 15))
                        last.then_inc(sems["sT"], 1)
                    E.inc("sT", 1)
                    gmm = E.val("sT")
                    if E.on("tensor"):
                        gGU.wait(e, len(gGU.marks) - 2)
                        for k in range(16):
                            last = e.matmul(pbu[:, :],
                                            wi_g_sb[:, k * 512 + m * 128: k * 512 + (m + 1) * 128],
                                            buf[:, k * TC:(k + 1) * TC],
                                            start=(k == 0), stop=(k == 15))
                        last.then_inc(sems["sT"], 1)
                    E.inc("sT", 1)
                    umm = E.val("sT")
                    if E.on("scalar"):
                        e.wait_ge(sems["sT"], gmm)
                        gGU.wait(e, len(gGU.marks) - 1)
                        e.activation(out=b_gs[:, :], in_=pbg[:, :], func=AF.Silu,
                                     scale=mbc[:, 0:1])
                        e.drain()
                        e.sem_inc(sems["sA"], 1)
                    E.inc("sA", 1)
                    gSilu.mark("sA")
                    silu_now = E.val("sA")
                    fidx = len(gFuSt.marks)
                    tbb = b_tb if fidx % 2 == 0 else b_tb2
                    lqs = "sQ2" if fidx % 2 == 0 else "sQ3"
                    if E.on("vector"):
                        e.wait_ge(sems["sT"], umm)
                        e.wait_ge(sems["sA"], silu_now)
                        gFuSt.wait(e, fidx - 2)
                        e.tensor_scalar(out=b_us[:, :], in0=pbu[:, :],
                                        scalar1=mbc[:, 1:2], scalar2=None, op0=ALU.mult)
                        e.drain()
                        e.tensor_tensor(out=tbb[:, :], in0=b_gs[:, :], in1=b_us[:, :],
                                        op=ALU.mult)
                        e.drain()
                        e.sem_inc(sems["sV"], 1)
                    E.inc("sV", 1)
                    gGU.mark("sV")
                    fu_now = E.val("sV")
                    if E.on("sync"):
                        e.wait_ge(sems["sV"], fu_now)
                        E.lane_open(e, lqs)
                        e.dma_start(out=h_fu_stage[owner, icb, :, trow:trow + TC],
                                    in_=tbb[:, :]).then_inc(sems[lqs], 16)
                    E.inc(lqs, 16)
                    gFuSt.mark(lqs)
                gMM.mark("sT")
        if E.on("sync"):
            e.wait_ge(sems["sQ"], E.val("sQ"))
            e.wait_ge(sems["sQ2"], E.val("sQ2"))
            e.wait_ge(sems["sQ3"], E.val("sQ3"))
            e.sem_inc(sems["sA2"], 1)
        E.inc("sA2", 1)
        flf = E.val("sA2")
        if E.on("gpsimd"):
            e.wait_ge(sems["sA2"], flf)
            with allow_ici_all_to_all():
                e.collective_compute(
                    "AllToAll", ALU.bypass, replica_groups=[list(range(NC))],
                    ins=[h_fu_stage.opt()], outs=[h_fu_recv.opt()],
                ).then_inc(sems["sCC"], 1)
        E.inc("sCC", 1)
        cc_f = E.val("sCC")
        gWI = E.guard()
        gH1 = E.guard()
        gH1t = E.guard()
        gHtm = E.guard()
        gBf = E.guard()
        gA2t = E.guard()
        gA2e = E.guard()
        for tt in range(8):
            if E.on("sync"):
                e.wait_ge(sems["sCC"], cc_f)
                gA2t.wait(e, 8 * tt - 1)
                e.dma_start(out=d_win[:, :].rearrange("p (q c) -> p q c", c=128),
                            in_=h_fu_recv[:, :, :, tt * 128:(tt + 1) * 128]
                            .rearrange("s b p c -> p (s b) c")).then_inc(sems["sQ"], 16)
            E.inc("sQ", 16)
            E.flush(e, "sQ")
            gWI.mark("sQ")
            for g4 in range(16):
                pb = psA if g4 % 2 == 0 else psB
                if E.on("tensor"):
                    if g4 == 0:
                        E.dma_wait(e, "sQ")
                        gBf.wait(e, tt - 1)
                    gH1.wait(e, 16 * tt + g4 - 2)
                    for q4 in range(4):
                        pp = g4 * 4 + q4
                        last = e.matmul(pb[:, q4 * 128:(q4 + 1) * 128], h128_sb[:, :],
                                        d_win[:, pp * 128:(pp + 1) * 128],
                                        start=True, stop=True)
                    last.then_inc(sems["sT"], 1)
                E.inc("sT", 1)
                h1_now = E.val("sT")
                if E.on("vector"):
                    e.wait_ge(sems["sT"], h1_now)
                    gH1t.wait(e, 16 * tt + g4 - 1)
                    e.tensor_copy(b_tb[:, :], pb[:, :])
                    e.drain()
                    e.sem_inc(sems["sV"], 1)
                E.inc("sV", 1)
                gH1.mark("sV")
                h1e = E.val("sV")
                if E.on("tensor"):
                    e.wait_ge(sems["sV"], h1e)
                    gHtm.wait(e, 16 * tt + g4 - 2)
                    pbank = psE if g4 % 2 == 0 else psF
                    for q4 in range(4):
                        last = e.transpose(pbank[:, q4 * 128:(q4 + 1) * 128],
                                           b_tb[:, q4 * 128:(q4 + 1) * 128], ident[:, :])
                    last.then_inc(sems["sT"], 1)
                E.inc("sT", 1)
                gH1t.mark("sT")
                tp_now = E.val("sT")
                if E.on("vector"):
                    e.wait_ge(sems["sT"], tp_now)
                    pbank = psE if g4 % 2 == 0 else psF
                    e.tensor_copy(d_wtm[:, g4 * 512:(g4 + 1) * 512], pbank[:, 0:512])
                    e.drain()
                    e.sem_inc(sems["sV"], 1)
                E.inc("sV", 1)
                gHtm.mark("sV")
            if E.on("vector"):
                cur, oth = d_wtm, d_bf0
                for st in range(6):
                    s_el = 128 * (1 << st)
                    ng = 8192 // (2 * s_el)
                    a = cur[:, :].rearrange("p (g two s) -> p g two s", two=2, s=s_el)
                    o = oth[:, :].rearrange("p (g two s) -> p g two s", two=2, s=s_el)
                    e.tensor_tensor(out=o[:, :, 0, :], in0=a[:, :, 0, :],
                                    in1=a[:, :, 1, :], op=ALU.add)
                    e.tensor_tensor(out=o[:, :, 1, :], in0=a[:, :, 0, :],
                                    in1=a[:, :, 1, :], op=ALU.subtract)
                    e.drain()
                    cur, oth = oth, cur
                e.tensor_reduce(out=d_amax[:, :], in_=cur[:, :], axis=AX.X, op=ALU.max,
                                apply_absolute_value=True)
                e.drain()
                e.tensor_scalar(out=d_amax[:, :], in0=d_amax[:, :],
                                scalar1=float(1.0 / np.sqrt(np.float32(DFF))),
                                scalar2=1e-5, op0=ALU.mult, op1=ALU.max)
                e.drain()
                e.reciprocal(out=d_scal[:, :], in_=d_amax[:, :])
                e.drain()
                e.tensor_scalar(out=d_scal[:, :], in0=d_scal[:, :],
                                scalar1=float(128.0 / np.sqrt(np.float32(DFF))),
                                scalar2=None, op0=ALU.mult)
                e.tensor_scalar(out=d_inv2[:, :], in0=d_amax[:, :], scalar1=1.0 / 128.0,
                                scalar2=None, op0=ALU.mult)
                e.drain()
                e.tensor_scalar(out=d_qf[:, :], in0=cur[:, :], scalar1=d_scal[:, 0:1],
                                scalar2=None, op0=ALU.mult)
                e.drain()
                e.tensor_scalar(out=d_qf[:, :], in0=d_qf[:, :], scalar1=MAGIC,
                                scalar2=-MAGIC, op0=ALU.add, op1=ALU.add)
                e.drain()
                e.tensor_scalar(out=d_qf[:, :], in0=d_qf[:, :], scalar1=127.0,
                                scalar2=-128.0, op0=ALU.min, op1=ALU.max)
                e.drain()
                e.tensor_scalar(out=d_a2b[:, :], in0=d_qf[:, :], scalar1=d_inv2[:, 0:1],
                                scalar2=None, op0=ALU.mult)
                e.drain()
                e.sem_inc(sems["sV"], 1)
            E.inc("sV", 1)
            gBf.mark("sV")
            bf_now = E.val("sV")
            for g8 in range(8):
                pbank = psE if g8 % 2 == 0 else psF
                if E.on("tensor"):
                    if g8 == 0:
                        e.wait_ge(sems["sV"], bf_now)
                    gA2e.wait(e, len(gA2e.marks) - 2)
                    for q8 in range(8):
                        pp = g8 * 8 + q8
                        last = e.transpose(pbank[:, q8 * 128:(q8 + 1) * 128],
                                           d_a2b[:, pp * 128:(pp + 1) * 128], ident[:, :])
                    last.then_inc(sems["sT"], 1)
                E.inc("sT", 1)
                gA2t.mark("sT")
                tq_now = E.val("sT")
                if E.on("vector"):
                    e.wait_ge(sems["sT"], tq_now)
                    if g8 == 0:
                        E.dma_wait(e, "sQ")
                    e.tensor_copy(d_a2t[:, g8 * 1024:(g8 + 1) * 1024], pbank[:, :])
                    e.drain()
                    e.sem_inc(sems["sV"], 1)
                E.inc("sV", 1)
                gA2e.mark("sV")
            a2_now = E.val("sV")
            if E.on("sync"):
                e.wait_ge(sems["sV"], a2_now)
                e.dma_start(out=h_a2[:, :, tt * 128:(tt + 1) * 128].transpose([1, 0, 2]),
                            in_=d_a2t[:, :].rearrange("p (q c) -> p q c", c=128)
                            ).then_inc(sems["sQ"], 16)
            E.inc("sQ", 16)
            E.flush(e, "sQ")
        a2_all_st = E.val("sQ")
        gDL = E.guard()
        gDmm = E.guard()
        gDacc = E.guard()
        gZ2 = E.guard()
        gTpO = E.guard()
        gRes = E.guard()
        oi = 0
        for h2 in range(2):
            if E.on("vector"):
                gRes.wait(e, 16 * h2 - 1)
                e.memset(b_acc[:, :], 0.0)
                e.drain()
                e.sem_inc(sems["sV"], 1)
            E.inc("sV", 1)
            gZ2.mark("sV")
            for icg in range(16):
                wd = d_wd0 if icg % 2 == 0 else d_wd1
                d_ast = d_ast0 if icg % 2 == 0 else d_ast1
                if E.on("sync"):
                    if h2 == 0 and icg == 0:
                        e.wait_ge(sems["sQ"], a2_all_st)
                    gDmm.wait(e, (16 * h2 + icg - 2) * 16 + 15)
                    e.dma_start(out=d_ast[:, :].rearrange("p (ic c) -> p ic c", c=512),
                                in_=h_a2[icg * 4:(icg + 1) * 4, :, h2 * 512:(h2 + 1) * 512]
                                .transpose([1, 0, 2])).then_inc(sems["sQ"], 16)
                    e.dma_start(out=wd[:, :].rearrange("p (ic c) -> p ic c", c=D),
                                in_=h_wdT.rearrange("s q c -> (s q) c")[icg * 512:(icg + 1) * 512, :]
                                .rearrange("(ic p) c -> p ic c", p=128)).then_inc(sems["sQ"], 16)
                E.inc("sQ", 32)
                E.flush(e, "sQ")
                gDL.mark("sQ")
                for oc in range(16):
                    pb = psG if oc % 2 == 0 else psH
                    if E.on("tensor"):
                        if oc == 0:
                            E.dma_wait(e, "sQ")
                            if icg == 0:
                                gZ2.wait(e, h2)
                        gDacc.wait(e, len(gDacc.marks) - 2)
                        for ic in range(4):
                            last = e.matmul(pb[:, :],
                                            wd[:, ic * D + oc * 128: ic * D + (oc + 1) * 128],
                                            d_ast[:, ic * 512:(ic + 1) * 512],
                                            start=(ic == 0), stop=(ic == 3))
                        last.then_inc(sems["sT"], 1)
                    E.inc("sT", 1)
                    gDmm.mark("sT")
                    dmm = E.val("sT")
                    if E.on("vector"):
                        e.wait_ge(sems["sT"], dmm)
                        if icg == 0 and oc == 0:
                            gZ2.wait(e, h2)
                        dst = b_acc[:, oc * 512:(oc + 1) * 512]
                        e.tensor_tensor(out=dst, in0=dst, in1=pb[:, :], op=ALU.add)
                        e.drain()
                        e.sem_inc(sems["sV"], 1)
                    E.inc("sV", 1)
                    gDacc.mark("sV")
            for oc in range(16):
                if E.on("vector"):
                    gTpO.wait(e, oi - 1)
                    e.tensor_scalar(out=b_tb[:, 0:512], in0=b_acc[:, oc * 512:(oc + 1) * 512],
                                    scalar1=mbc[:, 2:3], scalar2=None, op0=ALU.mult)
                    e.drain()
                    e.sem_inc(sems["sV"], 1)
                E.inc("sV", 1)
                prep_now = E.val("sV")
                if E.on("tensor"):
                    e.wait_ge(sems["sV"], prep_now)
                    gRes.wait(e, oi - 1)
                    for tc4 in range(4):
                        last = e.transpose(psE[:, tc4 * 128:(tc4 + 1) * 128],
                                           b_tb[:, tc4 * 128:(tc4 + 1) * 128], ident[:, :])
                    last.then_inc(sems["sT"], 1)
                E.inc("sT", 1)
                gTpO.mark("sT")
                tp_now = E.val("sT")
                if E.on("vector"):
                    e.wait_ge(sems["sT"], tp_now)
                    for tc4 in range(4):
                        tt_g = h2 * 4 + tc4
                        dst = res[:, tt_g * D + oc * 128: tt_g * D + (oc + 1) * 128]
                        e.tensor_tensor(out=dst, in0=dst,
                                        in1=psE[:, tc4 * 128:(tc4 + 1) * 128], op=ALU.add)
                    e.drain()
                    e.sem_inc(sems["sV"], 1)
                E.inc("sV", 1)
                gRes.mark("sV")
                oi += 1
        r_now = E.val("sV")
        for nm in ("sync", "gpsimd", "scalar", "tensor"):
            if E.on(nm):
                e.wait_ge(sems["sV"], r_now)

    def ph_finish(E):
        e = E.eng
        v_now = E.val("sV")
        dbg_srcs = {"d_xnT": h_xnT_all, "d_hsT": h_hsT, "d_sgT": h_sgT,
                    "d_ssmp": h_ssmp_recv, "d_nrm": h_nrm_all, "d_yg": h_yg_recv}
        if "d_op" in dbg_t:
            if E.on("sync"):
                e.wait_ge(sems["sV"], v_now)
                e.dma_start(out=dbg_t["d_op"].opt(), in_=s_sc[:, :].opt()).then_inc(sems["sQ"], 16)
            E.inc("sQ", 16)
            E.flush(e, "sQ")
        for nm, hsrc in dbg_srcs.items():
            if nm in dbg_t:
                if E.on("sync"):
                    e.dma_start(out=dbg_t[nm].opt(), in_=hsrc.opt()).then_inc(sems["sQ"], 16)
                E.inc("sQ", 16)
                E.flush(e, "sQ")
        if E.on("sync"):
            if "d_x1" in dbg_t:
                e.dma_start(out=dbg_t["d_x1"].rearrange("(t p) d -> p t d", p=128),
                            in_=res[:, :].rearrange("p (t d) -> p t d", d=D)).then_inc(sems["sQ"], 16)
                E.inc("sQ", 16)
                E.flush(e, "sQ")
            e.dma_start(out=out_own.rearrange("(t p) d -> p t d", p=128),
                        in_=res[:, :].rearrange("p (t d) -> p t d", d=D)).then_inc(sems["sQ"], 16)
            E.inc("sQ", 16)
            E.flush(e, "sQ")
            e.wait_ge(sems["sQ"], E.val("sQ"))
        else:
            if "d_x1" in dbg_t:
                E.inc("sQ", 16)
                E.flush(e, "sQ")
            E.inc("sQ", 16)
            E.flush(e, "sQ")

    def full_script(E):
        ph_init(E)
        ph_a0(E)
        ph_w8(E)
        ph_wth(E)
        g1 = ph_a_pass1(E)
        ph_a_reduce(E, g1)
        g2 = ph_a_pass2(E)
        ph_a_out(E, g2)
        ph_b(E)
        ph_c(E)
        ph_d(E)
        ph_finish(E)

    with nc.Block() as block:
        @block.sync
        def _(eng):
            full_script(Script("sync", eng, sems))

        @block.gpsimd
        def _(eng):
            full_script(Script("gpsimd", eng, sems))

        @block.vector
        def _(eng):
            full_script(Script("vector", eng, sems))

        @block.scalar
        def _(eng):
            full_script(Script("scalar", eng, sems))

        @block.tensor
        def _(eng):
            full_script(Script("tensor", eng, sems))

    return nc, dbg_t



def make_in_maps(inputs):
    g = {k: np.ascontiguousarray(np.asarray(v)) for k, v in inputs.items()}
    xf = g["L0_attn_x"].reshape(T, D)
    maps = []
    for c in range(NC):
        ds = slice(c * DSL, (c + 1) * DSL)
        m = {
            "x_own": xf[c * TOK:(c + 1) * TOK],
            "wi_h": g["L0_attn_in_proj_w"][:, ds],
            "wi_g": g["L0_attn_in_proj_w"][:, INTER + c * DSL: INTER + (c + 1) * DSL],
            "conv_w_pp": g["L0_attn_conv_w"][ds, 0, :].reshape(4, 128, KCONV)
                .transpose(1, 0, 2).reshape(128, 4 * KCONV),
            "conv_b_pp": g["L0_attn_conv_b"][ds].reshape(4, 128).T,
            "xw_slice": g["L0_attn_x_proj_w"][ds, :],
            "wdt_slice": g["L0_attn_dt_proj_w"][:, ds],
            "dtb_pp": g["L0_attn_dt_proj_b"][ds].reshape(4, 128).T,
            "alog_pp": g["L0_attn_A_log"][ds, :].reshape(4, 128, NSTATE)
                .transpose(1, 0, 2).reshape(128, 4 * NSTATE),
            "dpar_pp": g["L0_attn_D_param"][ds].reshape(4, 128).T,
            "wout": g["L0_attn_out_proj_w"],
            "w8g_slice": g["L0_ffn_gate_w"][c * 256:(c + 1) * 256, :],
            "w8u_slice": g["L0_ffn_up_w"][c * 256:(c + 1) * 256, :],
            "w8d_slice": g["L0_ffn_down_w"][c * 768:(c + 1) * 768, :],
            "gs_pp": g["L0_ffn_gate_scale"].reshape(48, 128).T,
            "us_pp": g["L0_ffn_up_scale"].reshape(48, 128).T,
            "ds_pp": g["L0_ffn_down_scale"].reshape(16, 128).T,
            "wq_s": g["L1_attn_Wq"][:, c * 256:(c + 1) * 256],
            "wk_s": g["L1_attn_Wk"][:, c * 256:(c + 1) * 256],
            "wv_s": g["L1_attn_Wv"][:, c * 256:(c + 1) * 256],
            "bq_pp": g["L1_attn_bq"][c * 256:(c + 1) * 256].reshape(2, 128).T,
            "bk_pp": g["L1_attn_bk"][c * 256:(c + 1) * 256].reshape(2, 128).T,
            "bv_row": g["L1_attn_bv"][c * 256:(c + 1) * 256].reshape(1, 256),
            "wg_slice": g["L1_ffn_w_gate"][c * ISL:(c + 1) * ISL, :],
            "wu_slice": g["L1_ffn_w_up"][c * ISL:(c + 1) * ISL, :],
            "wd_slice": g["L1_ffn_w_down"][:, c * ISL:(c + 1) * ISL],
        }
        maps.append({k: np.ascontiguousarray(v) for k, v in m.items()})
    return maps


_NC_CACHE = {}


def kernel(**inputs):
    key = tuple(sorted(DEBUG))
    if key not in _NC_CACHE:
        _NC_CACHE[key] = build_program(debug=key)
    nc, dbg_t = _NC_CACHE[key]
    maps = make_in_maps(inputs)
    r = run_bass_kernel_spmd(nc, maps, core_ids=list(range(NC)))
    out = np.concatenate([np.asarray(r.results[c]["out_own"]) for c in range(NC)], 0)
    kernel.last_results = r.results
    return out.reshape(2, 4096, 2048)
